# version 02
# baseline (speedup 1.0000x reference)
import contextlib
import numpy as np

import concourse.bass as bass
import concourse.mybir as mybir
from concourse.bass_utils import run_bass_kernel_spmd

F32 = mybir.dt.float32
BF16 = mybir.dt.bfloat16
AF = mybir.ActivationFunctionType
ALU = mybir.AluOpType
AX = mybir.AxisListType.X

P = 128
T = 2048
NT = 16
D = 1024
DC = 8
NST = 4
STW = 512
FI = 1408
FIC = 11
NE = 8
EPS_LN = 1e-5

ENGS = ("sy", "gp", "te", "ve", "sc")


class Sem:
    __slots__ = ("h", "n", "name")

    def __init__(self, h, name):
        self.h = h
        self.n = 0
        self.name = name


class KB:
    def __init__(self, nc, ctx):
        self.nc = nc
        self.ctx = ctx
        self.q = {e: [] for e in ENGS}

    def sem(self, name):
        return Sem(self.ctx.enter_context(self.nc.semaphore(name)), name)

    def op(self, eng, fn, inc=None, inc_by=1):
        if inc is not None:
            inc.n += inc_by
            self.q[eng].append(lambda e, fn=fn, s=inc, b=inc_by: fn(e).then_inc(s.h, b))
        else:
            self.q[eng].append(lambda e, fn=fn: fn(e))

    def wait(self, eng, sem, val=None):
        v = sem.n if val is None else val
        if v <= 0:
            return
        self.q[eng].append(lambda e, s=sem, v=v: e.wait_ge(s.h, v))

    def dma(self, eng, out, in_, inc):
        self.op(eng, lambda e, o=out, i=in_: e.dma_start(out=o, in_=i), inc=inc, inc_by=16)

    def flush_block(self, name):
        nc = self.nc
        q = self.q
        self.q = {e: [] for e in ENGS}
        with nc.Block() as blk:
            @blk.sync
            def _(eng):
                for f in q["sy"]:
                    f(eng)

            @blk.gpsimd
            def _(eng):
                for f in q["gp"]:
                    f(eng)

            @blk.tensor
            def _(eng):
                for f in q["te"]:
                    f(eng)

            @blk.vector
            def _(eng):
                for f in q["ve"]:
                    f(eng)

            @blk.scalar
            def _(eng):
                for f in q["sc"]:
                    f(eng)


def build_program(phases=("attn_a", "ffn1", "attn_b", "ffn2"), alevel=3):
    nc = bass.Bass("TRN2", target_bir_lowering=False)
    ctx = contextlib.ExitStack()
    E = ctx.enter_context
    kb = KB(nc, ctx)

    def din(name, shape, dt=F32):
        return nc.dram_tensor(name, list(shape), dt, kind="ExternalInput").ap()

    xs = din("xs", (T, D))
    aWqT = din("aWqT", (D, D)); aWkT = din("aWkT", (D, D))
    aWvT = din("aWvT", (D, D)); aWoT = din("aWoT", (D, D))
    aBq = din("aBq", (P, DC)); aBk = din("aBk", (P, DC))
    aBv = din("aBv", (D,)); aBo = din("aBo", (D,)); aBkF = din("aBkF", (D,))
    bNormG = din("bNormG", (P, DC))
    bWq = din("bWq", (D, 512)); bWkv = din("bWkv", (D, 128))
    bWout = din("bWout", (512, D))
    gOutB = din("gOutB", (P, D))
    gateW9T = din("gateW9T", (D, 9))
    fWg = din("fWg", (NE, D, FI)); fWu = din("fWu", (NE, D, FI))
    fWd = din("fWd", (NE, FI, D))
    fswG = din("fswG", (D, FI)); fswU = din("fswU", (D, FI)); fswD = din("fswD", (FI, D))
    consts = din("consts", (P, 4))
    ident = din("ident", (P, P))
    tril = din("tril", (P, P))
    rmaskD = din("rmask_in", (P, 4))

    out = nc.dram_tensor("out", [T, D], F32, kind="ExternalOutput").ap()

    cc_s_in = nc.dram_tensor("cc_s_in", [D, D + 1], F32).ap()
    cc_s_out = nc.dram_tensor("cc_s_out", [4 * D, D + 1], F32).ap()
    cc_k_in = nc.dram_tensor("cc_k_in", [64, T], BF16).ap()
    cc_k_out = nc.dram_tensor("cc_k_out", [256, T], BF16).ap()
    ccvi = nc.dram_tensor("ccvi", [T, 64], BF16).ap()
    ccvo = nc.dram_tensor("ccvo", [4 * T, 64], BF16).ap()
    GROUPS = [[0, 1, 2, 3], [4, 5, 6, 7]]

    X = E(nc.sbuf_tensor("X", [P, NT, D], F32))
    slotA = E(nc.sbuf_tensor("slotA", [P, 11264], BF16))
    slotB = E(nc.sbuf_tensor("slotB", [P, 11264], BF16))
    sconst = E(nc.sbuf_tensor("sconst", [P, 4], F32))
    sid32 = E(nc.sbuf_tensor("sid32", [P, P], F32))
    sidbf = E(nc.sbuf_tensor("sidbf", [P, P], BF16))
    stril = E(nc.sbuf_tensor("stril", [P, P], F32))
    rmask = E(nc.sbuf_tensor("rmask", [P, 4], F32))
    lnst = E(nc.sbuf_tensor("lnst", [P, 12], F32))
    lnmv = E(nc.sbuf_tensor("lnmv", [P, 4], F32))
    xnbuf = E(nc.sbuf_tensor("xnbuf", [P, D], F32))
    zbuf = E(nc.sbuf_tensor("zbuf", [P, STW], F32))
    rbuf = E(nc.sbuf_tensor("rbuf", [P, STW], F32))
    ebuf = E(nc.sbuf_tensor("ebuf", [P, STW], F32))

    onesrow = slotA[0:1, 8192:8320]
    rowBv = slotA[0:1, 8320:9344]
    rowBo = slotA[0:1, 9344:10368]
    Atb = slotB[:, 8192:8320]
    onescol = slotB[:, 8320:8321]
    rowBk = slotB[0:1, 8448:9472]
    OTb = slotB[:, 9472:10496].rearrange("p (c n) -> p c n", c=DC)
    Obuf = xnbuf

    nc.const_aps.register(F32, EPS_LN, sconst[:, 0:1])
    nc.const_aps.register(F32, -16.0, sconst[:, 1:2])
    nc.const_aps.register(F32, 16.0, sconst[:, 2:3])
    nc.const_aps.register(F32, 1e-6, sconst[:, 3:4])

    P0 = E(nc.psum_tensor("P0", [P, 512], F32))
    P1 = E(nc.psum_tensor("P1", [P, 512], F32))
    PT0 = E(nc.psum_tensor("PT0", [P, 512], F32))
    PT1 = E(nc.psum_tensor("PT1", [P, 512], F32))
    PW0 = E(nc.psum_tensor("PW0", [P, 1024], F32))
    PW1 = E(nc.psum_tensor("PW1", [P, 1024], F32))
    TRP = (PT0[:, 0:128], PT1[:, 0:128])
    APS = P0[:, 0:128]
    ZCOL = P1[:, 0:1]
    DENC = P1[:, 4:5]

    sX = kb.sem("sX")
    sW = kb.sem("sW")
    sT = kb.sem("sT")
    sV = kb.sem("sV")
    sS = kb.sem("sS")
    sO = kb.sem("sO")
    sC1 = kb.sem("sC1")
    sC2 = kb.sem("sC2")
    sC3 = kb.sem("sC3")
    sM = kb.sem("sM")
    sLA = kb.sem("sLA")
    sLB = kb.sem("sLB")
    sEX = kb.sem("sEX")
    sAB = kb.sem("sAB")

    def slotv(slot, nchunk=DC, width=D):
        return slot[:, 0:nchunk * width].rearrange("p (c n) -> p c n", c=nchunk)

    def load_weight(slot, w_dram, nchunk=DC, width=D, sem=None):
        if sem is None:
            sem = sLA if slot is slotA else sLB
        kb.wait("gp", sem)
        kb.dma("gp", slotv(slot, nchunk, width),
               w_dram.rearrange("(c p) n -> p c n", p=P), sem)
        return sem.n

    def ln_tile(tt, src=None):
        s = X[:, tt, :] if src is None else src
        kb.wait("ve", sT)
        kb.wait("ve", sV)
        kb.op("ve", lambda e, s=s: e.bn_stats(lnst[:, 0:6], s[:, 0:512]))
        kb.op("ve", lambda e, s=s: e.bn_stats(lnst[:, 6:12], s[:, 512:1024]), inc=sV)
        kb.wait("ve", sV)
        kb.op("ve", lambda e: e.bn_aggr(lnmv[:, 0:2], lnst[:]), inc=sV)
        kb.wait("sc", sV)
        kb.op("sc", lambda e: e.activation(lnmv[:, 2:3], lnmv[:, 1:2], AF.Sqrt,
                                           bias=EPS_LN, scale=1.0), inc=sS)
        kb.wait("ve", sS)
        kb.op("ve", lambda e: e.reciprocal(lnmv[:, 3:4], lnmv[:, 2:3]), inc=sV)
        kb.wait("ve", sV)
        kb.op("ve", lambda e, s=s: e.tensor_scalar(
            xnbuf[:], s, lnmv[:, 0:1], lnmv[:, 3:4], ALU.subtract, ALU.mult), inc=sV)
        return sV.n

    tr_last = {0: 0, 1: 0}

    def transpose128(src_ap, dst_ap, k, rows=128):
        pv = TRP[k % 2][0:rows, :]
        kb.wait("te", sS, tr_last[k % 2])
        kb.op("te", lambda e, s=src_ap, p=pv: e.transpose(p, s, sid32[:]), inc=sT)
        kb.wait("sc", sT)
        kb.op("sc", lambda e, p=pv, d=dst_ap: e.copy(d, p), inc=sS)
        tr_last[k % 2] = sS.n
        return sS.n

    def phi_evac(psum, bias_ap, dst_ap, width=STW):
        kb.wait("ve", sT)
        kb.wait("ve", sS)
        kb.op("ve", lambda e, p=psum, b=bias_ap: e.tensor_scalar(
            zbuf[:, 0:width], p, b, None, ALU.add), inc=sV)
        kb.wait("sc", sV)
        kb.op("sc", lambda e: e.activation(rbuf[:, 0:width], zbuf[:, 0:width], AF.Relu), inc=sS)
        kb.wait("ve", sS)
        kb.op("ve", lambda e: e.tensor_sub(zbuf[:, 0:width], zbuf[:, 0:width],
                                           rbuf[:, 0:width]), inc=sV)
        kb.wait("sc", sV)
        kb.op("sc", lambda e: e.activation(ebuf[:, 0:width], zbuf[:, 0:width], AF.Exp), inc=sS)
        kb.wait("ve", sS)
        kb.op("ve", lambda e, d=dst_ap: e.tensor_add(d, ebuf[:, 0:width],
                                                     rbuf[:, 0:width]), inc=sV)
        return sV.n

    kb.dma("sy", sconst[:], consts, sX)
    kb.dma("sy", sid32[:], ident, sX)
    kb.dma("sy", stril[:], tril, sX)
    kb.dma("sy", rmask[:], rmaskD, sX)
    kb.dma("sy", X[:], xs.rearrange("(n p) d -> p n d", p=P), sX)
    kb.dma("gp", sidbf[:], ident, sW)
    kb.op("ve", lambda e: e.memset(onesrow, 1.0))
    kb.op("ve", lambda e: e.memset(onescol, 1.0), inc=sV)
    wK = load_weight(slotA, aWkT)
    wV = load_weight(slotB, aWvT)
    kb.flush_block("init")

    if "attn_a" in phases:
        actx = contextlib.ExitStack()
        Ea = actx.enter_context
        arena = Ea(nc.sbuf_tensor("aArena", [P, 20480], BF16))
        xnT = arena[:, 0:4096].rearrange("p (c n) -> p c n", c=DC)
        QpT = arena[:, 4096:8192].rearrange("p (c n) -> p c n", c=DC)
        KpT = arena[:, 8192:12288].rearrange("p (c n) -> p c n", c=DC)
        Vst = arena[:, 12288:16384].rearrange("p (c n) -> p c n", c=4)
        Kptok = arena[:, 16384:20480].rearrange("p (c n) -> p c n", c=4)
        SgF = arena[:].bitcast(F32)
        Sg = SgF[:, 0:8200].rearrange("p (c n) -> p c n", c=DC)
        S = Ea(nc.sbuf_tensor("a_S", [P, DC, 1025], F32))
        Sbf = Ea(nc.sbuf_tensor("a_Sbf", [P, DC, 1025], BF16))
        denb = Ea(nc.sbuf_tensor("a_den", [P, 2], F32))
        aBqc = Ea(nc.sbuf_tensor("a_Bq", [P, DC], F32))
        aBkc = Ea(nc.sbuf_tensor("a_Bk", [P, DC], F32))

        kb.dma("sy", aBqc[:], aBq, sX)
        kb.dma("sy", aBkc[:], aBk, sX)
        kb.dma("gp", rowBv, aBv.rearrange("(a d) -> a d", a=1), sW)
        kb.dma("gp", rowBo, aBo.rearrange("(a d) -> a d", a=1), sW)
        kb.dma("gp", rowBk, aBkF.rearrange("(a d) -> a d", a=1), sW)
        kb.op("ve", lambda e: e.memset(S[:], 0.0), inc=sV)
        ldx = (sX.n, sW.n)

        def st_ln_transpose(st):
            import os as _os
            trlim = int(_os.environ.get("TRLIM", str(DC)))
            for i in range(4):
                vln = ln_tile(st * 4 + i)
                kb.wait("te", sV, vln)
                for jc in range(trlim):
                    transpose128(xnbuf[:, jc * P:(jc + 1) * P],
                                 xnT[:, jc, i * P:(i + 1) * P], jc)
            return sS.n

        pz_last = {0: 0, 1: 0}

        def proj_feature(slot, bias_c, dstT, vready):
            sl = slotv(slot)
            for oc in range(DC):
                ps = P0 if oc % 2 == 0 else P1
                kb.wait("te", sS, vready)
                kb.wait("te", sV, pz_last[oc % 2])
                for jc in range(DC):
                    kb.op("te", lambda e, o=oc, j=jc, p=ps, s=sl: e.matmul(
                        p[:], s[:, j, o * P:(o + 1) * P], xnT[:, j, :],
                        start=(j == 0), stop=(j == DC - 1)),
                        inc=sT if jc == DC - 1 else None)
                phi_evac(ps[:], bias_c[:, oc:oc + 1], dstT[:, oc, :])
                pz_last[oc % 2] = sV.n - 2
            return sV.n

        def phi_psum(psum_half, dst_ap, width=STW):
            kb.wait("sc", sT)
            kb.wait("sc", sV)
            kb.op("sc", lambda e, p=psum_half: e.activation(rbuf[:, 0:width], p, AF.Relu), inc=sS)
            kb.wait("ve", sS)
            kb.op("ve", lambda e, p=psum_half: e.tensor_sub(zbuf[:, 0:width], p, rbuf[:, 0:width]), inc=sV)
            kb.wait("sc", sV)
            kb.op("sc", lambda e: e.activation(ebuf[:, 0:width], zbuf[:, 0:width], AF.Exp), inc=sS)
            kb.wait("ve", sS)
            kb.op("ve", lambda e, d=dst_ap: e.tensor_add(d, ebuf[:, 0:width], rbuf[:, 0:width]), inc=sV)
            return sV.n

        def proj_Ktok(vready):
            sl = slotv(slotA)
            for i in range(4):
                kb.wait("te", sS, vready)
                kb.wait("te", sV)
                for jc in range(DC):
                    kb.op("te", lambda e, i=i, j=jc, s=sl: e.matmul(
                        PW0[:, 0:512], xnT[:, j, i * P:(i + 1) * P], s[:, j, 0:512],
                        start=(j == 0), stop=False))
                    kb.op("te", lambda e, i=i, j=jc, s=sl: e.matmul(
                        PW0[:, 512:1024], xnT[:, j, i * P:(i + 1) * P], s[:, j, 512:1024],
                        start=(j == 0), stop=False))
                kb.op("te", lambda e: e.matmul(PW0[:, 0:512], onesrow, rowBk[:, 0:512],
                                               start=False, stop=True))
                kb.op("te", lambda e: e.matmul(PW0[:, 512:1024], onesrow, rowBk[:, 512:1024],
                                               start=False, stop=True), inc=sT)
                phi_psum(PW0[:, 0:512], Kptok[:, i, 0:512])
                phi_psum(PW0[:, 512:1024], Kptok[:, i, 512:1024])

        def proj_V(vready):
            sl = slotv(slotB)
            for i in range(4):
                kb.wait("te", sS, vready)
                kb.wait("te", sV)
                for jc in range(DC):
                    kb.op("te", lambda e, i=i, j=jc, s=sl: e.matmul(
                        PW0[:, 0:512], xnT[:, j, i * P:(i + 1) * P], s[:, j, 0:512],
                        start=(j == 0), stop=False))
                    kb.op("te", lambda e, i=i, j=jc, s=sl: e.matmul(
                        PW0[:, 512:1024], xnT[:, j, i * P:(i + 1) * P], s[:, j, 512:1024],
                        start=(j == 0), stop=False))
                kb.op("te", lambda e: e.matmul(PW0[:, 0:512], onesrow, rowBv[:, 0:512],
                                               start=False, stop=True))
                kb.op("te", lambda e: e.matmul(PW0[:, 512:1024], onesrow, rowBv[:, 512:1024],
                                               start=False, stop=True), inc=sT)
                kb.wait("ve", sT)
                kb.op("ve", lambda e, i=i: e.tensor_copy(Vst[:, i, :], PW0[:]), inc=sV)
            return sV.n

        def chunk_KV(i):
            for dc in range(DC):
                kb.wait("te", sV)
                kb.op("te", lambda e, d=dc, i=i: e.matmul(
                    PW1[:, 0:512], Kptok[:, i, d * P:(d + 1) * P], Vst[:, i, 0:512],
                    start=True, stop=True))
                kb.op("te", lambda e, d=dc, i=i: e.matmul(
                    PW1[:, 512:1024], Kptok[:, i, d * P:(d + 1) * P], Vst[:, i, 512:1024],
                    start=True, stop=True))
                kb.op("te", lambda e, d=dc, i=i: e.matmul(
                    ZCOL, Kptok[:, i, d * P:(d + 1) * P], onescol, start=True, stop=True), inc=sT)
                kb.wait("ve", sT)
                kb.op("ve", lambda e, d=dc: e.tensor_add(S[:, d, 0:1024], S[:, d, 0:1024], PW1[:]))
                kb.op("ve", lambda e, d=dc: e.tensor_add(S[:, d, 1024:1025], S[:, d, 1024:1025],
                                                         ZCOL), inc=sV)

        state = {"vsbf": 0}

        def mixing_chunk(st, i, s2=9):
            tt = st * 4 + i
            if s2 < 1:
                return
            kb.wait("te", sV)
            for dc in range(DC):
                kb.op("te", lambda e, d=dc, i=i: e.matmul(
                    APS, KpT[:, d, i * P:(i + 1) * P], QpT[:, d, i * P:(i + 1) * P],
                    start=(d == 0), stop=(d == DC - 1)), inc=sT if dc == DC - 1 else None)
            kb.wait("ve", sT)
            kb.op("ve", lambda e: e.tensor_mul(Atb, APS, stril[:]), inc=sV)
            vatb = sV.n
            if s2 < 2:
                return
            chunk_KV(i)
            if s2 < 3:
                return
            kb.wait("te", sV, state["vsbf"])
            kb.wait("te", sV)
            for dc in range(DC):
                kb.op("te", lambda e, d=dc, i=i: e.matmul(
                    PW0[:, 0:512], QpT[:, d, i * P:(i + 1) * P], Sbf[:, d, 0:512],
                    start=(d == 0), stop=False))
                kb.op("te", lambda e, d=dc, i=i: e.matmul(
                    PW0[:, 512:1024], QpT[:, d, i * P:(i + 1) * P], Sbf[:, d, 512:1024],
                    start=(d == 0), stop=False))
                kb.op("te", lambda e, d=dc, i=i: e.matmul(
                    DENC, QpT[:, d, i * P:(i + 1) * P], Sbf[:, d, 1024:1025],
                    start=(d == 0), stop=False))
            kb.op("te", lambda e, i=i: e.matmul(PW0[:, 0:512], Atb, Vst[:, i, 0:512],
                                                start=False, stop=True))
            kb.op("te", lambda e, i=i: e.matmul(PW0[:, 512:1024], Atb, Vst[:, i, 512:1024],
                                                start=False, stop=True))
            kb.op("te", lambda e: e.matmul(DENC, Atb, onescol, start=False, stop=True), inc=sT)
            vnum = sT.n
            kb.wait("ve", sT, vnum)
            kb.op("ve", lambda e: e.tensor_scalar(denb[:, 0:1], DENC, 1e-6, None, ALU.add), inc=sV)
            kb.wait("ve", sV)
            kb.op("ve", lambda e: e.reciprocal(denb[:, 1:2], denb[:, 0:1]), inc=sV)
            kb.wait("ve", sV)
            kb.op("ve", lambda e: e.tensor_scalar(Obuf[:], PW0[:], denb[:, 1:2], None, ALU.mult), inc=sV)
            vO = sV.n
            if s2 < 4:
                return
            kb.op("ve", lambda e: e.tensor_copy(Sbf[:], S[:]), inc=sV)
            state["vsbf"] = sV.n
            kb.wait("te", sV, vO)
            for ec in range(DC):
                transpose128(Obuf[:, ec * P:(ec + 1) * P], OTb[:, ec, :], ec)
            vOT = sS.n
            sl = slotv(slotB)
            kb.wait("te", sS, vOT)
            kb.wait("te", sV)
            for ec in range(DC):
                kb.op("te", lambda e, c=ec, s=sl: e.matmul(
                    PW1[:, 0:512], OTb[:, c, :], s[:, c, 0:512], start=(c == 0), stop=False))
                kb.op("te", lambda e, c=ec, s=sl: e.matmul(
                    PW1[:, 512:1024], OTb[:, c, :], s[:, c, 512:1024], start=(c == 0), stop=False))
            kb.op("te", lambda e: e.matmul(PW1[:, 0:512], onesrow, rowBo[:, 0:512],
                                           start=False, stop=True))
            kb.op("te", lambda e: e.matmul(PW1[:, 512:1024], onesrow, rowBo[:, 512:1024],
                                           start=False, stop=True), inc=sT)
            kb.wait("ve", sT)
            kb.op("ve", lambda e, t=tt: e.tensor_add(X[:, t, :], X[:, t, :], PW1[:]), inc=sV)

        import os
        sub = int(os.environ.get("SUBLEVEL", "9"))
        nst_lim = int(os.environ.get("NST_LIM", str(NST)))
        kb.wait("te", sX, ldx[0])
        kb.wait("te", sW, ldx[1])
        kb.wait("te", sLA, wK)
        kb.wait("te", sLB, wV)
        kb.wait("ve", sX, ldx[0])
        for st in range(nst_lim):
            vtr = st_ln_transpose(st)
            if sub == 4:
                proj_feature(slotA, aBkc, KpT, vtr)
            if sub >= 1 and sub != 4:
                proj_Ktok(vtr)
            if sub >= 2:
                proj_V(vtr)
            if sub >= 3:
                for i in range(4):
                    chunk_KV(i)

        if alevel < 2:
            actx.close()
            kb.flush_block("attn_a")
    if "attn_a" in phases and alevel >= 2:
        kb.wait("gp", sV)
        kb.dma("gp", cc_s_in.rearrange("(c p) e -> p c e", p=P), S[:], sEX)
        kb.wait("gp", sEX)
        kb.op("gp", lambda e: e.collective_compute(
            "AllGather", ALU.bypass, replica_groups=GROUPS,
            ins=[cc_s_in], outs=[cc_s_out]), inc=sC1)
        kb.wait("gp", sC1, 1)
        for r in range(4):
            kb.wait("gp", sV)
            kb.dma("gp", Sg, cc_s_out[r * D:(r + 1) * D, :].rearrange("(c p) e -> p c e", p=P), sEX)
            kb.wait("ve", sEX)
            if r == 0:
                kb.op("ve", lambda e: e.tensor_scalar(
                    S[:], Sg, rmask[:, 0:1], None, ALU.mult), inc=sV)
            else:
                kb.op("ve", lambda e, r=r: e.scalar_tensor_tensor(
                    S[:], Sg, rmask[:, r:r + 1], S[:], ALU.mult, ALU.add), inc=sV)
        kb.wait("ve", sV)
        kb.op("ve", lambda e: e.tensor_copy(Sbf[:], S[:]), inc=sV)
        state["vsbf"] = sV.n
        vexch = sV.n

        kb.wait("gp", sT)
        wQ = load_weight(slotA, aWqT)

        if alevel < 3:
            actx.close()
            kb.flush_block("attn_a")

    if "attn_a" in phases and alevel >= 3:
        import os
        nst_lim2 = int(os.environ.get("NST_LIM2", str(NST)))
        wVcur = wV
        s3 = int(os.environ.get("SUB3", "9"))
        for st in range(nst_lim2):
            if st == 0:
                kb.wait("ve", sV, vexch)
                kb.wait("sc", sV, vexch)
            if s3 >= -1:
                if s3 == -1:
                    for _i4 in range(4):
                        ln_tile(st * 4 + _i4)
                else:
                    vtr = st_ln_transpose(st)
            if s3 >= 1:
                kb.wait("te", sLA, wQ)
                proj_feature(slotA, aBqc, QpT, vtr)
            if s3 >= 2:
                kb.wait("gp", sT)
                wKn = load_weight(slotA, aWkT)
                kb.wait("te", sLA, wKn)
                proj_feature(slotA, aBkc, KpT, vtr)
            if s3 >= 3:
                proj_Ktok(vtr)
            if s3 >= 4:
                kb.wait("te", sLB, wVcur)
                proj_V(vtr)
            if s3 >= 5:
                kb.wait("gp", sT)
                wO = load_weight(slotB, aWoT)
                kb.wait("te", sLB, wO)
            if s3 >= 6:
                for i in range(4):
                    mixing_chunk(st, i, s2=int(os.environ.get("SUB2", "9")))
            if s3 >= 5 and st < NST - 1:
                kb.wait("gp", sT)
                wVcur = load_weight(slotB, aWvT)
                wQ = load_weight(slotA, aWqT)

        actx.close()
        kb.flush_block("attn_a")

    def emit_ffn(tag):
        fctx = contextlib.ExitStack()
        Ef = fctx.enter_context
        xnTf = Ef(nc.sbuf_tensor(tag + "_xnT", [P, DC, T], BF16))
        midT = Ef(nc.sbuf_tensor(tag + "_midT", [P, FIC, T], BF16))
        cmb = Ef(nc.sbuf_tensor(tag + "_cmb", [P, NT, 9], F32))
        rt = Ef(nc.sbuf_tensor(tag + "_rt", [P, 24], F32))
        gw9 = Ef(nc.sbuf_tensor(tag + "_gw9", [P, DC, 9], BF16))

        kb.dma("gp", gw9[:], gateW9T.rearrange("(c p) n -> p c n", p=P), sW)
        wg9 = sW.n
        kb.wait("ve", sX)

        wlist = []
        for e in range(NE):
            wlist += [(fWg[e], DC, FI), (fWu[e], DC, FI), (fWd[e], FIC, D)]
        wlist += [(fswG, DC, FI), (fswU, DC, FI), (fswD, FIC, D)]
        slots = [slotA, slotB]
        ssem = [sLA, sLB]
        wcnt = [0] * len(wlist)

        def load_wi(i):
            if i >= len(wlist):
                return
            dram, nch, wid = wlist[i]
            kb.wait("gp", sT)
            wcnt[i] = load_weight(slots[i % 2], dram, nch, wid)

        load_wi(0)
        load_wi(1)

        for tt in range(NT):
            vln = ln_tile(tt)
            kb.wait("te", sV, vln)
            for jc in range(DC):
                transpose128(xnbuf[:, jc * P:(jc + 1) * P],
                             xnTf[:, jc, tt * P:(tt + 1) * P], jc)
            vtr = sS.n
            kb.wait("te", sS, vtr)
            kb.wait("te", sV)
            kb.wait("te", sW, wg9)
            for jc in range(DC):
                kb.op("te", lambda e, j=jc, t=tt: e.matmul(
                    P0[:, 0:9], xnTf[:, j, t * P:(t + 1) * P], gw9[:, j, :],
                    start=(j == 0), stop=(j == DC - 1)), inc=sT if jc == DC - 1 else None)
            kb.wait("ve", sT)
            kb.wait("ve", sS)
            kb.op("ve", lambda e: e.tensor_copy(rt[:, 0:9], P0[:, 0:9]), inc=sV)
            kb.wait("ve", sV)
            kb.op("ve", lambda e: e.reduce_max(rt[:, 9:10], rt[:, 0:8], axis=AX,
                                               negate=True), inc=sV)
            kb.wait("sc", sV)
            kb.op("sc", lambda e: e.activation(rt[:, 16:24], rt[:, 0:8], AF.Exp,
                                               bias=rt[:, 9:10], scale=1.0), inc=sS)
            kb.op("sc", lambda e, t=tt: e.activation(cmb[:, t, 8:9], rt[:, 8:9],
                                                     AF.Sigmoid), inc=sS)
            kb.wait("ve", sS)
            kb.op("ve", lambda e: e.reduce_max(rt[:, 13:14], rt[:, 16:24], axis=AX), inc=sV)
            kb.wait("ve", sV)
            kb.op("ve", lambda e, t=tt: e.tensor_scalar(
                cmb[:, t, 0:8], rt[:, 16:24], rt[:, 13:14], 1e30, ALU.is_ge, ALU.mult), inc=sV)
            kb.wait("ve", sV)
            kb.op("ve", lambda e, t=tt: e.tensor_sub(cmb[:, t, 0:8], rt[:, 16:24],
                                                     cmb[:, t, 0:8]), inc=sV)
            kb.wait("ve", sV)
            kb.op("ve", lambda e, t=tt: e.reduce_max(rt[:, 10:11], cmb[:, t, 0:8],
                                                     axis=AX), inc=sV)
            kb.wait("ve", sV)
            kb.op("ve", lambda e: e.tensor_tensor(rt[:, 11:12], rt[:, 13:14],
                                                  rt[:, 10:11], ALU.add), inc=sV)
            kb.wait("ve", sV)
            kb.op("ve", lambda e: e.reciprocal(rt[:, 12:13], rt[:, 11:12]), inc=sV)
            kb.wait("ve", sV)
            kb.op("ve", lambda e, t=tt: e.scalar_tensor_tensor(
                cmb[:, t, 0:8], rt[:, 16:24], rt[:, 10:11], rt[:, 16:24],
                ALU.is_ge, ALU.mult), inc=sV)
            kb.wait("ve", sV)
            kb.op("ve", lambda e, t=tt: e.tensor_scalar(
                cmb[:, t, 0:8], cmb[:, t, 0:8], rt[:, 12:13], None, ALU.mult), inc=sV)
        vxnT = sS.n

        gbank = (P0, PT0)
        ubank = (P1, PT1)
        gfree = {0: 0, 1: 0}
        ufree = {0: 0, 1: 0}
        dfree = {0: 0, 1: 0}
        vdown_prev = 0

        kb.wait("te", sV)
        for ei in range(NE + 1):
            i_g, i_u, i_d = 3 * ei, 3 * ei + 1, 3 * ei + 2
            slg = slotv(slots[i_g % 2], DC, FI)
            slu = slotv(slots[i_u % 2], DC, FI)
            kb.wait("te", ssem[i_g % 2], wcnt[i_g])
            kb.wait("te", ssem[i_u % 2], wcnt[i_u])
            kb.wait("te", sS, vxnT)
            kb.wait("ve", sT, vdown_prev)
            for ic in range(FIC):
                for tq in range(4):
                    b = (ic * 4 + tq) % 2
                    pg, pu = gbank[b], ubank[b]
                    sil = (zbuf if b == 0 else rbuf)
                    kb.wait("te", sS, gfree[b])
                    kb.wait("te", sV, ufree[b])
                    for jc in range(DC):
                        kb.op("te", lambda e, j=jc, c=ic, q=tq, s=slg, p=pg: e.matmul(
                            p[:], s[:, j, c * P:(c + 1) * P], xnTf[:, j, q * 512:(q + 1) * 512],
                            start=(j == 0), stop=(j == DC - 1)))
                    for jc in range(DC):
                        kb.op("te", lambda e, j=jc, c=ic, q=tq, s=slu, p=pu: e.matmul(
                            p[:], s[:, j, c * P:(c + 1) * P], xnTf[:, j, q * 512:(q + 1) * 512],
                            start=(j == 0), stop=(j == DC - 1)),
                            inc=sT if jc == DC - 1 else None)
                    kb.wait("sc", sT)
                    kb.op("sc", lambda e, p=pg, s=sil: e.activation(
                        s[:, 0:512], p[:], AF.Silu), inc=sS)
                    gfree[b] = sS.n
                    kb.wait("ve", sS)
                    kb.op("ve", lambda e, p=pu, s=sil, c=ic, q=tq: e.tensor_tensor(
                        midT[:, c, q * 512:(q + 1) * 512], s[:, 0:512], p[:],
                        ALU.mult), inc=sV)
                    ufree[b] = sV.n
            vmid = sV.n
            load_wi(i_d)
            load_wi(i_g + 3)
            sld = slotv(slots[i_d % 2], FIC, D)
            kb.wait("te", ssem[i_d % 2], wcnt[i_d])
            kb.wait("ve", sT)
            kb.wait("te", sV, vmid)
            for tc in range(NT):
                b = tc % 2
                pw = (PW0, PW1)[b]
                kb.wait("te", sV, dfree[b])
                for ic in range(FIC):
                    kb.op("te", lambda e, c=ic, t=tc, s=sld, p=pw: e.matmul(
                        p[:, 0:512], midT[:, c, t * P:(t + 1) * P], s[:, c, 0:512],
                        start=(c == 0), stop=(c == FIC - 1)))
                    kb.op("te", lambda e, c=ic, t=tc, s=sld, p=pw: e.matmul(
                        p[:, 512:1024], midT[:, c, t * P:(t + 1) * P], s[:, c, 512:1024],
                        start=(c == 0), stop=(c == FIC - 1)),
                        inc=sT if ic == FIC - 1 else None)
                kb.wait("ve", sT)
                kb.op("ve", lambda e, t=tc, ei=ei, p=pw: e.scalar_tensor_tensor(
                    X[:, t, :], p[:], cmb[:, t, ei:ei + 1], X[:, t, :],
                    ALU.mult, ALU.add), inc=sV)
                dfree[b] = sV.n
            vdown_prev = sT.n
            load_wi(i_u + 3)
        fctx.close()

    if "ffn1" in phases:
        emit_ffn("f1")
        kb.flush_block("ffn1")

    def emit_attn_b():
        import os as _os
        b3 = int(_os.environ.get("B3LEVEL", "9"))
        bctx = contextlib.ExitStack()
        Eb = bctx.enter_context
        qhT = Eb(nc.sbuf_tensor("b_qhT", [P, 4, T], BF16))
        ob2 = Eb(nc.sbuf_tensor("b_ob2", [P, D], F32))
        pta = (slotA[:, 1024:1536], slotA[:, 1536:2048])
        ptb2 = (slotA[:, 2048:2560], slotA[:, 2560:3072])
        onesb = slotA[0:1, 3072:3136]
        dre = (slotA[0:1, 3136:3648], slotA[0:1, 3648:4160])
        abtmp = slotA[:, 4224:4736]
        bq_s = Eb(nc.sbuf_tensor("b_qs", [P, 32], F32))

        sub1 = contextlib.ExitStack()
        xnnT = sub1.enter_context(nc.sbuf_tensor("b_xnnT", [P, DC, T], BF16))
        bWqS = sub1.enter_context(nc.sbuf_tensor("b_Wq", [P, DC, 512], BF16))
        bWkvS = sub1.enter_context(nc.sbuf_tensor("b_Wkv", [P, DC, P], BF16))
        bGc = sub1.enter_context(nc.sbuf_tensor("b_gc", [P, DC], F32))
        kTl = sub1.enter_context(nc.sbuf_tensor("b_kTl", [64, T], BF16))
        vloc = sub1.enter_context(nc.sbuf_tensor("b_vloc", [P, NT, 64], BF16))

        for _e in ("ve", "sc", "te", "sy"):
            kb.wait(_e, sLA)
            kb.wait(_e, sLB)
        kb.op("ve", lambda e: e.memset(onesb, 1.0), inc=sV)
        kb.wait("sy", sM)
        kb.dma("sy", bGc[:], bNormG, sM)
        kb.dma("gp", bWqS[:], bWq.rearrange("(c p) n -> p c n", p=P), sW)
        kb.dma("gp", bWkvS[:], bWkv.rearrange("(c p) n -> p c n", p=P), sW)
        wqv = sW.n
        vgc = sM.n
        kb.wait("ve", sW, wqv)
        kb.wait("ve", sM, vgc)
        for jc in range(DC):
            kb.op("ve", lambda e, j=jc: e.tensor_scalar(
                bWqS[:, j, :], bWqS[:, j, :], bGc[:, j:j + 1], None, ALU.mult), inc=sV)
            kb.op("ve", lambda e, j=jc: e.tensor_scalar(
                bWkvS[:, j, :], bWkvS[:, j, :], bGc[:, j:j + 1], None, ALU.mult), inc=sV)
        vfold = sV.n
        kb.wait("ve", sX)

        for tt in range(NT):
            ln_tile(tt)
            ln_tile(tt, src=xnbuf[:])
            vln = sV.n
            kb.wait("te", sV, vln)
            for jc in range(DC):
                transpose128(xnbuf[:, jc * P:(jc + 1) * P],
                             xnnT[:, jc, tt * P:(tt + 1) * P], jc)
            vtr = sS.n
            kb.wait("te", sS, vtr)
            kb.wait("te", sV)
            kb.wait("te", sW, wqv)
            kb.wait("te", sV, vfold)
            for jc in range(DC):
                kb.op("te", lambda e, j=jc, t=tt: e.matmul(
                    PW0[:, 0:512], xnnT[:, j, t * P:(t + 1) * P], bWqS[:, j, :],
                    start=(j == 0), stop=(j == DC - 1)))
            for jc in range(DC):
                kb.op("te", lambda e, j=jc, t=tt: e.matmul(
                    P1[:, 0:128], xnnT[:, j, t * P:(t + 1) * P], bWkvS[:, j, :],
                    start=(j == 0), stop=(j == DC - 1)), inc=sT if jc == DC - 1 else None)
            vqkv = sT.n
            kb.wait("ve", sT, vqkv)
            kb.wait("ve", sS)
            kb.op("ve", lambda e: e.tensor_copy(zbuf[:], PW0[:, 0:512]), inc=sV)
            kb.wait("sc", sV)
            kb.op("sc", lambda e: e.activation(rbuf[:], zbuf[:], AF.Square), inc=sS)
            kb.wait("ve", sS)
            kb.op("ve", lambda e: e.reduce_sum(
                bq_s[:, 0:8], rbuf[:].rearrange("p (h d) -> p h d", h=8), axis=AX), inc=sV)
            kb.wait("sc", sV)
            kb.op("sc", lambda e: e.activation(bq_s[:, 8:16], bq_s[:, 0:8], AF.Sqrt), inc=sS)
            kb.wait("ve", sS)
            kb.op("ve", lambda e: e.tensor_scalar(bq_s[:, 8:16], bq_s[:, 8:16], 1e-12,
                                                  None, ALU.max), inc=sV)
            kb.wait("ve", sV)
            kb.op("ve", lambda e: e.reciprocal(bq_s[:, 16:24], bq_s[:, 8:16]), inc=sV)
            kb.wait("ve", sV)
            for h in range(8):
                kb.op("ve", lambda e, h=h: e.tensor_scalar(
                    rbuf[:, h * 64:(h + 1) * 64], zbuf[:, h * 64:(h + 1) * 64],
                    bq_s[:, 16 + h:17 + h], None, ALU.mult), inc=sV)
            vqh = sV.n
            kb.wait("te", sV, vqh)
            for c4 in range(4):
                transpose128(rbuf[:, c4 * P:(c4 + 1) * P], qhT[:, c4, tt * P:(tt + 1) * P],
                             c4)
            kb.wait("ve", sT, vqkv)
            kb.op("ve", lambda e: e.tensor_copy(ebuf[:, 0:128], P1[:, 0:128]), inc=sV)
            kb.wait("sc", sV)
            kb.op("sc", lambda e: e.activation(ebuf[:, 128:192], ebuf[:, 0:64], AF.Square), inc=sS)
            kb.wait("ve", sS)
            kb.op("ve", lambda e: e.reduce_sum(bq_s[:, 24:25], ebuf[:, 128:192], axis=AX), inc=sV)
            kb.wait("sc", sV)
            kb.op("sc", lambda e: e.activation(bq_s[:, 25:26], bq_s[:, 24:25], AF.Sqrt), inc=sS)
            kb.wait("ve", sS)
            kb.op("ve", lambda e: e.tensor_scalar(bq_s[:, 25:26], bq_s[:, 25:26], 1e-12,
                                                  None, ALU.max), inc=sV)
            kb.wait("ve", sV)
            kb.op("ve", lambda e: e.reciprocal(bq_s[:, 26:27], bq_s[:, 25:26]), inc=sV)
            kb.wait("ve", sV)
            kb.op("ve", lambda e: e.tensor_scalar(ebuf[:, 192:256], ebuf[:, 0:64],
                                                  bq_s[:, 26:27], None, ALU.mult), inc=sV)
            kb.op("ve", lambda e, t=tt: e.tensor_copy(vloc[:, t, :], ebuf[:, 64:128]), inc=sV)
            vkh = sV.n
            kb.wait("te", sV, vkh)
            transpose128(ebuf[:, 192:256], kTl[:, tt * P:(tt + 1) * P], tt, rows=64)
        vprojs = sS.n

        if b3 < 2:
            sub1.close()
            bctx.close()
            return
        kb.wait("gp", sS, vprojs)
        kb.wait("gp", sV)
        kb.dma("gp", cc_k_in, kTl[:], sEX)
        kb.dma("gp", ccvi.rearrange("(n p) d -> p n d", p=P), vloc[:], sEX)
        kb.wait("gp", sEX)
        kb.op("gp", lambda e: e.collective_compute(
            "AllGather", ALU.bypass, replica_groups=GROUPS,
            ins=[cc_k_in], outs=[cc_k_out]), inc=sC2)
        kb.op("gp", lambda e: e.collective_compute(
            "AllGather", ALU.bypass, replica_groups=GROUPS,
            ins=[ccvi], outs=[ccvo]), inc=sC3)
        kb.wait("gp", sC2, 1)
        kb.wait("gp", sC3, 1)
        kb.wait("gp", sT)
        sub1.close()
        kTg = Eb(nc.sbuf_tensor("b_kTg", [P, 4, T], BF16))
        vg = Eb(nc.sbuf_tensor("b_vg", [P, 64, 65], BF16))
        attT = Eb(nc.sbuf_tensor("b_attT", [P, 4, T], BF16))
        bWoS = Eb(nc.sbuf_tensor("b_Wo", [P, 4, D], BF16))
        gOutS = Eb(nc.sbuf_tensor("b_go", [P, D], F32))
        kb.dma("gp", kTg[0:64, :, :], cc_k_out.rearrange("(r d) t -> d r t", d=64), sEX)
        kb.dma("gp", kTg[64:128, :, :], cc_k_out.rearrange("(r d) t -> d r t", d=64), sEX)
        kb.dma("gp", vg[:, :, 0:64], ccvo.rearrange("(n p) d -> p n d", p=P), sEX)
        kb.dma("gp", bWoS[:], bWout.rearrange("(c p) n -> p c n", p=P), sW)
        kb.wait("sy", sM)
        kb.dma("sy", gOutS[:], gOutB, sM)
        vld2 = (sEX.n, sW.n, sM.n)
        kb.wait("ve", sT)
        kb.op("ve", lambda e: e.memset(vg[:, :, 64:65], 1.0), inc=sV)
        vvg = sV.n

        if b3 < 3:
            bctx.close()
            return
        simb = ((P0, P1), (PT0, PT1))
        sfree = {0: 0, 1: 0}
        pfree = {0: 0, 1: 0}
        kb.wait("te", sEX, vld2[0])
        kb.wait("te", sW, vld2[1])
        kb.wait("te", sV, vvg)
        vout_ev = 0
        for hp in range(4):
            for qt in range(4):
                qsl = slice(qt * 512, (qt + 1) * 512)
                kb.wait("te", sV, vout_ev)
                for jc in range(64):
                    b = jc % 2
                    pa, pb = simb[b]
                    r, tl = jc // 16, (jc % 16) * 128
                    kb.wait("te", sS, sfree[b])
                    kb.op("te", lambda e, p=pa, r=r, tl=tl, hp=hp, q=qsl: e.matmul(
                        p[:], kTg[0:64, r, tl:tl + 128], qhT[0:64, hp, q],
                        start=True, stop=True))
                    kb.op("te", lambda e, p=pb, r=r, tl=tl, hp=hp, q=qsl: e.matmul(
                        p[:], kTg[64:128, r, tl:tl + 128], qhT[64:128, hp, q],
                        start=True, stop=True), inc=sT)
                    kb.wait("sc", sT)
                    kb.op("sc", lambda e, p=pa, d=pta[b]: e.activation(
                        d, p[:], AF.Exp, bias=-16.0, scale=16.0), inc=sS)
                    kb.op("sc", lambda e, p=pb, d=ptb2[b]: e.activation(
                        d, p[:], AF.Exp, bias=-16.0, scale=16.0), inc=sS)
                    sfree[b] = sS.n
                    kb.wait("te", sS)
                    kb.op("te", lambda e, j=jc, d=pta[b]: e.matmul(
                        PW0[0:65, 0:512], vg[:, j, :], d,
                        start=(j == 0), stop=(j == 63)))
                    kb.op("te", lambda e, j=jc, d=ptb2[b]: e.matmul(
                        PW0[0:65, 512:1024], vg[:, j, :], d,
                        start=(j == 0), stop=(j == 63)), inc=sT)
                    pfree[b] = sT.n
                vpv = sT.n
                kb.wait("ve", sT, vpv)
                kb.op("ve", lambda e: e.tensor_copy(xnbuf[0:1, 0:512], PW0[64:65, 0:512]))
                kb.op("ve", lambda e: e.tensor_copy(xnbuf[0:1, 512:1024], PW0[64:65, 512:1024]), inc=sV)
                kb.wait("ve", sV)
                kb.op("ve", lambda e: e.reciprocal(zbuf[0:1, 0:512], xnbuf[0:1, 0:512]), inc=sV)
                kb.op("ve", lambda e: e.reciprocal(rbuf[0:1, 0:512], xnbuf[0:1, 512:1024]), inc=sV)
                kb.wait("ve", sV)
                kb.op("ve", lambda e: e.tensor_copy(dre[0], zbuf[0:1, 0:512]))
                kb.op("ve", lambda e: e.tensor_copy(dre[1], rbuf[0:1, 0:512]), inc=sV)
                vdre = sV.n
                kb.wait("te", sV, vdre)
                kb.wait("te", sS)
                kb.op("te", lambda e: e.matmul(PW1[0:64, 0:512], onesb, dre[0],
                                               start=True, stop=True))
                kb.op("te", lambda e: e.matmul(PW1[0:64, 512:1024], onesb, dre[1],
                                               start=True, stop=True), inc=sT)
                kb.wait("sc", sT)
                kb.op("sc", lambda e: e.copy(zbuf[0:64, :], PW1[0:64, 0:512]))
                kb.op("sc", lambda e: e.copy(rbuf[0:64, :], PW1[0:64, 512:1024]), inc=sS)
                kb.wait("ve", sS)
                kb.op("ve", lambda e, hp=hp, q=qsl: e.tensor_mul(
                    attT[0:64, hp, q], PW0[0:64, 0:512], zbuf[0:64, :]), inc=sV)
                kb.wait("ve", sV)
                kb.op("ve", lambda e: e.tensor_mul(
                    abtmp[0:64, :], PW0[0:64, 512:1024], rbuf[0:64, :]), inc=sV)
                vout_ev = sV.n
                kb.wait("sy", sV, vout_ev)
                kb.dma("sy", attT[64:128, hp, qsl], abtmp[0:64, :], sAB)
                kb.wait("ve", sAB)

        if b3 < 4:
            bctx.close()
            return
        vatt = sAB.n
        kb.wait("te", sAB, vatt)
        ofree = 0
        xfree = {0: 0, 1: 0}
        for tc in range(NT):
            xnb = xnbuf[:] if tc % 2 == 0 else ob2[:]
            kb.wait("te", sV, ofree)
            kb.wait("te", sS)
            for hp in range(4):
                for half in range(2):
                    hs = slice(half * 512, (half + 1) * 512)
                    kb.op("te", lambda e, hp=hp, t=tc, hs=hs: e.matmul(
                        PW0[:, hs], attT[0:64, hp, t * P:(t + 1) * P], bWoS[0:64, hp, hs],
                        start=(hp == 0), stop=(hp == 3)))
                    kb.op("te", lambda e, hp=hp, t=tc, hs=hs: e.matmul(
                        PW1[:, hs], attT[64:128, hp, t * P:(t + 1) * P], bWoS[64:128, hp, hs],
                        start=(hp == 0), stop=(hp == 3)),
                        inc=sT if (hp == 3 and half == 1) else None)
            kb.wait("sc", sT)
            kb.wait("sc", sV, xfree[tc % 2])
            kb.op("sc", lambda e, d=xnb: e.copy(d, PW1[:]), inc=sS)
            kb.wait("ve", sS)
            kb.wait("ve", sM, vld2[2])
            kb.op("ve", lambda e, d=xnb: e.tensor_add(d, d, PW0[:]), inc=sV)
            kb.wait("ve", sV)
            kb.op("ve", lambda e, d=xnb: e.bn_stats(lnst[:, 0:6], d[:, 0:512]))
            kb.op("ve", lambda e, d=xnb: e.bn_stats(lnst[:, 6:12], d[:, 512:1024]), inc=sV)
            kb.wait("ve", sV)
            kb.op("ve", lambda e: e.bn_aggr(lnmv[:, 0:2], lnst[:]), inc=sV)
            kb.wait("sc", sV)
            kb.op("sc", lambda e: e.activation(lnmv[:, 2:3], lnmv[:, 1:2], AF.Sqrt,
                                               bias=EPS_LN, scale=1.0), inc=sS)
            kb.wait("ve", sS)
            kb.op("ve", lambda e: e.reciprocal(lnmv[:, 3:4], lnmv[:, 2:3]), inc=sV)
            kb.wait("ve", sV)
            kb.op("ve", lambda e, d=xnb: e.tensor_scalar(
                d, d, lnmv[:, 0:1], lnmv[:, 3:4],
                ALU.subtract, ALU.mult), inc=sV)
            kb.wait("ve", sV)
            kb.op("ve", lambda e, d=xnb: e.tensor_mul(d, d, gOutS[:]), inc=sV)
            kb.wait("ve", sV)
            kb.op("ve", lambda e, t=tc, d=xnb: e.tensor_add(X[:, t, :], X[:, t, :], d), inc=sV)
            xfree[tc % 2] = sV.n
            ofree = sV.n - 6
        bctx.close()

    if "attn_b" in phases:
        emit_attn_b()
        kb.flush_block("attn_b")

    if "ffn2" in phases:
        emit_ffn("f2")
        kb.flush_block("ffn2")

    kb.wait("sy", sV)
    kb.wait("sy", sT)
    kb.wait("sy", sS)
    kb.dma("sy", out.rearrange("(n p) d -> p n d", p=P), X[:], sO)
    kb.wait("gp", sO)
    kb.flush_block("store")

    ctx.close()
    return nc


def _prep_inputs(inputs):
    gi = lambda k: np.ascontiguousarray(np.asarray(inputs[k], np.float32))
    x = gi("L0_attn_a_x")
    xsh = np.ascontiguousarray(x.reshape(8, T, D))
    base = {
        "aWqT": np.ascontiguousarray(gi("L0_attn_a_Wq").T),
        "aWkT": np.ascontiguousarray(gi("L0_attn_a_Wk").T),
        "aWvT": np.ascontiguousarray(gi("L0_attn_a_Wv").T),
        "aWoT": np.ascontiguousarray(gi("L0_attn_a_Wo").T),
        "aBq": np.ascontiguousarray(gi("L0_attn_a_bq").reshape(DC, P).T),
        "aBk": np.ascontiguousarray(gi("L0_attn_a_bk").reshape(DC, P).T),
        "aBv": gi("L0_attn_a_bv"), "aBo": gi("L0_attn_a_bo"),
        "aBkF": gi("L0_attn_a_bk"),
        "bNormG": np.ascontiguousarray(gi("L0_attn_b_norm_g").reshape(DC, P).T),
        "bWq": gi("L0_attn_b_Wq"), "bWkv": gi("L0_attn_b_Wkv"),
        "bWout": gi("L0_attn_b_Wout"),
        "gOutB": np.ascontiguousarray(np.broadcast_to(gi("L0_attn_b_out_norm_g"), (P, D))),
        "gateW9T": np.ascontiguousarray(np.concatenate(
            [gi("L0_ffn_gate_w").T, gi("L0_ffn_shared_gate_w").T], axis=1)),
        "fWg": gi("L0_ffn_w_gate"), "fWu": gi("L0_ffn_w_up"), "fWd": gi("L0_ffn_w_down"),
        "fswG": gi("L0_ffn_sw_gate"), "fswU": gi("L0_ffn_sw_up"), "fswD": gi("L0_ffn_sw_down"),
        "consts": np.ascontiguousarray(np.broadcast_to(
            np.array([EPS_LN, -16.0, 16.0, 1e-6], np.float32), (P, 4))),
        "ident": np.eye(P, dtype=np.float32),
        "tril": np.ascontiguousarray(np.triu(np.ones((P, P), np.float32))),
    }
    in_maps = []
    for c in range(8):
        r = c % 4
        rm = np.zeros((P, 4), np.float32)
        rm[:, :r] = 1.0
        im = dict(base)
        im["xs"] = xsh[c]
        im["rmask_in"] = rm
        in_maps.append(im)
    return in_maps


def kernel(**inputs):
    in_maps = _prep_inputs(inputs)
    nc = build_program()
    res = run_bass_kernel_spmd(nc, in_maps, core_ids=list(range(8)))
    outs = np.stack([np.asarray(res.results[c]["out"], np.float32) for c in range(8)])
    return np.ascontiguousarray(outs.reshape(2, 8192, 1024))


# version 07
# speedup vs baseline: 1.1973x; 1.1973x over previous
import contextlib
import numpy as np

import concourse.bass as bass
import concourse.mybir as mybir
from concourse.bass_utils import run_bass_kernel_spmd

F32 = mybir.dt.float32
BF16 = mybir.dt.bfloat16
AF = mybir.ActivationFunctionType
ALU = mybir.AluOpType
AX = mybir.AxisListType.X

P = 128
T = 2048
NT = 16
D = 1024
DC = 8
NST = 4
STW = 512
FI = 1408
FIC = 11
NE = 8
EPS_LN = 1e-5

ENGS = ("sy", "gp", "te", "ve", "sc")


class Sem:
    __slots__ = ("h", "n", "name")

    def __init__(self, h, name):
        self.h = h
        self.n = 0
        self.name = name


class KB:
    def __init__(self, nc, ctx):
        self.nc = nc
        self.ctx = ctx
        self.q = {e: [] for e in ENGS}

    def sem(self, name):
        return Sem(self.ctx.enter_context(self.nc.semaphore(name)), name)

    def op(self, eng, fn, inc=None, inc_by=1):
        if inc is not None:
            inc.n += inc_by
            self.q[eng].append(lambda e, fn=fn, s=inc, b=inc_by: fn(e).then_inc(s.h, b))
        else:
            self.q[eng].append(lambda e, fn=fn: fn(e))

    def wait(self, eng, sem, val=None):
        v = sem.n if val is None else val
        if v <= 0:
            return
        self.q[eng].append(lambda e, s=sem, v=v: e.wait_ge(s.h, v))

    def dma(self, eng, out, in_, inc):
        self.op(eng, lambda e, o=out, i=in_: e.dma_start(out=o, in_=i), inc=inc, inc_by=16)

    def flush_block(self, name):
        nc = self.nc
        q = self.q
        self.q = {e: [] for e in ENGS}
        with nc.Block() as blk:
            @blk.sync
            def _(eng):
                for f in q["sy"]:
                    f(eng)

            @blk.gpsimd
            def _(eng):
                for f in q["gp"]:
                    f(eng)

            @blk.tensor
            def _(eng):
                for f in q["te"]:
                    f(eng)

            @blk.vector
            def _(eng):
                for f in q["ve"]:
                    f(eng)

            @blk.scalar
            def _(eng):
                for f in q["sc"]:
                    f(eng)


def build_program(phases=("attn_a", "ffn1", "attn_b", "ffn2"), alevel=3):
    nc = bass.Bass("TRN2", target_bir_lowering=False)
    ctx = contextlib.ExitStack()
    E = ctx.enter_context
    kb = KB(nc, ctx)

    def din(name, shape, dt=F32):
        return nc.dram_tensor(name, list(shape), dt, kind="ExternalInput").ap()

    xs = din("xs", (T, D))
    aWqT = din("aWqT", (D, D)); aWkT = din("aWkT", (D, D))
    aWvT = din("aWvT", (D, D)); aWoT = din("aWoT", (D, D))
    aBq = din("aBq", (P, DC)); aBk = din("aBk", (P, DC))
    aBv = din("aBv", (D,)); aBo = din("aBo", (D,)); aBkF = din("aBkF", (D,))
    bNormG = din("bNormG", (P, DC))
    bWq = din("bWq", (D, 512)); bWkv = din("bWkv", (D, 128))
    bWout = din("bWout", (512, D))
    gOutB = din("gOutB", (P, D))
    gateW9T = din("gateW9T", (D, 9))
    fWg = din("fWg", (NE, D, FI)); fWu = din("fWu", (NE, D, FI))
    fWd = din("fWd", (NE, FI, D))
    fswG = din("fswG", (D, FI)); fswU = din("fswU", (D, FI)); fswD = din("fswD", (FI, D))
    consts = din("consts", (P, 4))
    ident = din("ident", (P, P))
    tril = din("tril", (P, P))
    rmaskD = din("rmask_in", (P, 4))

    out = nc.dram_tensor("out", [T, D], F32, kind="ExternalOutput").ap()

    cc_s_in = nc.dram_tensor("cc_s_in", [D, D + 1], F32).ap()
    cc_s_out = nc.dram_tensor("cc_s_out", [4 * D, D + 1], F32).ap()
    cc_k_in = nc.dram_tensor("cc_k_in", [64, T], BF16).ap()
    cc_k_out = nc.dram_tensor("cc_k_out", [256, T], BF16).ap()
    ccvi = nc.dram_tensor("ccvi", [T, 64], BF16).ap()
    ccvo = nc.dram_tensor("ccvo", [4 * T, 64], BF16).ap()
    GROUPS = [[0, 1, 2, 3], [4, 5, 6, 7]]

    X = E(nc.sbuf_tensor("X", [P, NT, D], F32))
    slotA = E(nc.sbuf_tensor("slotA", [P, 11264], BF16))
    slotB = E(nc.sbuf_tensor("slotB", [P, 11264], BF16))
    sconst = E(nc.sbuf_tensor("sconst", [P, 4], F32))
    sid32 = E(nc.sbuf_tensor("sid32", [P, P], F32))
    sidbf = E(nc.sbuf_tensor("sidbf", [P, P], BF16))
    stril = E(nc.sbuf_tensor("stril", [P, P], F32))
    rmask = E(nc.sbuf_tensor("rmask", [P, 4], F32))
    lnst = E(nc.sbuf_tensor("lnst", [P, 12], F32))
    lnmv = E(nc.sbuf_tensor("lnmv", [P, 4], F32))
    xnbuf = E(nc.sbuf_tensor("xnbuf", [P, D], F32))
    zbuf = E(nc.sbuf_tensor("zbuf", [P, STW], F32))
    rbuf = E(nc.sbuf_tensor("rbuf", [P, STW], F32))
    ebuf = E(nc.sbuf_tensor("ebuf", [P, STW], F32))

    onesrow = slotA[0:1, 8192:8320]
    rowBv = slotA[0:1, 8320:9344]
    rowBo = slotA[0:1, 9344:10368]
    Atb = slotB[:, 8192:8320]
    onescol = slotB[:, 8320:8321]
    rowBk = slotB[0:1, 8448:9472]
    OTb = slotB[:, 9472:10496].rearrange("p (c n) -> p c n", c=DC)
    Obuf = xnbuf

    nc.const_aps.register(F32, EPS_LN, sconst[:, 0:1])
    nc.const_aps.register(F32, -16.0, sconst[:, 1:2])
    nc.const_aps.register(F32, 16.0, sconst[:, 2:3])
    nc.const_aps.register(F32, 1e-6, sconst[:, 3:4])

    P0 = E(nc.psum_tensor("P0", [P, 512], F32))
    P1 = E(nc.psum_tensor("P1", [P, 512], F32))
    PT0 = E(nc.psum_tensor("PT0", [P, 512], F32))
    PT1 = E(nc.psum_tensor("PT1", [P, 512], F32))
    PW0 = E(nc.psum_tensor("PW0", [P, 1024], F32))
    PW1 = E(nc.psum_tensor("PW1", [P, 1024], F32))
    TRP = (PT0[:, 0:128], PT1[:, 0:128])
    APS = P0[:, 0:128]
    ZCOL1 = P1[:, 0:1]
    DENC = P1[:, 4:5]

    sX = kb.sem("sX")
    sW = kb.sem("sW")
    sT = kb.sem("sT")
    sV = kb.sem("sV")
    sS = kb.sem("sS")
    sO = kb.sem("sO")
    sC1 = kb.sem("sC1")
    sC2 = kb.sem("sC2")
    sC3 = kb.sem("sC3")
    sM = kb.sem("sM")
    sLA = kb.sem("sLA")
    sLB = kb.sem("sLB")
    sEX = kb.sem("sEX")
    sAB = kb.sem("sAB")

    def slotv(slot, nchunk=DC, width=D):
        return slot[:, 0:nchunk * width].rearrange("p (c n) -> p c n", c=nchunk)

    def load_weight(slot, w_dram, nchunk=DC, width=D, sem=None):
        if sem is None:
            sem = sLA if slot is slotA else sLB
        kb.wait("gp", sem)
        kb.dma("gp", slotv(slot, nchunk, width),
               w_dram.rearrange("(c p) n -> p c n", p=P), sem)
        return sem.n

    def ln_tile(tt, src=None):
        s = X[:, tt, :] if src is None else src
        kb.wait("ve", sT)
        kb.wait("ve", sV)
        kb.op("ve", lambda e, s=s: e.bn_stats(lnst[:, 0:6], s[:, 0:512]))
        kb.op("ve", lambda e, s=s: e.bn_stats(lnst[:, 6:12], s[:, 512:1024]), inc=sV)
        kb.wait("ve", sV)
        kb.op("ve", lambda e: e.bn_aggr(lnmv[:, 0:2], lnst[:]), inc=sV)
        kb.wait("sc", sV)
        kb.op("sc", lambda e: e.activation(lnmv[:, 2:3], lnmv[:, 1:2], AF.Sqrt,
                                           bias=EPS_LN, scale=1.0), inc=sS)
        kb.wait("ve", sS)
        kb.op("ve", lambda e: e.reciprocal(lnmv[:, 3:4], lnmv[:, 2:3]), inc=sV)
        kb.wait("ve", sV)
        kb.op("ve", lambda e, s=s: e.tensor_scalar(
            xnbuf[:], s, lnmv[:, 0:1], lnmv[:, 3:4], ALU.subtract, ALU.mult), inc=sV)
        return sV.n

    tr_last = {0: 0, 1: 0}

    def transpose128(src_ap, dst_ap, k, rows=128):
        pv = TRP[k % 2][0:rows, :]
        kb.wait("te", sS, tr_last[k % 2])
        kb.op("te", lambda e, s=src_ap, p=pv: e.transpose(p, s, sid32[:]), inc=sT)
        kb.wait("sc", sT)
        kb.op("sc", lambda e, p=pv, d=dst_ap: e.copy(d, p), inc=sS)
        tr_last[k % 2] = sS.n
        return sS.n

    def phi_evac(psum, bias_ap, dst_ap, width=STW):
        kb.wait("ve", sT)
        kb.wait("ve", sS)
        kb.op("ve", lambda e, p=psum, b=bias_ap: e.tensor_scalar(
            zbuf[:, 0:width], p, b, None, ALU.add), inc=sV)
        kb.wait("sc", sV)
        kb.op("sc", lambda e: e.activation(rbuf[:, 0:width], zbuf[:, 0:width], AF.Relu), inc=sS)
        kb.wait("ve", sS)
        kb.op("ve", lambda e: e.tensor_sub(zbuf[:, 0:width], zbuf[:, 0:width],
                                           rbuf[:, 0:width]), inc=sV)
        kb.wait("sc", sV)
        kb.op("sc", lambda e: e.activation(ebuf[:, 0:width], zbuf[:, 0:width], AF.Exp), inc=sS)
        kb.wait("ve", sS)
        kb.op("ve", lambda e, d=dst_ap: e.tensor_add(d, ebuf[:, 0:width],
                                                     rbuf[:, 0:width]), inc=sV)
        return sV.n

    kb.dma("sy", sconst[:], consts, sX)
    kb.dma("sy", sid32[:], ident, sX)
    kb.dma("sy", stril[:], tril, sX)
    kb.dma("sy", rmask[:], rmaskD, sX)
    kb.dma("sy", X[:], xs.rearrange("(n p) d -> p n d", p=P), sX)
    kb.dma("gp", sidbf[:], ident, sW)
    kb.op("ve", lambda e: e.memset(onesrow, 1.0))
    kb.op("ve", lambda e: e.memset(onescol, 1.0), inc=sV)
    wK = load_weight(slotA, aWkT)
    wV = load_weight(slotB, aWvT)
    kb.flush_block("init")

    if "attn_a" in phases:
        actx = contextlib.ExitStack()
        Ea = actx.enter_context
        arena = Ea(nc.sbuf_tensor("aArena", [P, 20480], BF16))
        xnT = arena[:, 0:4096].rearrange("p (c n) -> p c n", c=DC)
        QpT = arena[:, 4096:8192].rearrange("p (c n) -> p c n", c=DC)
        KpT = arena[:, 8192:12288].rearrange("p (c n) -> p c n", c=DC)
        Vst = arena[:, 12288:16384].rearrange("p (c n) -> p c n", c=4)
        Kptok = arena[:, 16384:20480].rearrange("p (c n) -> p c n", c=4)
        SgF = arena[:].bitcast(F32)
        Sg = SgF[:, 0:8200].rearrange("p (c n) -> p c n", c=DC)
        S = Ea(nc.sbuf_tensor("a_S", [P, DC, 1025], F32))
        Sbf = Ea(nc.sbuf_tensor("a_Sbf", [P, DC, 1025], BF16))
        denb = Ea(nc.sbuf_tensor("a_den", [P, 2], F32))
        aBqc = Ea(nc.sbuf_tensor("a_Bq", [P, DC], F32))
        aBkc = Ea(nc.sbuf_tensor("a_Bk", [P, DC], F32))

        kb.dma("sy", aBqc[:], aBq, sX)
        kb.dma("sy", aBkc[:], aBk, sX)
        kb.dma("gp", rowBv, aBv.rearrange("(a d) -> a d", a=1), sW)
        kb.dma("gp", rowBo, aBo.rearrange("(a d) -> a d", a=1), sW)
        kb.dma("gp", rowBk, aBkF.rearrange("(a d) -> a d", a=1), sW)
        kb.op("ve", lambda e: e.memset(S[:], 0.0), inc=sV)
        ldx = (sX.n, sW.n)

        def st_ln_transpose(st):
            import os as _os
            trlim = int(_os.environ.get("TRLIM", str(DC)))
            for i in range(4):
                vln = ln_tile(st * 4 + i)
                kb.wait("te", sV, vln)
                for jc in range(trlim):
                    transpose128(xnbuf[:, jc * P:(jc + 1) * P],
                                 xnT[:, jc, i * P:(i + 1) * P], jc)
            return sS.n

        pz_last = {0: 0, 1: 0}

        def proj_feature(slot, bias_c, dstT, vready):
            sl = slotv(slot)
            for oc in range(DC):
                ps = P0 if oc % 2 == 0 else P1
                kb.wait("te", sS, vready)
                kb.wait("te", sV, pz_last[oc % 2])
                for jc in range(DC):
                    kb.op("te", lambda e, o=oc, j=jc, p=ps, s=sl: e.matmul(
                        p[:], s[:, j, o * P:(o + 1) * P], xnT[:, j, :],
                        start=(j == 0), stop=(j == DC - 1)),
                        inc=sT if jc == DC - 1 else None)
                phi_evac(ps[:], bias_c[:, oc:oc + 1], dstT[:, oc, :])
                pz_last[oc % 2] = sV.n - 2
            return sV.n

        def phi_psum(psum_half, dst_ap, width=STW):
            kb.wait("sc", sT)
            kb.wait("sc", sV)
            kb.op("sc", lambda e, p=psum_half: e.activation(rbuf[:, 0:width], p, AF.Relu), inc=sS)
            kb.wait("ve", sS)
            kb.op("ve", lambda e, p=psum_half: e.tensor_sub(zbuf[:, 0:width], p, rbuf[:, 0:width]), inc=sV)
            kb.wait("sc", sV)
            kb.op("sc", lambda e: e.activation(ebuf[:, 0:width], zbuf[:, 0:width], AF.Exp), inc=sS)
            kb.wait("ve", sS)
            kb.op("ve", lambda e, d=dst_ap: e.tensor_add(d, ebuf[:, 0:width], rbuf[:, 0:width]), inc=sV)
            return sV.n

        def proj_Ktok(vready):
            sl = slotv(slotA)
            for i in range(4):
                kb.wait("te", sS, vready)
                kb.wait("te", sV)
                for jc in range(DC):
                    kb.op("te", lambda e, i=i, j=jc, s=sl: e.matmul(
                        PW0[:, 0:512], xnT[:, j, i * P:(i + 1) * P], s[:, j, 0:512],
                        start=(j == 0), stop=False))
                    kb.op("te", lambda e, i=i, j=jc, s=sl: e.matmul(
                        PW0[:, 512:1024], xnT[:, j, i * P:(i + 1) * P], s[:, j, 512:1024],
                        start=(j == 0), stop=False))
                kb.op("te", lambda e: e.matmul(PW0[:, 0:512], onesrow, rowBk[:, 0:512],
                                               start=False, stop=True))
                kb.op("te", lambda e: e.matmul(PW0[:, 512:1024], onesrow, rowBk[:, 512:1024],
                                               start=False, stop=True), inc=sT)
                phi_psum(PW0[:, 0:512], Kptok[:, i, 0:512])
                phi_psum(PW0[:, 512:1024], Kptok[:, i, 512:1024])

        def proj_V(vready):
            sl = slotv(slotB)
            for i in range(4):
                kb.wait("te", sS, vready)
                kb.wait("te", sV)
                for jc in range(DC):
                    kb.op("te", lambda e, i=i, j=jc, s=sl: e.matmul(
                        PW0[:, 0:512], xnT[:, j, i * P:(i + 1) * P], s[:, j, 0:512],
                        start=(j == 0), stop=False))
                    kb.op("te", lambda e, i=i, j=jc, s=sl: e.matmul(
                        PW0[:, 512:1024], xnT[:, j, i * P:(i + 1) * P], s[:, j, 512:1024],
                        start=(j == 0), stop=False))
                kb.op("te", lambda e: e.matmul(PW0[:, 0:512], onesrow, rowBv[:, 0:512],
                                               start=False, stop=True))
                kb.op("te", lambda e: e.matmul(PW0[:, 512:1024], onesrow, rowBv[:, 512:1024],
                                               start=False, stop=True), inc=sT)
                kb.wait("ve", sT)
                kb.op("ve", lambda e, i=i: e.tensor_copy(Vst[:, i, :], PW0[:]), inc=sV)
            return sV.n

        kvfree = {0: 0, 1: 0}

        def chunk_KV(i):
            for dc in range(DC):
                pw = PW1 if dc % 2 == 0 else PW0
                if dc < 2:
                    kb.wait("te", sV)
                else:
                    kb.wait("te", sV, kvfree[dc % 2])
                kb.op("te", lambda e, d=dc, i=i, p=pw: e.matmul(
                    p[:, 0:512], Kptok[:, i, d * P:(d + 1) * P], Vst[:, i, 0:512],
                    start=True, stop=True))
                kb.op("te", lambda e, d=dc, i=i, p=pw: e.matmul(
                    p[:, 512:1024], Kptok[:, i, d * P:(d + 1) * P], Vst[:, i, 512:1024],
                    start=True, stop=True))
                kb.wait("te", sV, state["zfree"])
                kb.op("te", lambda e, d=dc, i=i: e.matmul(
                    ZCOL1, Kptok[:, i, d * P:(d + 1) * P], onescol, start=True, stop=True), inc=sT)
                kb.wait("ve", sT)
                kb.op("ve", lambda e, d=dc, p=pw: e.tensor_add(S[:, d, 0:1024], S[:, d, 0:1024], p[:]))
                kb.op("ve", lambda e, d=dc: e.tensor_add(S[:, d, 1024:1025], S[:, d, 1024:1025],
                                                         ZCOL1), inc=sV)
                kvfree[dc % 2] = sV.n
                state["zfree"] = sV.n

        state = {"vsbf": 0, "zfree": 0}

        def mixing_chunk(st, i, s2=9):
            tt = st * 4 + i
            if s2 < 1:
                return
            kb.wait("te", sV)
            for dc in range(DC):
                kb.op("te", lambda e, d=dc, i=i: e.matmul(
                    APS, KpT[:, d, i * P:(i + 1) * P], QpT[:, d, i * P:(i + 1) * P],
                    start=(d == 0), stop=(d == DC - 1)), inc=sT if dc == DC - 1 else None)
            kb.wait("ve", sT)
            kb.op("ve", lambda e: e.tensor_mul(Atb, APS, stril[:]), inc=sV)
            vatb = sV.n
            if s2 < 2:
                return
            chunk_KV(i)
            if s2 < 3:
                return
            vs = state["vsbf"]
            if isinstance(vs, tuple):
                kb.wait("te", sS, vs[1])
            else:
                kb.wait("te", sV, vs)
            kb.wait("te", sV)
            for dc in range(DC):
                kb.op("te", lambda e, d=dc, i=i: e.matmul(
                    PW0[:, 0:512], QpT[:, d, i * P:(i + 1) * P], Sbf[:, d, 0:512],
                    start=(d == 0), stop=False))
                kb.op("te", lambda e, d=dc, i=i: e.matmul(
                    PW0[:, 512:1024], QpT[:, d, i * P:(i + 1) * P], Sbf[:, d, 512:1024],
                    start=(d == 0), stop=False))
                kb.op("te", lambda e, d=dc, i=i: e.matmul(
                    DENC, QpT[:, d, i * P:(i + 1) * P], Sbf[:, d, 1024:1025],
                    start=(d == 0), stop=False))
            kb.op("te", lambda e, i=i: e.matmul(PW0[:, 0:512], Atb, Vst[:, i, 0:512],
                                                start=False, stop=True))
            kb.op("te", lambda e, i=i: e.matmul(PW0[:, 512:1024], Atb, Vst[:, i, 512:1024],
                                                start=False, stop=True))
            kb.op("te", lambda e: e.matmul(DENC, Atb, onescol, start=False, stop=True), inc=sT)
            vnum = sT.n
            kb.wait("ve", sT, vnum)
            kb.op("ve", lambda e: e.tensor_scalar(denb[:, 0:1], DENC, 1e-6, None, ALU.add), inc=sV)
            kb.wait("ve", sV)
            kb.op("ve", lambda e: e.reciprocal(denb[:, 1:2], denb[:, 0:1]), inc=sV)
            kb.wait("ve", sV)
            kb.op("ve", lambda e: e.tensor_scalar(Obuf[:], PW0[:], denb[:, 1:2], None, ALU.mult), inc=sV)
            vO = sV.n
            if s2 < 4:
                return
            kb.wait("sc", sT, vnum)
            kb.wait("sc", sV)
            kb.op("sc", lambda e: e.copy(Sbf[:], S[:]), inc=sS)
            state["vsbf"] = ("sS", sS.n)
            kb.wait("te", sV, vO)
            for ec in range(DC):
                transpose128(Obuf[:, ec * P:(ec + 1) * P], OTb[:, ec, :], ec)
            vOT = sS.n
            sl = slotv(slotB)
            kb.wait("te", sS, vOT)
            kb.wait("te", sV)
            for ec in range(DC):
                kb.op("te", lambda e, c=ec, s=sl: e.matmul(
                    PW1[:, 0:512], OTb[:, c, :], s[:, c, 0:512], start=(c == 0), stop=False))
                kb.op("te", lambda e, c=ec, s=sl: e.matmul(
                    PW1[:, 512:1024], OTb[:, c, :], s[:, c, 512:1024], start=(c == 0), stop=False))
            kb.op("te", lambda e: e.matmul(PW1[:, 0:512], onesrow, rowBo[:, 0:512],
                                           start=False, stop=True))
            kb.op("te", lambda e: e.matmul(PW1[:, 512:1024], onesrow, rowBo[:, 512:1024],
                                           start=False, stop=True), inc=sT)
            kb.wait("ve", sT)
            kb.op("ve", lambda e, t=tt: e.tensor_add(X[:, t, :], X[:, t, :], PW1[:]), inc=sV)

        import os
        sub = int(os.environ.get("SUBLEVEL", "9"))
        nst_lim = int(os.environ.get("NST_LIM", str(NST)))
        kb.wait("te", sX, ldx[0])
        kb.wait("te", sW, ldx[1])
        kb.wait("te", sLA, wK)
        kb.wait("te", sLB, wV)
        kb.wait("ve", sX, ldx[0])
        def st_KV_b1():
            for dc in range(DC):
                pw = PW1 if dc % 2 == 0 else PW0
                if dc < 2:
                    kb.wait("te", sV)
                else:
                    kb.wait("te", sV, kvfree[dc % 2])
                kb.wait("te", sV, state["zfree"])
                for i in range(4):
                    kb.op("te", lambda e, d=dc, i=i, p=pw: e.matmul(
                        p[:, 0:512], Kptok[:, i, d * P:(d + 1) * P], Vst[:, i, 0:512],
                        start=(i == 0), stop=(i == 3)))
                    kb.op("te", lambda e, d=dc, i=i, p=pw: e.matmul(
                        p[:, 512:1024], Kptok[:, i, d * P:(d + 1) * P], Vst[:, i, 512:1024],
                        start=(i == 0), stop=(i == 3)))
                    kb.op("te", lambda e, d=dc, i=i: e.matmul(
                        ZCOL1, Kptok[:, i, d * P:(d + 1) * P], onescol,
                        start=(i == 0), stop=(i == 3)),
                        inc=sT if i == 3 else None)
                kb.wait("ve", sT)
                kb.op("ve", lambda e, d=dc, p=pw: e.tensor_add(S[:, d, 0:1024], S[:, d, 0:1024], p[:]))
                kb.op("ve", lambda e, d=dc: e.tensor_add(S[:, d, 1024:1025], S[:, d, 1024:1025],
                                                         ZCOL1), inc=sV)
                kvfree[dc % 2] = sV.n
                state["zfree"] = sV.n

        for st in range(nst_lim):
            vtr = st_ln_transpose(st)
            if sub == 4:
                proj_feature(slotA, aBkc, KpT, vtr)
            if sub >= 1 and sub != 4:
                proj_Ktok(vtr)
            if sub >= 2:
                proj_V(vtr)
            if sub >= 3:
                st_KV_b1()

        if alevel < 2:
            actx.close()
            kb.flush_block("attn_a")
    if "attn_a" in phases and alevel >= 2:
        kb.wait("gp", sV)
        kb.dma("gp", cc_s_in.rearrange("(c p) e -> p c e", p=P), S[:], sEX)
        kb.wait("gp", sEX)
        kb.op("gp", lambda e: e.collective_compute(
            "AllGather", ALU.bypass, replica_groups=GROUPS,
            ins=[cc_s_in], outs=[cc_s_out]), inc=sC1)
        kb.wait("gp", sC1, 1)
        for r in range(4):
            kb.wait("gp", sV)
            kb.dma("gp", Sg, cc_s_out[r * D:(r + 1) * D, :].rearrange("(c p) e -> p c e", p=P), sEX)
            kb.wait("ve", sEX)
            if r == 0:
                kb.op("ve", lambda e: e.tensor_scalar(
                    S[:], Sg, rmask[:, 0:1], None, ALU.mult), inc=sV)
            else:
                kb.op("ve", lambda e, r=r: e.scalar_tensor_tensor(
                    S[:], Sg, rmask[:, r:r + 1], S[:], ALU.mult, ALU.add), inc=sV)
        kb.wait("ve", sV)
        kb.op("ve", lambda e: e.tensor_copy(Sbf[:], S[:]), inc=sV)
        state["vsbf"] = sV.n
        vexch = sV.n

        kb.wait("gp", sT)
        wQ = load_weight(slotA, aWqT)

        if alevel < 3:
            actx.close()
            kb.flush_block("attn_a")

    if "attn_a" in phases and alevel >= 3:
        import os
        nst_lim2 = int(os.environ.get("NST_LIM2", str(NST)))
        wVcur = wV
        s3 = int(os.environ.get("SUB3", "9"))
        for st in range(nst_lim2):
            if st == 0:
                kb.wait("ve", sV, vexch)
                kb.wait("sc", sV, vexch)
            if s3 >= -1:
                if s3 == -1:
                    for _i4 in range(4):
                        ln_tile(st * 4 + _i4)
                else:
                    vtr = st_ln_transpose(st)
            if s3 >= 1:
                kb.wait("te", sLA, wQ)
                proj_feature(slotA, aBqc, QpT, vtr)
            if s3 >= 2:
                kb.wait("gp", sT)
                wKn = load_weight(slotA, aWkT)
                kb.wait("te", sLA, wKn)
                proj_feature(slotA, aBkc, KpT, vtr)
            if s3 >= 3:
                proj_Ktok(vtr)
            if s3 >= 4:
                kb.wait("te", sLB, wVcur)
                proj_V(vtr)
            if s3 >= 5:
                kb.wait("gp", sT)
                wO = load_weight(slotB, aWoT)
                kb.wait("te", sLB, wO)
            if s3 >= 6:
                for i in range(4):
                    mixing_chunk(st, i, s2=int(os.environ.get("SUB2", "9")))
            if s3 >= 5 and st < NST - 1:
                kb.wait("gp", sT)
                wVcur = load_weight(slotB, aWvT)
                wQ = load_weight(slotA, aWqT)

        actx.close()
        kb.flush_block("attn_a")

    def emit_ffn(tag):
        fctx = contextlib.ExitStack()
        Ef = fctx.enter_context
        xnTf = Ef(nc.sbuf_tensor(tag + "_xnT", [P, DC, T], BF16))
        midT = Ef(nc.sbuf_tensor(tag + "_midT", [P, FIC, T], BF16))
        cmb = Ef(nc.sbuf_tensor(tag + "_cmb", [P, NT, 9], F32))
        rt = Ef(nc.sbuf_tensor(tag + "_rt", [P, 24], F32))
        gw9 = Ef(nc.sbuf_tensor(tag + "_gw9", [P, DC, 9], BF16))

        kb.dma("gp", gw9[:], gateW9T.rearrange("(c p) n -> p c n", p=P), sW)
        wg9 = sW.n
        kb.wait("ve", sX)

        wlist = []
        for e in range(NE):
            wlist += [(fWg[e], DC, FI), (fWu[e], DC, FI), (fWd[e], FIC, D)]
        wlist += [(fswG, DC, FI), (fswU, DC, FI), (fswD, FIC, D)]
        slots = [slotA, slotB]
        ssem = [sLA, sLB]
        wcnt = [0] * len(wlist)

        def load_wi(i):
            if i >= len(wlist):
                return
            dram, nch, wid = wlist[i]
            kb.wait("gp", sT)
            wcnt[i] = load_weight(slots[i % 2], dram, nch, wid)

        load_wi(0)
        load_wi(1)

        for tt in range(NT):
            vln = ln_tile(tt)
            kb.wait("te", sV, vln)
            for jc in range(DC):
                transpose128(xnbuf[:, jc * P:(jc + 1) * P],
                             xnTf[:, jc, tt * P:(tt + 1) * P], jc)
            vtr = sS.n
            kb.wait("te", sS, vtr)
            kb.wait("te", sV)
            kb.wait("te", sW, wg9)
            for jc in range(DC):
                kb.op("te", lambda e, j=jc, t=tt: e.matmul(
                    P0[:, 0:9], xnTf[:, j, t * P:(t + 1) * P], gw9[:, j, :],
                    start=(j == 0), stop=(j == DC - 1)), inc=sT if jc == DC - 1 else None)
            kb.wait("ve", sT)
            kb.wait("ve", sS)
            kb.op("ve", lambda e: e.tensor_copy(rt[:, 0:9], P0[:, 0:9]), inc=sV)
            kb.wait("ve", sV)
            kb.op("ve", lambda e: e.reduce_max(rt[:, 9:10], rt[:, 0:8], axis=AX,
                                               negate=True), inc=sV)
            kb.wait("sc", sV)
            kb.op("sc", lambda e: e.activation(rt[:, 16:24], rt[:, 0:8], AF.Exp,
                                               bias=rt[:, 9:10], scale=1.0), inc=sS)
            kb.op("sc", lambda e, t=tt: e.activation(cmb[:, t, 8:9], rt[:, 8:9],
                                                     AF.Sigmoid), inc=sS)
            kb.wait("ve", sS)
            kb.op("ve", lambda e: e.reduce_max(rt[:, 13:14], rt[:, 16:24], axis=AX), inc=sV)
            kb.wait("ve", sV)
            kb.op("ve", lambda e, t=tt: e.tensor_scalar(
                cmb[:, t, 0:8], rt[:, 16:24], rt[:, 13:14], 1e30, ALU.is_ge, ALU.mult), inc=sV)
            kb.wait("ve", sV)
            kb.op("ve", lambda e, t=tt: e.tensor_sub(cmb[:, t, 0:8], rt[:, 16:24],
                                                     cmb[:, t, 0:8]), inc=sV)
            kb.wait("ve", sV)
            kb.op("ve", lambda e, t=tt: e.reduce_max(rt[:, 10:11], cmb[:, t, 0:8],
                                                     axis=AX), inc=sV)
            kb.wait("ve", sV)
            kb.op("ve", lambda e: e.tensor_tensor(rt[:, 11:12], rt[:, 13:14],
                                                  rt[:, 10:11], ALU.add), inc=sV)
            kb.wait("ve", sV)
            kb.op("ve", lambda e: e.reciprocal(rt[:, 12:13], rt[:, 11:12]), inc=sV)
            kb.wait("ve", sV)
            kb.op("ve", lambda e, t=tt: e.scalar_tensor_tensor(
                cmb[:, t, 0:8], rt[:, 16:24], rt[:, 10:11], rt[:, 16:24],
                ALU.is_ge, ALU.mult), inc=sV)
            kb.wait("ve", sV)
            kb.op("ve", lambda e, t=tt: e.tensor_scalar(
                cmb[:, t, 0:8], cmb[:, t, 0:8], rt[:, 12:13], None, ALU.mult), inc=sV)
        vxnT = sS.n

        gbank = (P0, PT0)
        ubank = (P1, PT1)
        gfree = {0: 0, 1: 0}
        ufree = {0: 0, 1: 0}
        dfree = {0: 0, 1: 0}
        vdown_prev = 0

        kb.wait("te", sV)
        for ei in range(NE + 1):
            i_g, i_u, i_d = 3 * ei, 3 * ei + 1, 3 * ei + 2
            slg = slotv(slots[i_g % 2], DC, FI)
            slu = slotv(slots[i_u % 2], DC, FI)
            kb.wait("te", ssem[i_g % 2], wcnt[i_g])
            kb.wait("te", ssem[i_u % 2], wcnt[i_u])
            kb.wait("te", sS, vxnT)
            kb.wait("ve", sT, vdown_prev)
            kb.wait("te", sV)
            gps2 = ((P0[:], PT0[:]), (PW0[:, 0:512], PW0[:, 512:1024]))
            ups2 = ((P1[:], PT1[:]), (PW1[:, 0:512], PW1[:, 512:1024]))
            sil2 = ((zbuf[:, 0:512], rbuf[:, 0:512]), (ebuf[:, 0:512], xnbuf[:, 0:512]))
            gq2free = [0, 0]
            uq2free = [0, 0]
            it = 0
            for ic in range(FIC):
                for tqp in range(2):
                    par = it % 2
                    it += 1
                    tq0, tq1 = 2 * tqp, 2 * tqp + 1
                    kb.wait("te", sS, gq2free[par])
                    kb.wait("te", sV, uq2free[par])
                    for jc in range(DC):
                        for k, tq in enumerate((tq0, tq1)):
                            kb.op("te", lambda e, j=jc, c=ic, q=tq, s=slg, p=gps2[par][k]: e.matmul(
                                p, s[:, j, c * P:(c + 1) * P],
                                xnTf[:, j, q * 512:(q + 1) * 512],
                                start=(j == 0), stop=(j == DC - 1)))
                        for k, tq in enumerate((tq0, tq1)):
                            kb.op("te", lambda e, j=jc, c=ic, q=tq, s=slu, p=ups2[par][k]: e.matmul(
                                p, s[:, j, c * P:(c + 1) * P],
                                xnTf[:, j, q * 512:(q + 1) * 512],
                                start=(j == 0), stop=(j == DC - 1)),
                                inc=sT if (jc == DC - 1 and k == 1) else None)
                    kb.wait("sc", sT)
                    kb.wait("sc", sV, uq2free[par])
                    for k in range(2):
                        kb.op("sc", lambda e, k=k, par=par: e.activation(
                            sil2[par][k], gps2[par][k], AF.Silu), inc=sS)
                    gq2free[par] = sS.n
                    kb.wait("ve", sS)
                    for k, tq in enumerate((tq0, tq1)):
                        kb.op("ve", lambda e, c=ic, q=tq, k=k, par=par: e.tensor_tensor(
                            midT[:, c, q * 512:(q + 1) * 512], sil2[par][k], ups2[par][k],
                            ALU.mult), inc=sV)
                    uq2free[par] = sV.n
            vmid = sV.n
            load_wi(i_d)
            load_wi(i_g + 3)
            sld = slotv(slots[i_d % 2], FIC, D)
            kb.wait("te", ssem[i_d % 2], wcnt[i_d])
            kb.wait("ve", sT)
            kb.wait("te", sV, vmid)
            kb.wait("te", sS)
            for tc in range(NT):
                b = tc % 2
                pw = (PW0, PW1)[b]
                kb.wait("te", sV, dfree[b])
                for ic in range(FIC):
                    kb.op("te", lambda e, c=ic, t=tc, s=sld, p=pw: e.matmul(
                        p[:, 0:512], midT[:, c, t * P:(t + 1) * P], s[:, c, 0:512],
                        start=(c == 0), stop=(c == FIC - 1)))
                    kb.op("te", lambda e, c=ic, t=tc, s=sld, p=pw: e.matmul(
                        p[:, 512:1024], midT[:, c, t * P:(t + 1) * P], s[:, c, 512:1024],
                        start=(c == 0), stop=(c == FIC - 1)),
                        inc=sT if ic == FIC - 1 else None)
                kb.wait("ve", sT)
                kb.op("ve", lambda e, t=tc, ei=ei, p=pw: e.scalar_tensor_tensor(
                    X[:, t, :], p[:], cmb[:, t, ei:ei + 1], X[:, t, :],
                    ALU.mult, ALU.add), inc=sV)
                dfree[b] = sV.n
            vdown_prev = sT.n
            load_wi(i_u + 3)
        fctx.close()

    if "ffn1" in phases:
        emit_ffn("f1")
        kb.flush_block("ffn1")

    def emit_attn_b():
        import os as _os
        b3 = int(_os.environ.get("B3LEVEL", "9"))
        bctx = contextlib.ExitStack()
        Eb = bctx.enter_context
        qhT = Eb(nc.sbuf_tensor("b_qhT", [P, 4, T], BF16))
        ob2 = Eb(nc.sbuf_tensor("b_ob2", [P, D], F32))
        pta = (slotA[:, 1024:1536], slotA[:, 1536:2048])
        ptb2 = (slotA[:, 2048:2560], slotA[:, 2560:3072])
        onesb = slotA[0:1, 3072:3136]
        dre = (slotA[0:1, 3136:3648], slotA[0:1, 3648:4160])
        abtmp = slotA[:, 4224:4736]
        bq_s = Eb(nc.sbuf_tensor("b_qs", [P, 32], F32))

        sub1 = contextlib.ExitStack()
        xnnT = sub1.enter_context(nc.sbuf_tensor("b_xnnT", [P, DC, T], BF16))
        bWqS = sub1.enter_context(nc.sbuf_tensor("b_Wq", [P, DC, 512], BF16))
        bWkvS = sub1.enter_context(nc.sbuf_tensor("b_Wkv", [P, DC, P], BF16))
        bGc = sub1.enter_context(nc.sbuf_tensor("b_gc", [P, DC], F32))
        kTl = sub1.enter_context(nc.sbuf_tensor("b_kTl", [64, T], BF16))
        vloc = sub1.enter_context(nc.sbuf_tensor("b_vloc", [P, NT, 64], BF16))

        for _e in ("ve", "sc", "te", "sy"):
            kb.wait(_e, sLA)
            kb.wait(_e, sLB)
        kb.op("ve", lambda e: e.memset(onesb, 1.0), inc=sV)
        kb.wait("sy", sM)
        kb.dma("sy", bGc[:], bNormG, sM)
        kb.dma("gp", bWqS[:], bWq.rearrange("(c p) n -> p c n", p=P), sW)
        kb.dma("gp", bWkvS[:], bWkv.rearrange("(c p) n -> p c n", p=P), sW)
        wqv = sW.n
        vgc = sM.n
        kb.wait("ve", sW, wqv)
        kb.wait("ve", sM, vgc)
        for jc in range(DC):
            kb.op("ve", lambda e, j=jc: e.tensor_scalar(
                bWqS[:, j, :], bWqS[:, j, :], bGc[:, j:j + 1], None, ALU.mult), inc=sV)
            kb.op("ve", lambda e, j=jc: e.tensor_scalar(
                bWkvS[:, j, :], bWkvS[:, j, :], bGc[:, j:j + 1], None, ALU.mult), inc=sV)
        vfold = sV.n
        kb.wait("ve", sX)

        for tt in range(NT):
            ln_tile(tt)
            ln_tile(tt, src=xnbuf[:])
            vln = sV.n
            kb.wait("te", sV, vln)
            for jc in range(DC):
                transpose128(xnbuf[:, jc * P:(jc + 1) * P],
                             xnnT[:, jc, tt * P:(tt + 1) * P], jc)
            vtr = sS.n
            kb.wait("te", sS, vtr)
            kb.wait("te", sV)
            kb.wait("te", sW, wqv)
            kb.wait("te", sV, vfold)
            for jc in range(DC):
                kb.op("te", lambda e, j=jc, t=tt: e.matmul(
                    PW0[:, 0:512], xnnT[:, j, t * P:(t + 1) * P], bWqS[:, j, :],
                    start=(j == 0), stop=(j == DC - 1)))
            for jc in range(DC):
                kb.op("te", lambda e, j=jc, t=tt: e.matmul(
                    P1[:, 0:128], xnnT[:, j, t * P:(t + 1) * P], bWkvS[:, j, :],
                    start=(j == 0), stop=(j == DC - 1)), inc=sT if jc == DC - 1 else None)
            vqkv = sT.n
            kb.wait("ve", sT, vqkv)
            kb.wait("ve", sS)
            kb.op("ve", lambda e: e.tensor_copy(zbuf[:], PW0[:, 0:512]), inc=sV)
            kb.wait("sc", sV)
            kb.op("sc", lambda e: e.activation(rbuf[:], zbuf[:], AF.Square), inc=sS)
            kb.wait("ve", sS)
            kb.op("ve", lambda e: e.reduce_sum(
                bq_s[:, 0:8], rbuf[:].rearrange("p (h d) -> p h d", h=8), axis=AX), inc=sV)
            kb.wait("sc", sV)
            kb.op("sc", lambda e: e.activation(bq_s[:, 8:16], bq_s[:, 0:8], AF.Sqrt), inc=sS)
            kb.wait("ve", sS)
            kb.op("ve", lambda e: e.tensor_scalar(bq_s[:, 8:16], bq_s[:, 8:16], 1e-12,
                                                  None, ALU.max), inc=sV)
            kb.wait("ve", sV)
            kb.op("ve", lambda e: e.reciprocal(bq_s[:, 16:24], bq_s[:, 8:16]), inc=sV)
            kb.wait("ve", sV)
            for h in range(8):
                kb.op("ve", lambda e, h=h: e.tensor_scalar(
                    rbuf[:, h * 64:(h + 1) * 64], zbuf[:, h * 64:(h + 1) * 64],
                    bq_s[:, 16 + h:17 + h], None, ALU.mult), inc=sV)
            vqh = sV.n
            kb.wait("te", sV, vqh)
            for c4 in range(4):
                transpose128(rbuf[:, c4 * P:(c4 + 1) * P], qhT[:, c4, tt * P:(tt + 1) * P],
                             c4)
            kb.wait("ve", sT, vqkv)
            kb.op("ve", lambda e: e.tensor_copy(ebuf[:, 0:128], P1[:, 0:128]), inc=sV)
            kb.wait("sc", sV)
            kb.op("sc", lambda e: e.activation(ebuf[:, 128:192], ebuf[:, 0:64], AF.Square), inc=sS)
            kb.wait("ve", sS)
            kb.op("ve", lambda e: e.reduce_sum(bq_s[:, 24:25], ebuf[:, 128:192], axis=AX), inc=sV)
            kb.wait("sc", sV)
            kb.op("sc", lambda e: e.activation(bq_s[:, 25:26], bq_s[:, 24:25], AF.Sqrt), inc=sS)
            kb.wait("ve", sS)
            kb.op("ve", lambda e: e.tensor_scalar(bq_s[:, 25:26], bq_s[:, 25:26], 1e-12,
                                                  None, ALU.max), inc=sV)
            kb.wait("ve", sV)
            kb.op("ve", lambda e: e.reciprocal(bq_s[:, 26:27], bq_s[:, 25:26]), inc=sV)
            kb.wait("ve", sV)
            kb.op("ve", lambda e: e.tensor_scalar(ebuf[:, 192:256], ebuf[:, 0:64],
                                                  bq_s[:, 26:27], None, ALU.mult), inc=sV)
            kb.op("ve", lambda e, t=tt: e.tensor_copy(vloc[:, t, :], ebuf[:, 64:128]), inc=sV)
            vkh = sV.n
            kb.wait("te", sV, vkh)
            transpose128(ebuf[:, 192:256], kTl[:, tt * P:(tt + 1) * P], tt, rows=64)
        vprojs = sS.n

        if b3 < 2:
            sub1.close()
            bctx.close()
            return
        kb.wait("gp", sS, vprojs)
        kb.wait("gp", sV)
        kb.dma("gp", cc_k_in, kTl[:], sEX)
        kb.dma("gp", ccvi.rearrange("(n p) d -> p n d", p=P), vloc[:], sEX)
        kb.wait("gp", sEX)
        kb.op("gp", lambda e: e.collective_compute(
            "AllGather", ALU.bypass, replica_groups=GROUPS,
            ins=[cc_k_in], outs=[cc_k_out]), inc=sC2)
        kb.op("gp", lambda e: e.collective_compute(
            "AllGather", ALU.bypass, replica_groups=GROUPS,
            ins=[ccvi], outs=[ccvo]), inc=sC3)
        kb.wait("gp", sC2, 1)
        kb.wait("gp", sC3, 1)
        kb.wait("gp", sT)
        sub1.close()
        kTg = Eb(nc.sbuf_tensor("b_kTg", [P, 4, T], BF16))
        vg = Eb(nc.sbuf_tensor("b_vg", [P, 64, 128], BF16))
        attT = Eb(nc.sbuf_tensor("b_attT", [P, 4, T], BF16))
        bWoS = Eb(nc.sbuf_tensor("b_Wo", [P, 4, D], BF16))
        gOutS = Eb(nc.sbuf_tensor("b_go", [P, D], F32))
        kb.dma("gp", kTg[0:64, :, :], cc_k_out.rearrange("(r d) t -> d r t", d=64), sEX)
        kb.dma("gp", kTg[64:128, :, :], cc_k_out.rearrange("(r d) t -> d r t", d=64), sEX)
        kb.dma("gp", vg[:, :, 0:64], ccvo.rearrange("(n p) d -> p n d", p=P), sEX)
        kb.dma("gp", bWoS[:], bWout.rearrange("(c p) n -> p c n", p=P), sW)
        kb.wait("sy", sM)
        kb.dma("sy", gOutS[:], gOutB, sM)
        vld2 = (sEX.n, sW.n, sM.n)
        kb.wait("ve", sT)
        kb.op("ve", lambda e: e.memset(vg[:, :, 64:128], 0.0), inc=sV)
        kb.wait("ve", sV)
        kb.op("ve", lambda e: e.memset(vg[:, :, 64:65], 1.0), inc=sV)
        vvg = sV.n

        if b3 < 3:
            bctx.close()
            return
        sim3 = ((P0[:], P1[:]), (PT0[:], PT1[:]), (PW1[:, 0:512], PW1[:, 512:1024]))
        kb.wait("te", sEX, vld2[0])
        kb.wait("te", sW, vld2[1])
        kb.wait("te", sV, vvg)
        vout_ev = 0
        for hp in range(4):
            for qt in range(4):
                qsl = slice(qt * 512, (qt + 1) * 512)
                simdone = {}
                expdone = {}
                pvdone = {}

                def rec_sim(j, hp=hp, qsl=qsl):
                    pa, pb = sim3[j % 3]
                    r, tl = j // 16, (j % 16) * 128
                    kb.op("te", lambda e, p=pa, r=r, tl=tl: e.matmul(
                        p, kTg[0:64, r, tl:tl + 128], qhT[0:64, hp, qsl],
                        start=True, stop=True))
                    kb.op("te", lambda e, p=pb, r=r, tl=tl: e.matmul(
                        p, kTg[64:128, r, tl:tl + 128], qhT[64:128, hp, qsl],
                        start=True, stop=True), inc=sT)
                    simdone[j] = sT.n

                def rec_exp(j):
                    kb.wait("sc", sT, simdone[j])
                    if j >= 2:
                        kb.wait("sc", sT, pvdone[j - 2])
                    pa, pb = sim3[j % 3]
                    kb.op("sc", lambda e, p=pa, d=pta[j % 2]: e.activation(
                        d, p, AF.Exp, bias=-16.0, scale=16.0))
                    kb.op("sc", lambda e, p=pb, d=ptb2[j % 2]: e.activation(
                        d, p, AF.Exp, bias=-16.0, scale=16.0), inc=sS)
                    expdone[j] = sS.n

                def rec_pv(j):
                    kb.wait("te", sS, expdone[j])
                    kb.op("te", lambda e, j=j, d=pta[j % 2]: e.matmul(
                        PW0[:, 0:512], vg[:, j, :], d,
                        start=(j == 0), stop=(j == 63)))
                    kb.op("te", lambda e, j=j, d=ptb2[j % 2]: e.matmul(
                        PW0[:, 512:1024], vg[:, j, :], d,
                        start=(j == 0), stop=(j == 63)), inc=sT)
                    pvdone[j] = sT.n

                kb.wait("te", sV, vout_ev)
                kb.wait("te", sS)
                rec_sim(0)
                rec_exp(0)
                rec_sim(1)
                rec_exp(1)
                for j in range(64):
                    if j + 2 < 64:
                        rec_sim(j + 2)
                    rec_pv(j)
                    if j + 2 < 64:
                        rec_exp(j + 2)
                vpv = sT.n
                kb.wait("ve", sT, vpv)
                kb.op("ve", lambda e: e.tensor_copy(xnbuf[0:1, 0:512], PW0[64:65, 0:512]))
                kb.op("ve", lambda e: e.tensor_copy(xnbuf[0:1, 512:1024], PW0[64:65, 512:1024]), inc=sV)
                kb.wait("ve", sV)
                kb.op("ve", lambda e: e.reciprocal(zbuf[0:1, 0:512], xnbuf[0:1, 0:512]), inc=sV)
                kb.op("ve", lambda e: e.reciprocal(rbuf[0:1, 0:512], xnbuf[0:1, 512:1024]), inc=sV)
                kb.wait("ve", sV)
                kb.op("ve", lambda e: e.tensor_copy(dre[0], zbuf[0:1, 0:512]))
                kb.op("ve", lambda e: e.tensor_copy(dre[1], rbuf[0:1, 0:512]), inc=sV)
                vdre = sV.n
                kb.wait("te", sV, vdre)
                kb.wait("te", sS)
                kb.op("te", lambda e: e.matmul(P0[0:64, :], onesb, dre[0],
                                               start=True, stop=True))
                kb.op("te", lambda e: e.matmul(P1[0:64, :], onesb, dre[1],
                                               start=True, stop=True), inc=sT)
                kb.wait("sc", sT)
                kb.op("sc", lambda e: e.copy(zbuf[0:64, :], P0[0:64, :]))
                kb.op("sc", lambda e: e.copy(rbuf[0:64, :], P1[0:64, :]), inc=sS)
                kb.wait("ve", sS)
                kb.op("ve", lambda e, hp=hp, q=qsl: e.tensor_mul(
                    attT[0:64, hp, q], PW0[0:64, 0:512], zbuf[0:64, :]), inc=sV)
                kb.wait("ve", sV)
                kb.op("ve", lambda e: e.tensor_mul(
                    abtmp[0:64, :], PW0[0:64, 512:1024], rbuf[0:64, :]), inc=sV)
                vout_ev = sV.n
                kb.wait("sy", sV, vout_ev)
                kb.dma("sy", attT[64:128, hp, qsl], abtmp[0:64, :], sAB)
                kb.wait("ve", sAB)

        if b3 < 4:
            bctx.close()
            return
        vatt = sAB.n
        kb.wait("te", sAB, vatt)
        ofree = 0
        xfree = {0: 0, 1: 0}
        for tc in range(NT):
            xnb = xnbuf[:] if tc % 2 == 0 else ob2[:]
            kb.wait("te", sV, ofree)
            kb.wait("te", sS)
            for hp in range(4):
                for half in range(2):
                    hs = slice(half * 512, (half + 1) * 512)
                    kb.op("te", lambda e, hp=hp, t=tc, hs=hs: e.matmul(
                        PW0[:, hs], attT[0:64, hp, t * P:(t + 1) * P], bWoS[0:64, hp, hs],
                        start=(hp == 0), stop=(hp == 3)))
                    kb.op("te", lambda e, hp=hp, t=tc, hs=hs: e.matmul(
                        PW1[:, hs], attT[64:128, hp, t * P:(t + 1) * P], bWoS[64:128, hp, hs],
                        start=(hp == 0), stop=(hp == 3)),
                        inc=sT if (hp == 3 and half == 1) else None)
            kb.wait("sc", sT)
            kb.wait("sc", sV, xfree[tc % 2])
            kb.op("sc", lambda e, d=xnb: e.copy(d, PW1[:]), inc=sS)
            kb.wait("ve", sS)
            kb.wait("ve", sM, vld2[2])
            kb.op("ve", lambda e, d=xnb: e.tensor_add(d, d, PW0[:]), inc=sV)
            kb.wait("ve", sV)
            kb.op("ve", lambda e, d=xnb: e.bn_stats(lnst[:, 0:6], d[:, 0:512]))
            kb.op("ve", lambda e, d=xnb: e.bn_stats(lnst[:, 6:12], d[:, 512:1024]), inc=sV)
            kb.wait("ve", sV)
            kb.op("ve", lambda e: e.bn_aggr(lnmv[:, 0:2], lnst[:]), inc=sV)
            kb.wait("sc", sV)
            kb.op("sc", lambda e: e.activation(lnmv[:, 2:3], lnmv[:, 1:2], AF.Sqrt,
                                               bias=EPS_LN, scale=1.0), inc=sS)
            kb.wait("ve", sS)
            kb.op("ve", lambda e: e.reciprocal(lnmv[:, 3:4], lnmv[:, 2:3]), inc=sV)
            kb.wait("ve", sV)
            kb.op("ve", lambda e, d=xnb: e.tensor_scalar(
                d, d, lnmv[:, 0:1], lnmv[:, 3:4],
                ALU.subtract, ALU.mult), inc=sV)
            kb.wait("ve", sV)
            kb.op("ve", lambda e, d=xnb: e.tensor_mul(d, d, gOutS[:]), inc=sV)
            kb.wait("ve", sV)
            kb.op("ve", lambda e, t=tc, d=xnb: e.tensor_add(X[:, t, :], X[:, t, :], d), inc=sV)
            xfree[tc % 2] = sV.n
            ofree = sV.n - 6
        bctx.close()

    if "attn_b" in phases:
        emit_attn_b()
        kb.flush_block("attn_b")

    if "ffn2" in phases:
        emit_ffn("f2")
        kb.flush_block("ffn2")

    kb.wait("sy", sV)
    kb.wait("sy", sT)
    kb.wait("sy", sS)
    kb.dma("sy", out.rearrange("(n p) d -> p n d", p=P), X[:], sO)
    kb.wait("gp", sO)
    kb.flush_block("store")

    ctx.close()
    return nc


def _prep_inputs(inputs):
    gi = lambda k: np.ascontiguousarray(np.asarray(inputs[k], np.float32))
    x = gi("L0_attn_a_x")
    xsh = np.ascontiguousarray(x.reshape(8, T, D))
    base = {
        "aWqT": np.ascontiguousarray(gi("L0_attn_a_Wq").T),
        "aWkT": np.ascontiguousarray(gi("L0_attn_a_Wk").T),
        "aWvT": np.ascontiguousarray(gi("L0_attn_a_Wv").T),
        "aWoT": np.ascontiguousarray(gi("L0_attn_a_Wo").T),
        "aBq": np.ascontiguousarray(gi("L0_attn_a_bq").reshape(DC, P).T),
        "aBk": np.ascontiguousarray(gi("L0_attn_a_bk").reshape(DC, P).T),
        "aBv": gi("L0_attn_a_bv"), "aBo": gi("L0_attn_a_bo"),
        "aBkF": gi("L0_attn_a_bk"),
        "bNormG": np.ascontiguousarray(gi("L0_attn_b_norm_g").reshape(DC, P).T),
        "bWq": gi("L0_attn_b_Wq"), "bWkv": gi("L0_attn_b_Wkv"),
        "bWout": gi("L0_attn_b_Wout"),
        "gOutB": np.ascontiguousarray(np.broadcast_to(gi("L0_attn_b_out_norm_g"), (P, D))),
        "gateW9T": np.ascontiguousarray(np.concatenate(
            [gi("L0_ffn_gate_w").T, gi("L0_ffn_shared_gate_w").T], axis=1)),
        "fWg": gi("L0_ffn_w_gate"), "fWu": gi("L0_ffn_w_up"), "fWd": gi("L0_ffn_w_down"),
        "fswG": gi("L0_ffn_sw_gate"), "fswU": gi("L0_ffn_sw_up"), "fswD": gi("L0_ffn_sw_down"),
        "consts": np.ascontiguousarray(np.broadcast_to(
            np.array([EPS_LN, -16.0, 16.0, 1e-6], np.float32), (P, 4))),
        "ident": np.eye(P, dtype=np.float32),
        "tril": np.ascontiguousarray(np.triu(np.ones((P, P), np.float32))),
    }
    in_maps = []
    for c in range(8):
        r = c % 4
        rm = np.zeros((P, 4), np.float32)
        rm[:, :r] = 1.0
        im = dict(base)
        im["xs"] = xsh[c]
        im["rmask_in"] = rm
        in_maps.append(im)
    return in_maps


def kernel(**inputs):
    in_maps = _prep_inputs(inputs)
    nc = build_program()
    res = run_bass_kernel_spmd(nc, in_maps, core_ids=list(range(8)))
    outs = np.stack([np.asarray(res.results[c]["out"], np.float32) for c in range(8)])
    return np.ascontiguousarray(outs.reshape(2, 8192, 1024))


# version 10
# speedup vs baseline: 1.1994x; 1.0017x over previous
import contextlib
import numpy as np

import concourse.bass as bass
import concourse.mybir as mybir
from concourse.bass_utils import run_bass_kernel_spmd

F32 = mybir.dt.float32
BF16 = mybir.dt.bfloat16
AF = mybir.ActivationFunctionType
ALU = mybir.AluOpType
AX = mybir.AxisListType.X

P = 128
T = 2048
NT = 16
D = 1024
DC = 8
NST = 4
STW = 512
FI = 1408
FIC = 11
NE = 8
EPS_LN = 1e-5

ENGS = ("sy", "gp", "te", "ve", "sc")


class Sem:
    __slots__ = ("h", "n", "name")

    def __init__(self, h, name):
        self.h = h
        self.n = 0
        self.name = name


class KB:
    def __init__(self, nc, ctx):
        self.nc = nc
        self.ctx = ctx
        self.q = {e: [] for e in ENGS}

    def sem(self, name):
        return Sem(self.ctx.enter_context(self.nc.semaphore(name)), name)

    def op(self, eng, fn, inc=None, inc_by=1):
        if inc is not None:
            inc.n += inc_by
            self.q[eng].append(lambda e, fn=fn, s=inc, b=inc_by: fn(e).then_inc(s.h, b))
        else:
            self.q[eng].append(lambda e, fn=fn: fn(e))

    def wait(self, eng, sem, val=None):
        v = sem.n if val is None else val
        if v <= 0:
            return
        self.q[eng].append(lambda e, s=sem, v=v: e.wait_ge(s.h, v))

    def dma(self, eng, out, in_, inc):
        self.op(eng, lambda e, o=out, i=in_: e.dma_start(out=o, in_=i), inc=inc, inc_by=16)

    def flush_block(self, name):
        nc = self.nc
        q = self.q
        self.q = {e: [] for e in ENGS}
        with nc.Block() as blk:
            @blk.sync
            def _(eng):
                for f in q["sy"]:
                    f(eng)

            @blk.gpsimd
            def _(eng):
                for f in q["gp"]:
                    f(eng)

            @blk.tensor
            def _(eng):
                for f in q["te"]:
                    f(eng)

            @blk.vector
            def _(eng):
                for f in q["ve"]:
                    f(eng)

            @blk.scalar
            def _(eng):
                for f in q["sc"]:
                    f(eng)


def build_program(phases=("attn_a", "ffn1", "attn_b", "ffn2"), alevel=3):
    nc = bass.Bass("TRN2", target_bir_lowering=False)
    ctx = contextlib.ExitStack()
    E = ctx.enter_context
    kb = KB(nc, ctx)

    def din(name, shape, dt=F32):
        return nc.dram_tensor(name, list(shape), dt, kind="ExternalInput").ap()

    xs = din("xs", (T, D))
    aWqT = din("aWqT", (D, D)); aWkT = din("aWkT", (D, D))
    aWvT = din("aWvT", (D, D)); aWoT = din("aWoT", (D, D))
    aBq = din("aBq", (P, DC)); aBk = din("aBk", (P, DC))
    aBv = din("aBv", (D,)); aBo = din("aBo", (D,)); aBkF = din("aBkF", (D,))
    bNormG = din("bNormG", (P, DC))
    bWq = din("bWq", (D, 512)); bWkv = din("bWkv", (D, 128))
    bWout = din("bWout", (512, D))
    gOutB = din("gOutB", (P, D))
    gateW9T = din("gateW9T", (D, 9))
    fWg = din("fWg", (NE, D, FI)); fWu = din("fWu", (NE, D, FI))
    fWd = din("fWd", (NE, FI, D))
    fswG = din("fswG", (D, FI)); fswU = din("fswU", (D, FI)); fswD = din("fswD", (FI, D))
    consts = din("consts", (P, 4))
    ident = din("ident", (P, P))
    tril = din("tril", (P, P))
    rmaskD = din("rmask_in", (P, 4))

    out = nc.dram_tensor("out", [T, D], F32, kind="ExternalOutput").ap()

    cc_s_in = nc.dram_tensor("cc_s_in", [D, D + 1], F32).ap()
    cc_s_out = nc.dram_tensor("cc_s_out", [4 * D, D + 1], F32).ap()
    cc_k_in = nc.dram_tensor("cc_k_in", [64, T], BF16).ap()
    cc_k_out = nc.dram_tensor("cc_k_out", [256, T], BF16).ap()
    ccvi = nc.dram_tensor("ccvi", [T, 64], BF16).ap()
    ccvo = nc.dram_tensor("ccvo", [4 * T, 64], BF16).ap()
    GROUPS = [[0, 1, 2, 3], [4, 5, 6, 7]]

    X = E(nc.sbuf_tensor("X", [P, NT, D], F32))
    slotA = E(nc.sbuf_tensor("slotA", [P, 11264], BF16))
    slotB = E(nc.sbuf_tensor("slotB", [P, 11264], BF16))
    sconst = E(nc.sbuf_tensor("sconst", [P, 4], F32))
    sid32 = E(nc.sbuf_tensor("sid32", [P, P], F32))
    sidbf = E(nc.sbuf_tensor("sidbf", [P, P], BF16))
    stril = E(nc.sbuf_tensor("stril", [P, P], F32))
    rmask = E(nc.sbuf_tensor("rmask", [P, 4], F32))
    lnst = E(nc.sbuf_tensor("lnst", [P, 12], F32))
    lnmv = E(nc.sbuf_tensor("lnmv", [P, 4], F32))
    xnbuf = E(nc.sbuf_tensor("xnbuf", [P, D], F32))
    zbuf = E(nc.sbuf_tensor("zbuf", [P, STW], F32))
    rbuf = E(nc.sbuf_tensor("rbuf", [P, STW], F32))
    ebuf = E(nc.sbuf_tensor("ebuf", [P, STW], F32))

    onesrow = slotA[0:1, 8192:8320]
    rowBv = slotA[0:1, 8320:9344]
    rowBo = slotA[0:1, 9344:10368]
    Atb = slotB[:, 8192:8320]
    onescol = slotB[:, 8320:8321]
    rowBk = slotB[0:1, 8448:9472]
    OTb = slotB[:, 9472:10496].rearrange("p (c n) -> p c n", c=DC)
    Obuf = xnbuf

    nc.const_aps.register(F32, EPS_LN, sconst[:, 0:1])
    nc.const_aps.register(F32, -16.0, sconst[:, 1:2])
    nc.const_aps.register(F32, 16.0, sconst[:, 2:3])
    nc.const_aps.register(F32, 1e-6, sconst[:, 3:4])

    P0 = E(nc.psum_tensor("P0", [P, 512], F32))
    P1 = E(nc.psum_tensor("P1", [P, 512], F32))
    PT0 = E(nc.psum_tensor("PT0", [P, 512], F32))
    PT1 = E(nc.psum_tensor("PT1", [P, 512], F32))
    PW0 = E(nc.psum_tensor("PW0", [P, 1024], F32))
    PW1 = E(nc.psum_tensor("PW1", [P, 1024], F32))
    TRP = (PT0[:, 0:128], PT1[:, 0:128])
    APS = P0[:, 0:128]
    ZCOLS = (P1[:, 0:1], P1[:, 8:9])
    DENC = P1[:, 4:5]

    sX = kb.sem("sX")
    sW = kb.sem("sW")
    sT = kb.sem("sT")
    sV = kb.sem("sV")
    sS = kb.sem("sS")
    sO = kb.sem("sO")
    sC1 = kb.sem("sC1")
    sC2 = kb.sem("sC2")
    sC3 = kb.sem("sC3")
    sM = kb.sem("sM")
    sLA = kb.sem("sLA")
    sLB = kb.sem("sLB")
    sEX = kb.sem("sEX")
    sAB = kb.sem("sAB")

    def slotv(slot, nchunk=DC, width=D):
        return slot[:, 0:nchunk * width].rearrange("p (c n) -> p c n", c=nchunk)

    def load_weight(slot, w_dram, nchunk=DC, width=D, sem=None):
        if sem is None:
            sem = sLA if slot is slotA else sLB
        kb.wait("gp", sem)
        kb.dma("gp", slotv(slot, nchunk, width),
               w_dram.rearrange("(c p) n -> p c n", p=P), sem)
        return sem.n

    def ln_tile(tt, src=None):
        s = X[:, tt, :] if src is None else src
        kb.wait("ve", sT)
        kb.wait("ve", sV)
        kb.op("ve", lambda e, s=s: e.bn_stats(lnst[:, 0:6], s[:, 0:512]))
        kb.op("ve", lambda e, s=s: e.bn_stats(lnst[:, 6:12], s[:, 512:1024]), inc=sV)
        kb.wait("ve", sV)
        kb.op("ve", lambda e: e.bn_aggr(lnmv[:, 0:2], lnst[:]), inc=sV)
        kb.wait("sc", sV)
        kb.op("sc", lambda e: e.activation(lnmv[:, 2:3], lnmv[:, 1:2], AF.Sqrt,
                                           bias=EPS_LN, scale=1.0), inc=sS)
        kb.wait("ve", sS)
        kb.op("ve", lambda e: e.reciprocal(lnmv[:, 3:4], lnmv[:, 2:3]), inc=sV)
        kb.wait("ve", sV)
        kb.op("ve", lambda e, s=s: e.tensor_scalar(
            xnbuf[:], s, lnmv[:, 0:1], lnmv[:, 3:4], ALU.subtract, ALU.mult), inc=sV)
        return sV.n

    tr_last = {0: 0, 1: 0}

    def transpose128(src_ap, dst_ap, k, rows=128):
        pv = TRP[k % 2][0:rows, :]
        kb.wait("te", sS, tr_last[k % 2])
        kb.op("te", lambda e, s=src_ap, p=pv: e.transpose(p, s, sid32[:]), inc=sT)
        kb.wait("sc", sT)
        kb.op("sc", lambda e, p=pv, d=dst_ap: e.copy(d, p), inc=sS)
        tr_last[k % 2] = sS.n
        return sS.n

    def phi_evac(psum, bias_ap, dst_ap, width=STW):
        kb.wait("ve", sT)
        kb.wait("ve", sS)
        kb.op("ve", lambda e, p=psum, b=bias_ap: e.tensor_scalar(
            zbuf[:, 0:width], p, b, None, ALU.add), inc=sV)
        kb.wait("sc", sV)
        kb.op("sc", lambda e: e.activation(rbuf[:, 0:width], zbuf[:, 0:width], AF.Relu), inc=sS)
        kb.wait("ve", sS)
        kb.op("ve", lambda e: e.tensor_sub(zbuf[:, 0:width], zbuf[:, 0:width],
                                           rbuf[:, 0:width]), inc=sV)
        kb.wait("sc", sV)
        kb.op("sc", lambda e: e.activation(ebuf[:, 0:width], zbuf[:, 0:width], AF.Exp), inc=sS)
        kb.wait("ve", sS)
        kb.op("ve", lambda e, d=dst_ap: e.tensor_add(d, ebuf[:, 0:width],
                                                     rbuf[:, 0:width]), inc=sV)
        return sV.n

    kb.dma("sy", sconst[:], consts, sX)
    kb.dma("sy", sid32[:], ident, sX)
    kb.dma("sy", stril[:], tril, sX)
    kb.dma("sy", rmask[:], rmaskD, sX)
    kb.dma("sy", X[:], xs.rearrange("(n p) d -> p n d", p=P), sX)
    kb.dma("gp", sidbf[:], ident, sW)
    kb.op("ve", lambda e: e.memset(onesrow, 1.0))
    kb.op("ve", lambda e: e.memset(onescol, 1.0), inc=sV)
    wK = load_weight(slotA, aWkT)
    wV = load_weight(slotB, aWvT)
    kb.flush_block("init")

    if "attn_a" in phases:
        actx = contextlib.ExitStack()
        Ea = actx.enter_context
        arena = Ea(nc.sbuf_tensor("aArena", [P, 20480], BF16))
        xnT = arena[:, 0:4096].rearrange("p (c n) -> p c n", c=DC)
        QpT = arena[:, 4096:8192].rearrange("p (c n) -> p c n", c=DC)
        KpT = arena[:, 8192:12288].rearrange("p (c n) -> p c n", c=DC)
        Vst = arena[:, 12288:16384].rearrange("p (c n) -> p c n", c=4)
        Kptok = arena[:, 16384:20480].rearrange("p (c n) -> p c n", c=4)
        SgF = arena[:].bitcast(F32)
        Sg = SgF[:, 0:8200].rearrange("p (c n) -> p c n", c=DC)
        S = Ea(nc.sbuf_tensor("a_S", [P, DC, 1025], F32))
        Sbf = Ea(nc.sbuf_tensor("a_Sbf", [P, DC, 1025], BF16))
        denb = Ea(nc.sbuf_tensor("a_den", [P, 2], F32))
        aBqc = Ea(nc.sbuf_tensor("a_Bq", [P, DC], F32))
        aBkc = Ea(nc.sbuf_tensor("a_Bk", [P, DC], F32))

        kb.dma("sy", aBqc[:], aBq, sX)
        kb.dma("sy", aBkc[:], aBk, sX)
        kb.dma("gp", rowBv, aBv.rearrange("(a d) -> a d", a=1), sW)
        kb.dma("gp", rowBo, aBo.rearrange("(a d) -> a d", a=1), sW)
        kb.dma("gp", rowBk, aBkF.rearrange("(a d) -> a d", a=1), sW)
        kb.op("ve", lambda e: e.memset(S[:], 0.0), inc=sV)
        ldx = (sX.n, sW.n)

        def st_ln_transpose(st):
            import os as _os
            trlim = int(_os.environ.get("TRLIM", str(DC)))
            for i in range(4):
                vln = ln_tile(st * 4 + i)
                kb.wait("te", sV, vln)
                for jc in range(trlim):
                    transpose128(xnbuf[:, jc * P:(jc + 1) * P],
                                 xnT[:, jc, i * P:(i + 1) * P], jc)
            return sS.n

        pz_last = {0: 0, 1: 0, 2: 0, 3: 0}

        def proj_feature(slot, bias_c, dstT, vready):
            sl = slotv(slot)
            pcyc = (P0[:], P1[:], PW1[:, 0:512], PW1[:, 512:1024])
            for oc in range(DC):
                ps = pcyc[oc % 4]
                kb.wait("te", sS, vready)
                kb.wait("te", sV, pz_last[oc % 4])
                for jc in range(DC):
                    kb.op("te", lambda e, o=oc, j=jc, p=ps, s=sl: e.matmul(
                        p, s[:, j, o * P:(o + 1) * P], xnT[:, j, :],
                        start=(j == 0), stop=(j == DC - 1)),
                        inc=sT if jc == DC - 1 else None)
                phi_evac(ps, bias_c[:, oc:oc + 1], dstT[:, oc, :])
                pz_last[oc % 4] = sV.n - 2
            return sV.n

        def phi_psum(psum_half, dst_ap, width=STW):
            kb.wait("sc", sT)
            kb.wait("sc", sV)
            kb.op("sc", lambda e, p=psum_half: e.activation(rbuf[:, 0:width], p, AF.Relu), inc=sS)
            kb.wait("ve", sS)
            kb.op("ve", lambda e, p=psum_half: e.tensor_sub(zbuf[:, 0:width], p, rbuf[:, 0:width]), inc=sV)
            kb.wait("sc", sV)
            kb.op("sc", lambda e: e.activation(ebuf[:, 0:width], zbuf[:, 0:width], AF.Exp), inc=sS)
            kb.wait("ve", sS)
            kb.op("ve", lambda e, d=dst_ap: e.tensor_add(d, ebuf[:, 0:width], rbuf[:, 0:width]), inc=sV)
            return sV.n

        def proj_Ktok(vready):
            sl = slotv(slotA)
            for i in range(4):
                kb.wait("te", sS, vready)
                kb.wait("te", sV)
                for jc in range(DC):
                    kb.op("te", lambda e, i=i, j=jc, s=sl: e.matmul(
                        PW0[:, 0:512], xnT[:, j, i * P:(i + 1) * P], s[:, j, 0:512],
                        start=(j == 0), stop=False))
                    kb.op("te", lambda e, i=i, j=jc, s=sl: e.matmul(
                        PW0[:, 512:1024], xnT[:, j, i * P:(i + 1) * P], s[:, j, 512:1024],
                        start=(j == 0), stop=False))
                kb.op("te", lambda e: e.matmul(PW0[:, 0:512], onesrow, rowBk[:, 0:512],
                                               start=False, stop=True))
                kb.op("te", lambda e: e.matmul(PW0[:, 512:1024], onesrow, rowBk[:, 512:1024],
                                               start=False, stop=True), inc=sT)
                phi_psum(PW0[:, 0:512], Kptok[:, i, 0:512])
                phi_psum(PW0[:, 512:1024], Kptok[:, i, 512:1024])

        def proj_V(vready):
            sl = slotv(slotB)
            for i in range(4):
                kb.wait("te", sS, vready)
                kb.wait("te", sV)
                for jc in range(DC):
                    kb.op("te", lambda e, i=i, j=jc, s=sl: e.matmul(
                        PW0[:, 0:512], xnT[:, j, i * P:(i + 1) * P], s[:, j, 0:512],
                        start=(j == 0), stop=False))
                    kb.op("te", lambda e, i=i, j=jc, s=sl: e.matmul(
                        PW0[:, 512:1024], xnT[:, j, i * P:(i + 1) * P], s[:, j, 512:1024],
                        start=(j == 0), stop=False))
                kb.op("te", lambda e: e.matmul(PW0[:, 0:512], onesrow, rowBv[:, 0:512],
                                               start=False, stop=True))
                kb.op("te", lambda e: e.matmul(PW0[:, 512:1024], onesrow, rowBv[:, 512:1024],
                                               start=False, stop=True), inc=sT)
                kb.wait("ve", sT)
                kb.op("ve", lambda e, i=i: e.tensor_copy(Vst[:, i, :], PW0[:]), inc=sV)
            return sV.n

        kvfree = {0: 0, 1: 0}

        def chunk_KV(i):
            if state.get("vsbf_sc"):
                kb.wait("ve", sS, state["vsbf"])
            for dc in range(DC):
                pw = PW1 if dc % 2 == 0 else PW0
                if dc < 2:
                    kb.wait("te", sV)
                else:
                    kb.wait("te", sV, kvfree[dc % 2])
                kb.op("te", lambda e, d=dc, i=i, p=pw: e.matmul(
                    p[:, 0:512], Kptok[:, i, d * P:(d + 1) * P], Vst[:, i, 0:512],
                    start=True, stop=True))
                kb.op("te", lambda e, d=dc, i=i, p=pw: e.matmul(
                    p[:, 512:1024], Kptok[:, i, d * P:(d + 1) * P], Vst[:, i, 512:1024],
                    start=True, stop=True))
                zc = ZCOLS[dc % 2]
                kb.op("te", lambda e, d=dc, i=i, z=zc: e.matmul(
                    z, Kptok[:, i, d * P:(d + 1) * P], onescol, start=True, stop=True), inc=sT)
                kb.wait("ve", sT)
                kb.op("ve", lambda e, d=dc, p=pw: e.tensor_add(S[:, d, 0:1024], S[:, d, 0:1024], p[:]))
                kb.op("ve", lambda e, d=dc, z=zc: e.tensor_add(S[:, d, 1024:1025], S[:, d, 1024:1025],
                                                               z), inc=sV)
                kvfree[dc % 2] = sV.n

        state = {"vsbf": 0}

        def mixing_chunk(st, i, s2=9):
            tt = st * 4 + i
            if s2 < 1:
                return
            kb.wait("te", sV)
            for dc in range(DC):
                kb.op("te", lambda e, d=dc, i=i: e.matmul(
                    APS, KpT[:, d, i * P:(i + 1) * P], QpT[:, d, i * P:(i + 1) * P],
                    start=(d == 0), stop=(d == DC - 1)), inc=sT if dc == DC - 1 else None)
            kb.wait("ve", sT)
            kb.op("ve", lambda e: e.tensor_mul(Atb, APS, stril[:]), inc=sV)
            vatb = sV.n
            if s2 < 2:
                return
            chunk_KV(i)
            if s2 < 3:
                return
            if state.get("vsbf_sc"):
                kb.wait("te", sS, state["vsbf"])
            else:
                kb.wait("te", sV, state["vsbf"])
            kb.wait("te", sV)
            for dc in range(DC):
                kb.op("te", lambda e, d=dc, i=i: e.matmul(
                    PW0[:, 0:512], QpT[:, d, i * P:(i + 1) * P], Sbf[:, d, 0:512],
                    start=(d == 0), stop=False))
                kb.op("te", lambda e, d=dc, i=i: e.matmul(
                    PW0[:, 512:1024], QpT[:, d, i * P:(i + 1) * P], Sbf[:, d, 512:1024],
                    start=(d == 0), stop=False))
                kb.op("te", lambda e, d=dc, i=i: e.matmul(
                    DENC, QpT[:, d, i * P:(i + 1) * P], Sbf[:, d, 1024:1025],
                    start=(d == 0), stop=False))
            kb.op("te", lambda e, i=i: e.matmul(PW0[:, 0:512], Atb, Vst[:, i, 0:512],
                                                start=False, stop=True))
            kb.op("te", lambda e, i=i: e.matmul(PW0[:, 512:1024], Atb, Vst[:, i, 512:1024],
                                                start=False, stop=True))
            kb.op("te", lambda e: e.matmul(DENC, Atb, onescol, start=False, stop=True), inc=sT)
            vnum = sT.n
            kb.wait("ve", sT, vnum)
            kb.op("ve", lambda e: e.tensor_scalar(denb[:, 0:1], DENC, 1e-6, None, ALU.add), inc=sV)
            kb.wait("ve", sV)
            kb.op("ve", lambda e: e.reciprocal(denb[:, 1:2], denb[:, 0:1]), inc=sV)
            kb.wait("ve", sV)
            kb.op("ve", lambda e: e.tensor_scalar(Obuf[:], PW0[:], denb[:, 1:2], None, ALU.mult), inc=sV)
            vO = sV.n
            if s2 < 4:
                return
            kb.wait("sc", sT, vnum)
            kb.wait("sc", sV)
            kb.op("sc", lambda e: e.copy(Sbf[:], S[:]), inc=sS)
            state["vsbf"] = sS.n
            state["vsbf_sc"] = True
            kb.wait("te", sV, vO)
            for ec in range(DC):
                transpose128(Obuf[:, ec * P:(ec + 1) * P], OTb[:, ec, :], ec)
            vOT = sS.n
            sl = slotv(slotB)
            kb.wait("te", sS, vOT)
            kb.wait("te", sV)
            for ec in range(DC):
                kb.op("te", lambda e, c=ec, s=sl: e.matmul(
                    PW1[:, 0:512], OTb[:, c, :], s[:, c, 0:512], start=(c == 0), stop=False))
                kb.op("te", lambda e, c=ec, s=sl: e.matmul(
                    PW1[:, 512:1024], OTb[:, c, :], s[:, c, 512:1024], start=(c == 0), stop=False))
            kb.op("te", lambda e: e.matmul(PW1[:, 0:512], onesrow, rowBo[:, 0:512],
                                           start=False, stop=True))
            kb.op("te", lambda e: e.matmul(PW1[:, 512:1024], onesrow, rowBo[:, 512:1024],
                                           start=False, stop=True), inc=sT)
            kb.wait("ve", sT)
            kb.op("ve", lambda e, t=tt: e.tensor_add(X[:, t, :], X[:, t, :], PW1[:]), inc=sV)

        import os
        sub = int(os.environ.get("SUBLEVEL", "9"))
        nst_lim = int(os.environ.get("NST_LIM", str(NST)))
        kb.wait("te", sX, ldx[0])
        kb.wait("te", sW, ldx[1])
        kb.wait("te", sLA, wK)
        kb.wait("te", sLB, wV)
        kb.wait("ve", sX, ldx[0])
        for st in range(nst_lim):
            vtr = st_ln_transpose(st)
            if sub == 4:
                proj_feature(slotA, aBkc, KpT, vtr)
            if sub >= 1 and sub != 4:
                proj_Ktok(vtr)
            if sub >= 2:
                proj_V(vtr)
            if sub >= 3:
                for i in range(4):
                    chunk_KV(i)

        if alevel < 2:
            actx.close()
            kb.flush_block("attn_a")
    if "attn_a" in phases and alevel >= 2:
        kb.wait("gp", sV)
        kb.dma("gp", cc_s_in.rearrange("(c p) e -> p c e", p=P), S[:], sEX)
        kb.wait("gp", sEX)
        kb.op("gp", lambda e: e.collective_compute(
            "AllGather", ALU.bypass, replica_groups=GROUPS,
            ins=[cc_s_in], outs=[cc_s_out]), inc=sC1)
        kb.wait("gp", sC1, 1)
        for r in range(4):
            kb.wait("gp", sV)
            kb.dma("gp", Sg, cc_s_out[r * D:(r + 1) * D, :].rearrange("(c p) e -> p c e", p=P), sEX)
            kb.wait("ve", sEX)
            if r == 0:
                kb.op("ve", lambda e: e.tensor_scalar(
                    S[:], Sg, rmask[:, 0:1], None, ALU.mult), inc=sV)
            else:
                kb.op("ve", lambda e, r=r: e.scalar_tensor_tensor(
                    S[:], Sg, rmask[:, r:r + 1], S[:], ALU.mult, ALU.add), inc=sV)
        kb.wait("ve", sV)
        kb.op("ve", lambda e: e.tensor_copy(Sbf[:], S[:]), inc=sV)
        state["vsbf"] = sV.n
        vexch = sV.n

        kb.wait("gp", sT)
        wQ = load_weight(slotA, aWqT)

        if alevel < 3:
            actx.close()
            kb.flush_block("attn_a")

    if "attn_a" in phases and alevel >= 3:
        import os
        nst_lim2 = int(os.environ.get("NST_LIM2", str(NST)))
        wVcur = wV
        s3 = int(os.environ.get("SUB3", "9"))
        for st in range(nst_lim2):
            if st == 0:
                kb.wait("ve", sV, vexch)
                kb.wait("sc", sV, vexch)
            if s3 >= -1:
                if s3 == -1:
                    for _i4 in range(4):
                        ln_tile(st * 4 + _i4)
                else:
                    vtr = st_ln_transpose(st)
            if s3 >= 1:
                kb.wait("te", sLA, wQ)
                proj_feature(slotA, aBqc, QpT, vtr)
            if s3 >= 2:
                kb.wait("gp", sT)
                wKn = load_weight(slotA, aWkT)
                kb.wait("te", sLA, wKn)
                proj_feature(slotA, aBkc, KpT, vtr)
            if s3 >= 3:
                proj_Ktok(vtr)
            if s3 >= 4:
                kb.wait("te", sLB, wVcur)
                proj_V(vtr)
            if s3 >= 5:
                kb.wait("gp", sT)
                wO = load_weight(slotB, aWoT)
                kb.wait("te", sLB, wO)
            if s3 >= 6:
                for i in range(4):
                    mixing_chunk(st, i, s2=int(os.environ.get("SUB2", "9")))
            if s3 >= 5 and st < NST - 1:
                kb.wait("gp", sT)
                wVcur = load_weight(slotB, aWvT)
                wQ = load_weight(slotA, aWqT)

        actx.close()
        kb.flush_block("attn_a")

    def emit_ffn(tag):
        fctx = contextlib.ExitStack()
        Ef = fctx.enter_context
        xnTf = Ef(nc.sbuf_tensor(tag + "_xnT", [P, DC, T], BF16))
        midT = Ef(nc.sbuf_tensor(tag + "_midT", [P, FIC, T], BF16))
        cmb = Ef(nc.sbuf_tensor(tag + "_cmb", [P, NT, 9], F32))
        rt = Ef(nc.sbuf_tensor(tag + "_rt", [P, 24], F32))
        gw9 = Ef(nc.sbuf_tensor(tag + "_gw9", [P, DC, 9], BF16))

        kb.dma("gp", gw9[:], gateW9T.rearrange("(c p) n -> p c n", p=P), sW)
        wg9 = sW.n
        kb.wait("ve", sX)

        wlist = []
        for e in range(NE):
            wlist += [(fWg[e], DC, FI), (fWu[e], DC, FI), (fWd[e], FIC, D)]
        wlist += [(fswG, DC, FI), (fswU, DC, FI), (fswD, FIC, D)]
        slots = [slotA, slotB]
        ssem = [sLA, sLB]
        wcnt = [0] * len(wlist)

        def load_wi(i):
            if i >= len(wlist):
                return
            dram, nch, wid = wlist[i]
            kb.wait("gp", sT)
            wcnt[i] = load_weight(slots[i % 2], dram, nch, wid)

        load_wi(0)
        load_wi(1)

        for tt in range(NT):
            vln = ln_tile(tt)
            kb.wait("te", sV, vln)
            for jc in range(DC):
                transpose128(xnbuf[:, jc * P:(jc + 1) * P],
                             xnTf[:, jc, tt * P:(tt + 1) * P], jc)
            vtr = sS.n
            kb.wait("te", sS, vtr)
            kb.wait("te", sV)
            kb.wait("te", sW, wg9)
            for jc in range(DC):
                kb.op("te", lambda e, j=jc, t=tt: e.matmul(
                    P0[:, 0:9], xnTf[:, j, t * P:(t + 1) * P], gw9[:, j, :],
                    start=(j == 0), stop=(j == DC - 1)), inc=sT if jc == DC - 1 else None)
            kb.wait("ve", sT)
            kb.wait("ve", sS)
            kb.op("ve", lambda e: e.tensor_copy(rt[:, 0:9], P0[:, 0:9]), inc=sV)
            kb.wait("ve", sV)
            kb.op("ve", lambda e: e.reduce_max(rt[:, 9:10], rt[:, 0:8], axis=AX,
                                               negate=True), inc=sV)
            kb.wait("sc", sV)
            kb.op("sc", lambda e: e.activation(rt[:, 16:24], rt[:, 0:8], AF.Exp,
                                               bias=rt[:, 9:10], scale=1.0), inc=sS)
            kb.op("sc", lambda e, t=tt: e.activation(cmb[:, t, 8:9], rt[:, 8:9],
                                                     AF.Sigmoid), inc=sS)
            kb.wait("ve", sS)
            kb.op("ve", lambda e: e.reduce_max(rt[:, 13:14], rt[:, 16:24], axis=AX), inc=sV)
            kb.wait("ve", sV)
            kb.op("ve", lambda e, t=tt: e.tensor_scalar(
                cmb[:, t, 0:8], rt[:, 16:24], rt[:, 13:14], 1e30, ALU.is_ge, ALU.mult), inc=sV)
            kb.wait("ve", sV)
            kb.op("ve", lambda e, t=tt: e.tensor_sub(cmb[:, t, 0:8], rt[:, 16:24],
                                                     cmb[:, t, 0:8]), inc=sV)
            kb.wait("ve", sV)
            kb.op("ve", lambda e, t=tt: e.reduce_max(rt[:, 10:11], cmb[:, t, 0:8],
                                                     axis=AX), inc=sV)
            kb.wait("ve", sV)
            kb.op("ve", lambda e: e.tensor_tensor(rt[:, 11:12], rt[:, 13:14],
                                                  rt[:, 10:11], ALU.add), inc=sV)
            kb.wait("ve", sV)
            kb.op("ve", lambda e: e.reciprocal(rt[:, 12:13], rt[:, 11:12]), inc=sV)
            kb.wait("ve", sV)
            kb.op("ve", lambda e, t=tt: e.scalar_tensor_tensor(
                cmb[:, t, 0:8], rt[:, 16:24], rt[:, 10:11], rt[:, 16:24],
                ALU.is_ge, ALU.mult), inc=sV)
            kb.wait("ve", sV)
            kb.op("ve", lambda e, t=tt: e.tensor_scalar(
                cmb[:, t, 0:8], cmb[:, t, 0:8], rt[:, 12:13], None, ALU.mult), inc=sV)
        vxnT = sS.n

        gbank = (P0, PT0)
        ubank = (P1, PT1)
        gfree = {0: 0, 1: 0}
        ufree = {0: 0, 1: 0}
        dfree = {0: 0, 1: 0}
        vdown_prev = 0

        kb.wait("te", sV)
        for ei in range(NE + 1):
            i_g, i_u, i_d = 3 * ei, 3 * ei + 1, 3 * ei + 2
            slg = slotv(slots[i_g % 2], DC, FI)
            slu = slotv(slots[i_u % 2], DC, FI)
            kb.wait("te", ssem[i_g % 2], wcnt[i_g])
            kb.wait("te", ssem[i_u % 2], wcnt[i_u])
            kb.wait("te", sS, vxnT)
            kb.wait("ve", sT, vdown_prev)
            kb.wait("te", sV)
            gps2 = ((P0[:], PT0[:]), (PW0[:, 0:512], PW0[:, 512:1024]))
            ups2 = ((P1[:], PT1[:]), (PW1[:, 0:512], PW1[:, 512:1024]))
            sil2 = ((zbuf[:, 0:512], rbuf[:, 0:512]), (ebuf[:, 0:512], xnbuf[:, 0:512]))
            gq2free = [0, 0]
            uq2free = [0, 0]
            it = 0
            for ic in range(FIC):
                for tqp in range(2):
                    par = it % 2
                    it += 1
                    tq0, tq1 = 2 * tqp, 2 * tqp + 1
                    kb.wait("te", sS, gq2free[par])
                    kb.wait("te", sV, uq2free[par])
                    for jc in range(DC):
                        for k, tq in enumerate((tq0, tq1)):
                            kb.op("te", lambda e, j=jc, c=ic, q=tq, s=slg, p=gps2[par][k]: e.matmul(
                                p, s[:, j, c * P:(c + 1) * P],
                                xnTf[:, j, q * 512:(q + 1) * 512],
                                start=(j == 0), stop=(j == DC - 1)))
                        for k, tq in enumerate((tq0, tq1)):
                            kb.op("te", lambda e, j=jc, c=ic, q=tq, s=slu, p=ups2[par][k]: e.matmul(
                                p, s[:, j, c * P:(c + 1) * P],
                                xnTf[:, j, q * 512:(q + 1) * 512],
                                start=(j == 0), stop=(j == DC - 1)),
                                inc=sT if (jc == DC - 1 and k == 1) else None)
                    kb.wait("sc", sT)
                    kb.wait("sc", sV, uq2free[par])
                    for k in range(2):
                        kb.op("sc", lambda e, k=k, par=par: e.activation(
                            sil2[par][k], gps2[par][k], AF.Silu), inc=sS)
                    gq2free[par] = sS.n
                    kb.wait("ve", sS)
                    for k, tq in enumerate((tq0, tq1)):
                        kb.op("ve", lambda e, c=ic, q=tq, k=k, par=par: e.tensor_tensor(
                            midT[:, c, q * 512:(q + 1) * 512], sil2[par][k], ups2[par][k],
                            ALU.mult), inc=sV)
                    uq2free[par] = sV.n
            vmid = sV.n
            load_wi(i_d)
            load_wi(i_g + 3)
            sld = slotv(slots[i_d % 2], FIC, D)
            kb.wait("te", ssem[i_d % 2], wcnt[i_d])
            kb.wait("ve", sT)
            kb.wait("te", sV, vmid)
            kb.wait("te", sS)
            for tc in range(NT):
                b = tc % 2
                pw = (PW0, PW1)[b]
                kb.wait("te", sV, dfree[b])
                for ic in range(FIC):
                    kb.op("te", lambda e, c=ic, t=tc, s=sld, p=pw: e.matmul(
                        p[:, 0:512], midT[:, c, t * P:(t + 1) * P], s[:, c, 0:512],
                        start=(c == 0), stop=(c == FIC - 1)))
                    kb.op("te", lambda e, c=ic, t=tc, s=sld, p=pw: e.matmul(
                        p[:, 512:1024], midT[:, c, t * P:(t + 1) * P], s[:, c, 512:1024],
                        start=(c == 0), stop=(c == FIC - 1)),
                        inc=sT if ic == FIC - 1 else None)
                kb.wait("ve", sT)
                kb.op("ve", lambda e, t=tc, ei=ei, p=pw: e.scalar_tensor_tensor(
                    X[:, t, :], p[:], cmb[:, t, ei:ei + 1], X[:, t, :],
                    ALU.mult, ALU.add), inc=sV)
                dfree[b] = sV.n
            vdown_prev = sT.n
            load_wi(i_u + 3)
        fctx.close()

    if "ffn1" in phases:
        emit_ffn("f1")
        kb.flush_block("ffn1")

    def emit_attn_b():
        import os as _os
        b3 = int(_os.environ.get("B3LEVEL", "9"))
        bctx = contextlib.ExitStack()
        Eb = bctx.enter_context
        qhT = Eb(nc.sbuf_tensor("b_qhT", [P, 4, T], BF16))
        ob2 = Eb(nc.sbuf_tensor("b_ob2", [P, D], F32))
        pta = (slotA[:, 1024:1536], slotA[:, 1536:2048])
        ptb2 = (slotA[:, 2048:2560], slotA[:, 2560:3072])
        onesb = slotA[0:1, 3072:3136]
        dre = (slotA[0:1, 3136:3648], slotA[0:1, 3648:4160])
        abtmp = slotA[:, 4224:4736]
        bq_s = Eb(nc.sbuf_tensor("b_qs", [P, 32], F32))

        sub1 = contextlib.ExitStack()
        xnnT = sub1.enter_context(nc.sbuf_tensor("b_xnnT", [P, DC, T], BF16))
        bWqS = sub1.enter_context(nc.sbuf_tensor("b_Wq", [P, DC, 512], BF16))
        bWkvS = sub1.enter_context(nc.sbuf_tensor("b_Wkv", [P, DC, P], BF16))
        bGc = sub1.enter_context(nc.sbuf_tensor("b_gc", [P, DC], F32))
        kTl = sub1.enter_context(nc.sbuf_tensor("b_kTl", [64, T], BF16))
        vloc = sub1.enter_context(nc.sbuf_tensor("b_vloc", [P, NT, 64], BF16))

        for _e in ("ve", "sc", "te", "sy"):
            kb.wait(_e, sLA)
            kb.wait(_e, sLB)
        kb.op("ve", lambda e: e.memset(onesb, 1.0), inc=sV)
        kb.wait("sy", sM)
        kb.dma("sy", bGc[:], bNormG, sM)
        kb.dma("gp", bWqS[:], bWq.rearrange("(c p) n -> p c n", p=P), sW)
        kb.dma("gp", bWkvS[:], bWkv.rearrange("(c p) n -> p c n", p=P), sW)
        wqv = sW.n
        vgc = sM.n
        kb.wait("ve", sW, wqv)
        kb.wait("ve", sM, vgc)
        for jc in range(DC):
            kb.op("ve", lambda e, j=jc: e.tensor_scalar(
                bWqS[:, j, :], bWqS[:, j, :], bGc[:, j:j + 1], None, ALU.mult), inc=sV)
            kb.op("ve", lambda e, j=jc: e.tensor_scalar(
                bWkvS[:, j, :], bWkvS[:, j, :], bGc[:, j:j + 1], None, ALU.mult), inc=sV)
        vfold = sV.n
        kb.wait("ve", sX)

        for tt in range(NT):
            ln_tile(tt)
            ln_tile(tt, src=xnbuf[:])
            vln = sV.n
            kb.wait("te", sV, vln)
            for jc in range(DC):
                transpose128(xnbuf[:, jc * P:(jc + 1) * P],
                             xnnT[:, jc, tt * P:(tt + 1) * P], jc)
            vtr = sS.n
            kb.wait("te", sS, vtr)
            kb.wait("te", sV)
            kb.wait("te", sW, wqv)
            kb.wait("te", sV, vfold)
            for jc in range(DC):
                kb.op("te", lambda e, j=jc, t=tt: e.matmul(
                    PW0[:, 0:512], xnnT[:, j, t * P:(t + 1) * P], bWqS[:, j, :],
                    start=(j == 0), stop=(j == DC - 1)))
            for jc in range(DC):
                kb.op("te", lambda e, j=jc, t=tt: e.matmul(
                    P1[:, 0:128], xnnT[:, j, t * P:(t + 1) * P], bWkvS[:, j, :],
                    start=(j == 0), stop=(j == DC - 1)), inc=sT if jc == DC - 1 else None)
            vqkv = sT.n
            kb.wait("ve", sT, vqkv)
            kb.wait("ve", sS)
            kb.op("ve", lambda e: e.tensor_copy(zbuf[:], PW0[:, 0:512]), inc=sV)
            kb.wait("sc", sV)
            kb.op("sc", lambda e: e.activation(rbuf[:], zbuf[:], AF.Square), inc=sS)
            kb.wait("ve", sS)
            kb.op("ve", lambda e: e.reduce_sum(
                bq_s[:, 0:8], rbuf[:].rearrange("p (h d) -> p h d", h=8), axis=AX), inc=sV)
            kb.wait("sc", sV)
            kb.op("sc", lambda e: e.activation(bq_s[:, 8:16], bq_s[:, 0:8], AF.Sqrt), inc=sS)
            kb.wait("ve", sS)
            kb.op("ve", lambda e: e.tensor_scalar(bq_s[:, 8:16], bq_s[:, 8:16], 1e-12,
                                                  None, ALU.max), inc=sV)
            kb.wait("ve", sV)
            kb.op("ve", lambda e: e.reciprocal(bq_s[:, 16:24], bq_s[:, 8:16]), inc=sV)
            kb.wait("ve", sV)
            for h in range(8):
                kb.op("ve", lambda e, h=h: e.tensor_scalar(
                    rbuf[:, h * 64:(h + 1) * 64], zbuf[:, h * 64:(h + 1) * 64],
                    bq_s[:, 16 + h:17 + h], None, ALU.mult), inc=sV)
            vqh = sV.n
            kb.wait("te", sV, vqh)
            for c4 in range(4):
                transpose128(rbuf[:, c4 * P:(c4 + 1) * P], qhT[:, c4, tt * P:(tt + 1) * P],
                             c4)
            kb.wait("ve", sT, vqkv)
            kb.op("ve", lambda e: e.tensor_copy(ebuf[:, 0:128], P1[:, 0:128]), inc=sV)
            kb.wait("sc", sV)
            kb.op("sc", lambda e: e.activation(ebuf[:, 128:192], ebuf[:, 0:64], AF.Square), inc=sS)
            kb.wait("ve", sS)
            kb.op("ve", lambda e: e.reduce_sum(bq_s[:, 24:25], ebuf[:, 128:192], axis=AX), inc=sV)
            kb.wait("sc", sV)
            kb.op("sc", lambda e: e.activation(bq_s[:, 25:26], bq_s[:, 24:25], AF.Sqrt), inc=sS)
            kb.wait("ve", sS)
            kb.op("ve", lambda e: e.tensor_scalar(bq_s[:, 25:26], bq_s[:, 25:26], 1e-12,
                                                  None, ALU.max), inc=sV)
            kb.wait("ve", sV)
            kb.op("ve", lambda e: e.reciprocal(bq_s[:, 26:27], bq_s[:, 25:26]), inc=sV)
            kb.wait("ve", sV)
            kb.op("ve", lambda e: e.tensor_scalar(ebuf[:, 192:256], ebuf[:, 0:64],
                                                  bq_s[:, 26:27], None, ALU.mult), inc=sV)
            kb.op("ve", lambda e, t=tt: e.tensor_copy(vloc[:, t, :], ebuf[:, 64:128]), inc=sV)
            vkh = sV.n
            kb.wait("te", sV, vkh)
            transpose128(ebuf[:, 192:256], kTl[:, tt * P:(tt + 1) * P], tt, rows=64)
        vprojs = sS.n

        if b3 < 2:
            sub1.close()
            bctx.close()
            return
        kb.wait("gp", sS, vprojs)
        kb.wait("gp", sV)
        kb.dma("gp", cc_k_in, kTl[:], sEX)
        kb.dma("gp", ccvi.rearrange("(n p) d -> p n d", p=P), vloc[:], sEX)
        kb.wait("gp", sEX)
        kb.op("gp", lambda e: e.collective_compute(
            "AllGather", ALU.bypass, replica_groups=GROUPS,
            ins=[cc_k_in], outs=[cc_k_out]), inc=sC2)
        kb.op("gp", lambda e: e.collective_compute(
            "AllGather", ALU.bypass, replica_groups=GROUPS,
            ins=[ccvi], outs=[ccvo]), inc=sC3)
        kb.wait("gp", sC2, 1)
        kb.wait("gp", sC3, 1)
        kb.wait("gp", sT)
        sub1.close()
        kTg = Eb(nc.sbuf_tensor("b_kTg", [P, 4, T], BF16))
        vg = Eb(nc.sbuf_tensor("b_vg", [P, 64, 128], BF16))
        attT = Eb(nc.sbuf_tensor("b_attT", [P, 4, T], BF16))
        bWoS = Eb(nc.sbuf_tensor("b_Wo", [P, 4, D], BF16))
        gOutS = Eb(nc.sbuf_tensor("b_go", [P, D], F32))
        kb.dma("gp", kTg[0:64, :, :], cc_k_out.rearrange("(r d) t -> d r t", d=64), sEX)
        kb.dma("gp", kTg[64:128, :, :], cc_k_out.rearrange("(r d) t -> d r t", d=64), sEX)
        kb.dma("gp", vg[:, :, 0:64], ccvo.rearrange("(n p) d -> p n d", p=P), sEX)
        kb.dma("gp", bWoS[:], bWout.rearrange("(c p) n -> p c n", p=P), sW)
        kb.wait("sy", sM)
        kb.dma("sy", gOutS[:], gOutB, sM)
        vld2 = (sEX.n, sW.n, sM.n)
        kb.wait("ve", sT)
        kb.op("ve", lambda e: e.memset(vg[:, :, 64:128], 0.0), inc=sV)
        kb.wait("ve", sV)
        kb.op("ve", lambda e: e.memset(vg[:, :, 64:65], 1.0), inc=sV)
        vvg = sV.n

        if b3 < 3:
            bctx.close()
            return
        sim3 = ((P0[:], P1[:]), (PT0[:], PT1[:]), (PW1[:, 0:512], PW1[:, 512:1024]))
        kb.wait("te", sEX, vld2[0])
        kb.wait("te", sW, vld2[1])
        kb.wait("te", sV, vvg)
        vout_ev = 0
        for hp in range(4):
            for qt in range(4):
                qsl = slice(qt * 512, (qt + 1) * 512)
                simdone = {}
                expdone = {}
                pvdone = {}

                def rec_sim(j, hp=hp, qsl=qsl):
                    pa, pb = sim3[j % 3]
                    r, tl = j // 16, (j % 16) * 128
                    kb.op("te", lambda e, p=pa, r=r, tl=tl: e.matmul(
                        p, kTg[0:64, r, tl:tl + 128], qhT[0:64, hp, qsl],
                        start=True, stop=True))
                    kb.op("te", lambda e, p=pb, r=r, tl=tl: e.matmul(
                        p, kTg[64:128, r, tl:tl + 128], qhT[64:128, hp, qsl],
                        start=True, stop=True), inc=sT)
                    simdone[j] = sT.n

                def rec_exp(j):
                    kb.wait("sc", sT, simdone[j])
                    if j >= 2:
                        kb.wait("sc", sT, pvdone[j - 2])
                    pa, pb = sim3[j % 3]
                    kb.op("sc", lambda e, p=pa, d=pta[j % 2]: e.activation(
                        d, p, AF.Exp, bias=-16.0, scale=16.0))
                    kb.op("sc", lambda e, p=pb, d=ptb2[j % 2]: e.activation(
                        d, p, AF.Exp, bias=-16.0, scale=16.0), inc=sS)
                    expdone[j] = sS.n

                def rec_pv(j):
                    kb.wait("te", sS, expdone[j])
                    kb.op("te", lambda e, j=j, d=pta[j % 2]: e.matmul(
                        PW0[:, 0:512], vg[:, j, :], d,
                        start=(j == 0), stop=(j == 63)))
                    kb.op("te", lambda e, j=j, d=ptb2[j % 2]: e.matmul(
                        PW0[:, 512:1024], vg[:, j, :], d,
                        start=(j == 0), stop=(j == 63)), inc=sT)
                    pvdone[j] = sT.n

                kb.wait("te", sV, vout_ev)
                kb.wait("te", sS)
                rec_sim(0)
                rec_exp(0)
                rec_sim(1)
                rec_exp(1)
                for j in range(64):
                    if j + 2 < 64:
                        rec_sim(j + 2)
                    rec_pv(j)
                    if j + 2 < 64:
                        rec_exp(j + 2)
                vpv = sT.n
                kb.wait("ve", sT, vpv)
                kb.op("ve", lambda e: e.tensor_copy(xnbuf[0:1, 0:512], PW0[64:65, 0:512]))
                kb.op("ve", lambda e: e.tensor_copy(xnbuf[0:1, 512:1024], PW0[64:65, 512:1024]), inc=sV)
                kb.wait("ve", sV)
                kb.op("ve", lambda e: e.reciprocal(zbuf[0:1, 0:512], xnbuf[0:1, 0:512]), inc=sV)
                kb.op("ve", lambda e: e.reciprocal(rbuf[0:1, 0:512], xnbuf[0:1, 512:1024]), inc=sV)
                kb.wait("ve", sV)
                kb.op("ve", lambda e: e.tensor_copy(dre[0], zbuf[0:1, 0:512]))
                kb.op("ve", lambda e: e.tensor_copy(dre[1], rbuf[0:1, 0:512]), inc=sV)
                vdre = sV.n
                kb.wait("te", sV, vdre)
                kb.wait("te", sS)
                kb.op("te", lambda e: e.matmul(P0[0:64, :], onesb, dre[0],
                                               start=True, stop=True))
                kb.op("te", lambda e: e.matmul(P1[0:64, :], onesb, dre[1],
                                               start=True, stop=True), inc=sT)
                kb.wait("sc", sT)
                kb.op("sc", lambda e: e.copy(zbuf[0:64, :], P0[0:64, :]))
                kb.op("sc", lambda e: e.copy(rbuf[0:64, :], P1[0:64, :]), inc=sS)
                kb.wait("ve", sS)
                kb.op("ve", lambda e, hp=hp, q=qsl: e.tensor_mul(
                    attT[0:64, hp, q], PW0[0:64, 0:512], zbuf[0:64, :]), inc=sV)
                kb.wait("ve", sV)
                kb.op("ve", lambda e: e.tensor_mul(
                    abtmp[0:64, :], PW0[0:64, 512:1024], rbuf[0:64, :]), inc=sV)
                vout_ev = sV.n
                kb.wait("sy", sV, vout_ev)
                kb.dma("sy", attT[64:128, hp, qsl], abtmp[0:64, :], sAB)
                kb.wait("ve", sAB)

        if b3 < 4:
            bctx.close()
            return
        vatt = sAB.n
        kb.wait("te", sAB, vatt)
        ofree = 0
        xfree = {0: 0, 1: 0}
        for tc in range(NT):
            xnb = xnbuf[:] if tc % 2 == 0 else ob2[:]
            kb.wait("te", sV, ofree)
            kb.wait("te", sS)
            for hp in range(4):
                for half in range(2):
                    hs = slice(half * 512, (half + 1) * 512)
                    kb.op("te", lambda e, hp=hp, t=tc, hs=hs: e.matmul(
                        PW0[:, hs], attT[0:64, hp, t * P:(t + 1) * P], bWoS[0:64, hp, hs],
                        start=(hp == 0), stop=(hp == 3)))
                    kb.op("te", lambda e, hp=hp, t=tc, hs=hs: e.matmul(
                        PW1[:, hs], attT[64:128, hp, t * P:(t + 1) * P], bWoS[64:128, hp, hs],
                        start=(hp == 0), stop=(hp == 3)),
                        inc=sT if (hp == 3 and half == 1) else None)
            kb.wait("sc", sT)
            kb.wait("sc", sV, xfree[tc % 2])
            kb.op("sc", lambda e, d=xnb: e.copy(d, PW1[:]), inc=sS)
            kb.wait("ve", sS)
            kb.wait("ve", sM, vld2[2])
            kb.op("ve", lambda e, d=xnb: e.tensor_add(d, d, PW0[:]), inc=sV)
            kb.wait("ve", sV)
            kb.op("ve", lambda e, d=xnb: e.bn_stats(lnst[:, 0:6], d[:, 0:512]))
            kb.op("ve", lambda e, d=xnb: e.bn_stats(lnst[:, 6:12], d[:, 512:1024]), inc=sV)
            kb.wait("ve", sV)
            kb.op("ve", lambda e: e.bn_aggr(lnmv[:, 0:2], lnst[:]), inc=sV)
            kb.wait("sc", sV)
            kb.op("sc", lambda e: e.activation(lnmv[:, 2:3], lnmv[:, 1:2], AF.Sqrt,
                                               bias=EPS_LN, scale=1.0), inc=sS)
            kb.wait("ve", sS)
            kb.op("ve", lambda e: e.reciprocal(lnmv[:, 3:4], lnmv[:, 2:3]), inc=sV)
            kb.wait("ve", sV)
            kb.op("ve", lambda e, d=xnb: e.tensor_scalar(
                d, d, lnmv[:, 0:1], lnmv[:, 3:4],
                ALU.subtract, ALU.mult), inc=sV)
            kb.wait("ve", sV)
            kb.op("ve", lambda e, d=xnb: e.tensor_mul(d, d, gOutS[:]), inc=sV)
            kb.wait("ve", sV)
            kb.op("ve", lambda e, t=tc, d=xnb: e.tensor_add(X[:, t, :], X[:, t, :], d), inc=sV)
            xfree[tc % 2] = sV.n
            ofree = sV.n - 6
        bctx.close()

    if "attn_b" in phases:
        emit_attn_b()
        kb.flush_block("attn_b")

    if "ffn2" in phases:
        emit_ffn("f2")
        kb.flush_block("ffn2")

    kb.wait("sy", sV)
    kb.wait("sy", sT)
    kb.wait("sy", sS)
    kb.dma("sy", out.rearrange("(n p) d -> p n d", p=P), X[:], sO)
    kb.wait("gp", sO)
    kb.flush_block("store")

    ctx.close()
    return nc


def _prep_inputs(inputs):
    gi = lambda k: np.ascontiguousarray(np.asarray(inputs[k], np.float32))
    x = gi("L0_attn_a_x")
    xsh = np.ascontiguousarray(x.reshape(8, T, D))
    base = {
        "aWqT": np.ascontiguousarray(gi("L0_attn_a_Wq").T),
        "aWkT": np.ascontiguousarray(gi("L0_attn_a_Wk").T),
        "aWvT": np.ascontiguousarray(gi("L0_attn_a_Wv").T),
        "aWoT": np.ascontiguousarray(gi("L0_attn_a_Wo").T),
        "aBq": np.ascontiguousarray(gi("L0_attn_a_bq").reshape(DC, P).T),
        "aBk": np.ascontiguousarray(gi("L0_attn_a_bk").reshape(DC, P).T),
        "aBv": gi("L0_attn_a_bv"), "aBo": gi("L0_attn_a_bo"),
        "aBkF": gi("L0_attn_a_bk"),
        "bNormG": np.ascontiguousarray(gi("L0_attn_b_norm_g").reshape(DC, P).T),
        "bWq": gi("L0_attn_b_Wq"), "bWkv": gi("L0_attn_b_Wkv"),
        "bWout": gi("L0_attn_b_Wout"),
        "gOutB": np.ascontiguousarray(np.broadcast_to(gi("L0_attn_b_out_norm_g"), (P, D))),
        "gateW9T": np.ascontiguousarray(np.concatenate(
            [gi("L0_ffn_gate_w").T, gi("L0_ffn_shared_gate_w").T], axis=1)),
        "fWg": gi("L0_ffn_w_gate"), "fWu": gi("L0_ffn_w_up"), "fWd": gi("L0_ffn_w_down"),
        "fswG": gi("L0_ffn_sw_gate"), "fswU": gi("L0_ffn_sw_up"), "fswD": gi("L0_ffn_sw_down"),
        "consts": np.ascontiguousarray(np.broadcast_to(
            np.array([EPS_LN, -16.0, 16.0, 1e-6], np.float32), (P, 4))),
        "ident": np.eye(P, dtype=np.float32),
        "tril": np.ascontiguousarray(np.triu(np.ones((P, P), np.float32))),
    }
    in_maps = []
    for c in range(8):
        r = c % 4
        rm = np.zeros((P, 4), np.float32)
        rm[:, :r] = 1.0
        im = dict(base)
        im["xs"] = xsh[c]
        im["rmask_in"] = rm
        in_maps.append(im)
    return in_maps


def kernel(**inputs):
    in_maps = _prep_inputs(inputs)
    nc = build_program()
    res = run_bass_kernel_spmd(nc, in_maps, core_ids=list(range(8)))
    outs = np.stack([np.asarray(res.results[c]["out"], np.float32) for c in range(8)])
    return np.ascontiguousarray(outs.reshape(2, 8192, 1024))


# version 11
# speedup vs baseline: 1.2173x; 1.0149x over previous
import contextlib
import numpy as np

import concourse.bass as bass
import concourse.mybir as mybir
from concourse.bass_utils import run_bass_kernel_spmd

F32 = mybir.dt.float32
BF16 = mybir.dt.bfloat16
AF = mybir.ActivationFunctionType
ALU = mybir.AluOpType
AX = mybir.AxisListType.X

P = 128
T = 2048
NT = 16
D = 1024
DC = 8
NST = 4
STW = 512
FI = 1408
FIC = 11
NE = 8
EPS_LN = 1e-5

ENGS = ("sy", "gp", "te", "ve", "sc")


class Sem:
    __slots__ = ("h", "n", "name")

    def __init__(self, h, name):
        self.h = h
        self.n = 0
        self.name = name


class KB:
    def __init__(self, nc, ctx):
        self.nc = nc
        self.ctx = ctx
        self.q = {e: [] for e in ENGS}

    def sem(self, name):
        return Sem(self.ctx.enter_context(self.nc.semaphore(name)), name)

    def op(self, eng, fn, inc=None, inc_by=1):
        if inc is not None:
            inc.n += inc_by
            self.q[eng].append(lambda e, fn=fn, s=inc, b=inc_by: fn(e).then_inc(s.h, b))
        else:
            self.q[eng].append(lambda e, fn=fn: fn(e))

    def wait(self, eng, sem, val=None):
        v = sem.n if val is None else val
        if v <= 0:
            return
        self.q[eng].append(lambda e, s=sem, v=v: e.wait_ge(s.h, v))

    def dma(self, eng, out, in_, inc):
        self.op(eng, lambda e, o=out, i=in_: e.dma_start(out=o, in_=i), inc=inc, inc_by=16)

    def flush_block(self, name):
        nc = self.nc
        q = self.q
        self.q = {e: [] for e in ENGS}
        with nc.Block() as blk:
            @blk.sync
            def _(eng):
                for f in q["sy"]:
                    f(eng)

            @blk.gpsimd
            def _(eng):
                for f in q["gp"]:
                    f(eng)

            @blk.tensor
            def _(eng):
                for f in q["te"]:
                    f(eng)

            @blk.vector
            def _(eng):
                for f in q["ve"]:
                    f(eng)

            @blk.scalar
            def _(eng):
                for f in q["sc"]:
                    f(eng)


def build_program(phases=("attn_a", "ffn1", "attn_b", "ffn2"), alevel=3):
    nc = bass.Bass("TRN2", target_bir_lowering=False)
    ctx = contextlib.ExitStack()
    E = ctx.enter_context
    kb = KB(nc, ctx)

    def din(name, shape, dt=F32):
        return nc.dram_tensor(name, list(shape), dt, kind="ExternalInput").ap()

    xs = din("xs", (T, D))
    aWqT = din("aWqT", (D, D)); aWkT = din("aWkT", (D, D))
    aWvT = din("aWvT", (D, D)); aWoT = din("aWoT", (D, D))
    aBq = din("aBq", (P, DC)); aBk = din("aBk", (P, DC))
    aBv = din("aBv", (D,)); aBo = din("aBo", (D,)); aBkF = din("aBkF", (D,))
    bNormG = din("bNormG", (P, DC))
    bWq = din("bWq", (D, 512)); bWkv = din("bWkv", (D, 128))
    bWout = din("bWout", (512, D))
    gOutB = din("gOutB", (P, D))
    gateW9T = din("gateW9T", (D, 9))
    fWg = din("fWg", (NE, D, FI)); fWu = din("fWu", (NE, D, FI))
    fWd = din("fWd", (NE, FI, D))
    fswG = din("fswG", (D, FI)); fswU = din("fswU", (D, FI)); fswD = din("fswD", (FI, D))
    consts = din("consts", (P, 4))
    ident = din("ident", (P, P))
    tril = din("tril", (P, P))
    rmaskD = din("rmask_in", (P, 4))

    out = nc.dram_tensor("out", [T, D], F32, kind="ExternalOutput").ap()

    cc_s_in = nc.dram_tensor("cc_s_in", [D, D + 1], F32).ap()
    cc_s_out = nc.dram_tensor("cc_s_out", [4 * D, D + 1], F32).ap()
    cc_k_in = nc.dram_tensor("cc_k_in", [64, T], BF16).ap()
    cc_k_out = nc.dram_tensor("cc_k_out", [256, T], BF16).ap()
    ccvi = nc.dram_tensor("ccvi", [T, 64], BF16).ap()
    ccvo = nc.dram_tensor("ccvo", [4 * T, 64], BF16).ap()
    GROUPS = [[0, 1, 2, 3], [4, 5, 6, 7]]

    X = E(nc.sbuf_tensor("X", [P, NT, D], F32))
    slotA = E(nc.sbuf_tensor("slotA", [P, 11264], BF16))
    slotB = E(nc.sbuf_tensor("slotB", [P, 11264], BF16))
    sconst = E(nc.sbuf_tensor("sconst", [P, 4], F32))
    sid32 = E(nc.sbuf_tensor("sid32", [P, P], F32))
    sidbf = E(nc.sbuf_tensor("sidbf", [P, P], BF16))
    stril = E(nc.sbuf_tensor("stril", [P, P], F32))
    rmask = E(nc.sbuf_tensor("rmask", [P, 4], F32))
    lnst = E(nc.sbuf_tensor("lnst", [P, 12], F32))
    lnmv = E(nc.sbuf_tensor("lnmv", [P, 4], F32))
    xnbuf = E(nc.sbuf_tensor("xnbuf", [P, D], F32))
    zbuf = E(nc.sbuf_tensor("zbuf", [P, STW], F32))
    rbuf = E(nc.sbuf_tensor("rbuf", [P, STW], F32))
    ebuf = E(nc.sbuf_tensor("ebuf", [P, STW], F32))

    onesrow = slotA[0:1, 8192:8320]
    rowBv = slotA[0:1, 8320:9344]
    rowBo = slotA[0:1, 9344:10368]
    Atb = slotB[:, 8192:8320]
    onescol = slotB[:, 8320:8321]
    rowBk = slotB[0:1, 8448:9472]
    OTb = slotB[:, 9472:10496].rearrange("p (c n) -> p c n", c=DC)
    Obuf = xnbuf

    nc.const_aps.register(F32, EPS_LN, sconst[:, 0:1])
    nc.const_aps.register(F32, -16.0, sconst[:, 1:2])
    nc.const_aps.register(F32, 16.0, sconst[:, 2:3])
    nc.const_aps.register(F32, 1e-6, sconst[:, 3:4])

    P0 = E(nc.psum_tensor("P0", [P, 512], F32))
    P1 = E(nc.psum_tensor("P1", [P, 512], F32))
    PT0 = E(nc.psum_tensor("PT0", [P, 512], F32))
    PT1 = E(nc.psum_tensor("PT1", [P, 512], F32))
    PW0 = E(nc.psum_tensor("PW0", [P, 1024], F32))
    PW1 = E(nc.psum_tensor("PW1", [P, 1024], F32))
    TRP = (PT0[:, 0:128], PT1[:, 0:128])
    APS = P0[:, 0:128]
    ZCOLS = (P1[:, 0:1], P1[:, 8:9])
    DENC = P1[:, 4:5]

    sX = kb.sem("sX")
    sW = kb.sem("sW")
    sT = kb.sem("sT")
    sV = kb.sem("sV")
    sS = kb.sem("sS")
    sO = kb.sem("sO")
    sC1 = kb.sem("sC1")
    sC2 = kb.sem("sC2")
    sC3 = kb.sem("sC3")
    sM = kb.sem("sM")
    sLA = kb.sem("sLA")
    sLB = kb.sem("sLB")
    sEX = kb.sem("sEX")
    sAB = kb.sem("sAB")

    def slotv(slot, nchunk=DC, width=D):
        return slot[:, 0:nchunk * width].rearrange("p (c n) -> p c n", c=nchunk)

    def load_weight(slot, w_dram, nchunk=DC, width=D, sem=None):
        if sem is None:
            sem = sLA if slot is slotA else sLB
        kb.wait("gp", sem)
        kb.dma("gp", slotv(slot, nchunk, width),
               w_dram.rearrange("(c p) n -> p c n", p=P), sem)
        return sem.n

    def ln_tile(tt, src=None):
        s = X[:, tt, :] if src is None else src
        kb.wait("ve", sT)
        kb.wait("ve", sV)
        kb.op("ve", lambda e, s=s: e.bn_stats(lnst[:, 0:6], s[:, 0:512]))
        kb.op("ve", lambda e, s=s: e.bn_stats(lnst[:, 6:12], s[:, 512:1024]), inc=sV)
        kb.wait("ve", sV)
        kb.op("ve", lambda e: e.bn_aggr(lnmv[:, 0:2], lnst[:]), inc=sV)
        kb.wait("sc", sV)
        kb.op("sc", lambda e: e.activation(lnmv[:, 2:3], lnmv[:, 1:2], AF.Sqrt,
                                           bias=EPS_LN, scale=1.0), inc=sS)
        kb.wait("ve", sS)
        kb.op("ve", lambda e: e.reciprocal(lnmv[:, 3:4], lnmv[:, 2:3]), inc=sV)
        kb.wait("ve", sV)
        kb.op("ve", lambda e, s=s: e.tensor_scalar(
            xnbuf[:], s, lnmv[:, 0:1], lnmv[:, 3:4], ALU.subtract, ALU.mult), inc=sV)
        return sV.n

    tr_last = {0: 0, 1: 0}

    def transpose128(src_ap, dst_ap, k, rows=128):
        pv = TRP[k % 2][0:rows, :]
        kb.wait("te", sS, tr_last[k % 2])
        kb.op("te", lambda e, s=src_ap, p=pv: e.transpose(p, s, sid32[:]), inc=sT)
        kb.wait("sc", sT)
        kb.op("sc", lambda e, p=pv, d=dst_ap: e.copy(d, p), inc=sS)
        tr_last[k % 2] = sS.n
        return sS.n

    def phi_evac(psum, bias_ap, dst_ap, width=STW):
        kb.wait("ve", sT)
        kb.wait("ve", sS)
        kb.op("ve", lambda e, p=psum, b=bias_ap: e.tensor_scalar(
            zbuf[:, 0:width], p, b, None, ALU.add), inc=sV)
        kb.wait("sc", sV)
        kb.op("sc", lambda e: e.activation(rbuf[:, 0:width], zbuf[:, 0:width], AF.Relu), inc=sS)
        kb.wait("ve", sS)
        kb.op("ve", lambda e: e.tensor_sub(zbuf[:, 0:width], zbuf[:, 0:width],
                                           rbuf[:, 0:width]), inc=sV)
        kb.wait("sc", sV)
        kb.op("sc", lambda e: e.activation(ebuf[:, 0:width], zbuf[:, 0:width], AF.Exp), inc=sS)
        kb.wait("ve", sS)
        kb.op("ve", lambda e, d=dst_ap: e.tensor_add(d, ebuf[:, 0:width],
                                                     rbuf[:, 0:width]), inc=sV)
        return sV.n

    kb.dma("sy", sconst[:], consts, sX)
    kb.dma("sy", sid32[:], ident, sX)
    kb.dma("sy", stril[:], tril, sX)
    kb.dma("sy", rmask[:], rmaskD, sX)
    kb.dma("sy", X[:], xs.rearrange("(n p) d -> p n d", p=P), sX)
    kb.dma("gp", sidbf[:], ident, sW)
    kb.op("ve", lambda e: e.memset(onesrow, 1.0))
    kb.op("ve", lambda e: e.memset(onescol, 1.0), inc=sV)
    wK = load_weight(slotA, aWkT)
    wV = load_weight(slotB, aWvT)
    kb.flush_block("init")

    if "attn_a" in phases:
        actx = contextlib.ExitStack()
        Ea = actx.enter_context
        arena = Ea(nc.sbuf_tensor("aArena", [P, 20480], BF16))
        xnT = arena[:, 0:4096].rearrange("p (c n) -> p c n", c=DC)
        QpT = arena[:, 4096:8192].rearrange("p (c n) -> p c n", c=DC)
        KpT = arena[:, 8192:12288].rearrange("p (c n) -> p c n", c=DC)
        Vst = arena[:, 12288:16384].rearrange("p (c n) -> p c n", c=4)
        Kptok = arena[:, 16384:20480].rearrange("p (c n) -> p c n", c=4)
        SgF = arena[:].bitcast(F32)
        Sg = SgF[:, 0:8200].rearrange("p (c n) -> p c n", c=DC)
        S = Ea(nc.sbuf_tensor("a_S", [P, DC, 1025], F32))
        Sbf = Ea(nc.sbuf_tensor("a_Sbf", [P, DC, 1025], BF16))
        denb = Ea(nc.sbuf_tensor("a_den", [P, 2], F32))
        aBqc = Ea(nc.sbuf_tensor("a_Bq", [P, DC], F32))
        aBkc = Ea(nc.sbuf_tensor("a_Bk", [P, DC], F32))

        kb.dma("sy", aBqc[:], aBq, sX)
        kb.dma("sy", aBkc[:], aBk, sX)
        kb.dma("gp", rowBv, aBv.rearrange("(a d) -> a d", a=1), sW)
        kb.dma("gp", rowBo, aBo.rearrange("(a d) -> a d", a=1), sW)
        kb.dma("gp", rowBk, aBkF.rearrange("(a d) -> a d", a=1), sW)
        kb.op("ve", lambda e: e.memset(S[:], 0.0), inc=sV)
        ldx = (sX.n, sW.n)

        def st_ln_transpose(st):
            import os as _os
            trlim = int(_os.environ.get("TRLIM", str(DC)))
            for i in range(4):
                vln = ln_tile(st * 4 + i)
                kb.wait("te", sV, vln)
                for jc in range(trlim):
                    transpose128(xnbuf[:, jc * P:(jc + 1) * P],
                                 xnT[:, jc, i * P:(i + 1) * P], jc)
            return sS.n

        pz_last = {0: 0, 1: 0, 2: 0, 3: 0}

        def proj_feature(slot, bias_c, dstT, vready):
            sl = slotv(slot)
            pcyc = (P0[:], P1[:], PW1[:, 0:512], PW1[:, 512:1024])
            for oc in range(DC):
                ps = pcyc[oc % 4]
                kb.wait("te", sS, vready)
                kb.wait("te", sV, pz_last[oc % 4])
                for jc in range(DC):
                    kb.op("te", lambda e, o=oc, j=jc, p=ps, s=sl: e.matmul(
                        p, s[:, j, o * P:(o + 1) * P], xnT[:, j, :],
                        start=(j == 0), stop=(j == DC - 1)),
                        inc=sT if jc == DC - 1 else None)
                phi_evac(ps, bias_c[:, oc:oc + 1], dstT[:, oc, :])
                pz_last[oc % 4] = sV.n - 2
            return sV.n

        def phi_psum(psum_half, dst_ap, width=STW):
            kb.wait("sc", sT)
            kb.wait("sc", sV)
            kb.op("sc", lambda e, p=psum_half: e.activation(rbuf[:, 0:width], p, AF.Relu), inc=sS)
            kb.wait("ve", sS)
            kb.op("ve", lambda e, p=psum_half: e.tensor_sub(zbuf[:, 0:width], p, rbuf[:, 0:width]), inc=sV)
            kb.wait("sc", sV)
            kb.op("sc", lambda e: e.activation(ebuf[:, 0:width], zbuf[:, 0:width], AF.Exp), inc=sS)
            kb.wait("ve", sS)
            kb.op("ve", lambda e, d=dst_ap: e.tensor_add(d, ebuf[:, 0:width], rbuf[:, 0:width]), inc=sV)
            return sV.n

        def proj_Ktok(vready):
            sl = slotv(slotA)
            for i in range(4):
                kb.wait("te", sS, vready)
                kb.wait("te", sV)
                for jc in range(DC):
                    kb.op("te", lambda e, i=i, j=jc, s=sl: e.matmul(
                        PW0[:, 0:512], xnT[:, j, i * P:(i + 1) * P], s[:, j, 0:512],
                        start=(j == 0), stop=False))
                    kb.op("te", lambda e, i=i, j=jc, s=sl: e.matmul(
                        PW0[:, 512:1024], xnT[:, j, i * P:(i + 1) * P], s[:, j, 512:1024],
                        start=(j == 0), stop=False))
                kb.op("te", lambda e: e.matmul(PW0[:, 0:512], onesrow, rowBk[:, 0:512],
                                               start=False, stop=True))
                kb.op("te", lambda e: e.matmul(PW0[:, 512:1024], onesrow, rowBk[:, 512:1024],
                                               start=False, stop=True), inc=sT)
                phi_psum(PW0[:, 0:512], Kptok[:, i, 0:512])
                phi_psum(PW0[:, 512:1024], Kptok[:, i, 512:1024])

        def proj_V(vready):
            sl = slotv(slotB)
            for i in range(4):
                kb.wait("te", sS, vready)
                kb.wait("te", sV)
                for jc in range(DC):
                    kb.op("te", lambda e, i=i, j=jc, s=sl: e.matmul(
                        PW0[:, 0:512], xnT[:, j, i * P:(i + 1) * P], s[:, j, 0:512],
                        start=(j == 0), stop=False))
                    kb.op("te", lambda e, i=i, j=jc, s=sl: e.matmul(
                        PW0[:, 512:1024], xnT[:, j, i * P:(i + 1) * P], s[:, j, 512:1024],
                        start=(j == 0), stop=False))
                kb.op("te", lambda e: e.matmul(PW0[:, 0:512], onesrow, rowBv[:, 0:512],
                                               start=False, stop=True))
                kb.op("te", lambda e: e.matmul(PW0[:, 512:1024], onesrow, rowBv[:, 512:1024],
                                               start=False, stop=True), inc=sT)
                kb.wait("ve", sT)
                kb.op("ve", lambda e, i=i: e.tensor_copy(Vst[:, i, :], PW0[:]), inc=sV)
            return sV.n

        kvfree = {0: 0, 1: 0}

        def chunk_KV(i):
            for dc in range(DC):
                pw = PW1 if dc % 2 == 0 else PW0
                if dc < 2:
                    kb.wait("te", sV)
                else:
                    kb.wait("te", sV, kvfree[dc % 2])
                kb.op("te", lambda e, d=dc, i=i, p=pw: e.matmul(
                    p[:, 0:512], Kptok[:, i, d * P:(d + 1) * P], Vst[:, i, 0:512],
                    start=True, stop=True))
                kb.op("te", lambda e, d=dc, i=i, p=pw: e.matmul(
                    p[:, 512:1024], Kptok[:, i, d * P:(d + 1) * P], Vst[:, i, 512:1024],
                    start=True, stop=True))
                zc = ZCOLS[dc % 2]
                kb.op("te", lambda e, d=dc, i=i, z=zc: e.matmul(
                    z, Kptok[:, i, d * P:(d + 1) * P], onescol, start=True, stop=True), inc=sT)
                kb.wait("ve", sT)
                kb.op("ve", lambda e, d=dc, p=pw: e.tensor_add(S[:, d, 0:1024], S[:, d, 0:1024], p[:]))
                kb.op("ve", lambda e, d=dc, z=zc: e.tensor_add(S[:, d, 1024:1025], S[:, d, 1024:1025],
                                                               z), inc=sV)
                kvfree[dc % 2] = sV.n

        state = {"vsbf": 0}

        def mixing_chunk(st, i, s2=9):
            tt = st * 4 + i
            if s2 < 1:
                return
            kb.wait("te", sV)
            for dc in range(DC):
                kb.op("te", lambda e, d=dc, i=i: e.matmul(
                    APS, KpT[:, d, i * P:(i + 1) * P], QpT[:, d, i * P:(i + 1) * P],
                    start=(d == 0), stop=(d == DC - 1)), inc=sT if dc == DC - 1 else None)
            kb.wait("ve", sT)
            kb.op("ve", lambda e: e.tensor_mul(Atb, APS, stril[:]), inc=sV)
            vatb = sV.n
            if s2 < 2:
                return
            chunk_KV(i)
            if s2 < 3:
                return
            kb.wait("te", sV, state["vsbf"])
            kb.wait("te", sV)
            for dc in range(DC):
                kb.op("te", lambda e, d=dc, i=i: e.matmul(
                    PW0[:, 0:512], QpT[:, d, i * P:(i + 1) * P], Sbf[:, d, 0:512],
                    start=(d == 0), stop=False))
                kb.op("te", lambda e, d=dc, i=i: e.matmul(
                    PW0[:, 512:1024], QpT[:, d, i * P:(i + 1) * P], Sbf[:, d, 512:1024],
                    start=(d == 0), stop=False))
                kb.op("te", lambda e, d=dc, i=i: e.matmul(
                    DENC, QpT[:, d, i * P:(i + 1) * P], Sbf[:, d, 1024:1025],
                    start=(d == 0), stop=False))
            kb.op("te", lambda e, i=i: e.matmul(PW0[:, 0:512], Atb, Vst[:, i, 0:512],
                                                start=False, stop=True))
            kb.op("te", lambda e, i=i: e.matmul(PW0[:, 512:1024], Atb, Vst[:, i, 512:1024],
                                                start=False, stop=True))
            kb.op("te", lambda e: e.matmul(DENC, Atb, onescol, start=False, stop=True), inc=sT)
            vnum = sT.n
            kb.wait("ve", sT, vnum)
            kb.op("ve", lambda e: e.tensor_scalar(denb[:, 0:1], DENC, 1e-6, None, ALU.add), inc=sV)
            kb.wait("ve", sV)
            kb.op("ve", lambda e: e.reciprocal(denb[:, 1:2], denb[:, 0:1]), inc=sV)
            kb.wait("ve", sV)
            kb.op("ve", lambda e: e.tensor_scalar(Obuf[:], PW0[:], denb[:, 1:2], None, ALU.mult), inc=sV)
            vO = sV.n
            if s2 < 4:
                return
            kb.op("ve", lambda e: e.tensor_copy(Sbf[:], S[:]), inc=sV)
            state["vsbf"] = sV.n
            kb.wait("te", sV, vO)
            for ec in range(DC):
                transpose128(Obuf[:, ec * P:(ec + 1) * P], OTb[:, ec, :], ec)
            vOT = sS.n
            sl = slotv(slotB)
            kb.wait("te", sS, vOT)
            kb.wait("te", sV)
            for ec in range(DC):
                kb.op("te", lambda e, c=ec, s=sl: e.matmul(
                    PW1[:, 0:512], OTb[:, c, :], s[:, c, 0:512], start=(c == 0), stop=False))
                kb.op("te", lambda e, c=ec, s=sl: e.matmul(
                    PW1[:, 512:1024], OTb[:, c, :], s[:, c, 512:1024], start=(c == 0), stop=False))
            kb.op("te", lambda e: e.matmul(PW1[:, 0:512], onesrow, rowBo[:, 0:512],
                                           start=False, stop=True))
            kb.op("te", lambda e: e.matmul(PW1[:, 512:1024], onesrow, rowBo[:, 512:1024],
                                           start=False, stop=True), inc=sT)
            kb.wait("ve", sT)
            kb.op("ve", lambda e, t=tt: e.tensor_add(X[:, t, :], X[:, t, :], PW1[:]), inc=sV)

        import os
        sub = int(os.environ.get("SUBLEVEL", "9"))
        nst_lim = int(os.environ.get("NST_LIM", str(NST)))
        kb.wait("te", sX, ldx[0])
        kb.wait("te", sW, ldx[1])
        kb.wait("te", sLA, wK)
        kb.wait("te", sLB, wV)
        kb.wait("ve", sX, ldx[0])
        def st_KV_b1():
            for dc in range(DC):
                pw = PW1 if dc % 2 == 0 else PW0
                zc = ZCOLS[dc % 2]
                if dc < 2:
                    kb.wait("te", sV)
                else:
                    kb.wait("te", sV, kvfree[dc % 2])
                for i in range(4):
                    kb.op("te", lambda e, d=dc, i=i, p=pw: e.matmul(
                        p[:, 0:512], Kptok[:, i, d * P:(d + 1) * P], Vst[:, i, 0:512],
                        start=(i == 0), stop=(i == 3)))
                    kb.op("te", lambda e, d=dc, i=i, p=pw: e.matmul(
                        p[:, 512:1024], Kptok[:, i, d * P:(d + 1) * P], Vst[:, i, 512:1024],
                        start=(i == 0), stop=(i == 3)))
                    kb.op("te", lambda e, d=dc, i=i, z=zc: e.matmul(
                        z, Kptok[:, i, d * P:(d + 1) * P], onescol,
                        start=(i == 0), stop=(i == 3)),
                        inc=sT if i == 3 else None)
                kb.wait("ve", sT)
                kb.op("ve", lambda e, d=dc, p=pw: e.tensor_add(S[:, d, 0:1024], S[:, d, 0:1024], p[:]))
                kb.op("ve", lambda e, d=dc, z=zc: e.tensor_add(S[:, d, 1024:1025], S[:, d, 1024:1025],
                                                               z), inc=sV)
                kvfree[dc % 2] = sV.n

        for st in range(nst_lim):
            vtr = st_ln_transpose(st)
            if sub == 4:
                proj_feature(slotA, aBkc, KpT, vtr)
            if sub >= 1 and sub != 4:
                proj_Ktok(vtr)
            if sub >= 2:
                proj_V(vtr)
            if sub >= 3:
                st_KV_b1()

        if alevel < 2:
            actx.close()
            kb.flush_block("attn_a")
    if "attn_a" in phases and alevel >= 2:
        kb.wait("gp", sV)
        kb.dma("gp", cc_s_in.rearrange("(c p) e -> p c e", p=P), S[:], sEX)
        kb.wait("gp", sEX)
        kb.op("gp", lambda e: e.collective_compute(
            "AllGather", ALU.bypass, replica_groups=GROUPS,
            ins=[cc_s_in], outs=[cc_s_out]), inc=sC1)
        kb.wait("gp", sC1, 1)
        for r in range(4):
            kb.wait("gp", sV)
            kb.dma("gp", Sg, cc_s_out[r * D:(r + 1) * D, :].rearrange("(c p) e -> p c e", p=P), sEX)
            kb.wait("ve", sEX)
            if r == 0:
                kb.op("ve", lambda e: e.tensor_scalar(
                    S[:], Sg, rmask[:, 0:1], None, ALU.mult), inc=sV)
            else:
                kb.op("ve", lambda e, r=r: e.scalar_tensor_tensor(
                    S[:], Sg, rmask[:, r:r + 1], S[:], ALU.mult, ALU.add), inc=sV)
        kb.wait("ve", sV)
        kb.op("ve", lambda e: e.tensor_copy(Sbf[:], S[:]), inc=sV)
        state["vsbf"] = sV.n
        vexch = sV.n

        kb.wait("gp", sT)
        wQ = load_weight(slotA, aWqT)

        if alevel < 3:
            actx.close()
            kb.flush_block("attn_a")

    if "attn_a" in phases and alevel >= 3:
        import os
        nst_lim2 = int(os.environ.get("NST_LIM2", str(NST)))
        wVcur = wV
        s3 = int(os.environ.get("SUB3", "9"))
        for st in range(nst_lim2):
            if st == 0:
                kb.wait("ve", sV, vexch)
                kb.wait("sc", sV, vexch)
            if s3 >= -1:
                if s3 == -1:
                    for _i4 in range(4):
                        ln_tile(st * 4 + _i4)
                else:
                    vtr = st_ln_transpose(st)
            if s3 >= 1:
                kb.wait("te", sLA, wQ)
                proj_feature(slotA, aBqc, QpT, vtr)
            if s3 >= 2:
                kb.wait("gp", sT)
                wKn = load_weight(slotA, aWkT)
                kb.wait("te", sLA, wKn)
                proj_feature(slotA, aBkc, KpT, vtr)
            if s3 >= 3:
                proj_Ktok(vtr)
            if s3 >= 4:
                kb.wait("te", sLB, wVcur)
                proj_V(vtr)
            if s3 >= 5:
                kb.wait("gp", sT)
                wO = load_weight(slotB, aWoT)
                kb.wait("te", sLB, wO)
            if s3 >= 6:
                for i in range(4):
                    mixing_chunk(st, i, s2=int(os.environ.get("SUB2", "9")))
            if s3 >= 5 and st < NST - 1:
                kb.wait("gp", sT)
                wVcur = load_weight(slotB, aWvT)
                wQ = load_weight(slotA, aWqT)

        actx.close()
        kb.flush_block("attn_a")

    def emit_ffn(tag):
        fctx = contextlib.ExitStack()
        Ef = fctx.enter_context
        xnTf = Ef(nc.sbuf_tensor(tag + "_xnT", [P, DC, T], BF16))
        midT = Ef(nc.sbuf_tensor(tag + "_midT", [P, FIC, T], BF16))
        cmb = Ef(nc.sbuf_tensor(tag + "_cmb", [P, NT, 9], F32))
        rt = Ef(nc.sbuf_tensor(tag + "_rt", [P, 24], F32))
        gw9 = Ef(nc.sbuf_tensor(tag + "_gw9", [P, DC, 9], BF16))

        kb.dma("gp", gw9[:], gateW9T.rearrange("(c p) n -> p c n", p=P), sW)
        wg9 = sW.n
        kb.wait("ve", sX)

        wlist = []
        for e in range(NE):
            wlist += [(fWg[e], DC, FI), (fWu[e], DC, FI), (fWd[e], FIC, D)]
        wlist += [(fswG, DC, FI), (fswU, DC, FI), (fswD, FIC, D)]
        slots = [slotA, slotB]
        ssem = [sLA, sLB]
        wcnt = [0] * len(wlist)

        def load_wi(i):
            if i >= len(wlist):
                return
            dram, nch, wid = wlist[i]
            kb.wait("gp", sT)
            wcnt[i] = load_weight(slots[i % 2], dram, nch, wid)

        load_wi(0)
        load_wi(1)

        for tt in range(NT):
            vln = ln_tile(tt)
            kb.wait("te", sV, vln)
            for jc in range(DC):
                transpose128(xnbuf[:, jc * P:(jc + 1) * P],
                             xnTf[:, jc, tt * P:(tt + 1) * P], jc)
            vtr = sS.n
            kb.wait("te", sS, vtr)
            kb.wait("te", sV)
            kb.wait("te", sW, wg9)
            for jc in range(DC):
                kb.op("te", lambda e, j=jc, t=tt: e.matmul(
                    P0[:, 0:9], xnTf[:, j, t * P:(t + 1) * P], gw9[:, j, :],
                    start=(j == 0), stop=(j == DC - 1)), inc=sT if jc == DC - 1 else None)
            kb.wait("ve", sT)
            kb.wait("ve", sS)
            kb.op("ve", lambda e: e.tensor_copy(rt[:, 0:9], P0[:, 0:9]), inc=sV)
            kb.wait("ve", sV)
            kb.op("ve", lambda e: e.reduce_max(rt[:, 9:10], rt[:, 0:8], axis=AX,
                                               negate=True), inc=sV)
            kb.wait("sc", sV)
            kb.op("sc", lambda e: e.activation(rt[:, 16:24], rt[:, 0:8], AF.Exp,
                                               bias=rt[:, 9:10], scale=1.0), inc=sS)
            kb.op("sc", lambda e, t=tt: e.activation(cmb[:, t, 8:9], rt[:, 8:9],
                                                     AF.Sigmoid), inc=sS)
            kb.wait("ve", sS)
            kb.op("ve", lambda e: e.reduce_max(rt[:, 13:14], rt[:, 16:24], axis=AX), inc=sV)
            kb.wait("ve", sV)
            kb.op("ve", lambda e, t=tt: e.tensor_scalar(
                cmb[:, t, 0:8], rt[:, 16:24], rt[:, 13:14], 1e30, ALU.is_ge, ALU.mult), inc=sV)
            kb.wait("ve", sV)
            kb.op("ve", lambda e, t=tt: e.tensor_sub(cmb[:, t, 0:8], rt[:, 16:24],
                                                     cmb[:, t, 0:8]), inc=sV)
            kb.wait("ve", sV)
            kb.op("ve", lambda e, t=tt: e.reduce_max(rt[:, 10:11], cmb[:, t, 0:8],
                                                     axis=AX), inc=sV)
            kb.wait("ve", sV)
            kb.op("ve", lambda e: e.tensor_tensor(rt[:, 11:12], rt[:, 13:14],
                                                  rt[:, 10:11], ALU.add), inc=sV)
            kb.wait("ve", sV)
            kb.op("ve", lambda e: e.reciprocal(rt[:, 12:13], rt[:, 11:12]), inc=sV)
            kb.wait("ve", sV)
            kb.op("ve", lambda e, t=tt: e.scalar_tensor_tensor(
                cmb[:, t, 0:8], rt[:, 16:24], rt[:, 10:11], rt[:, 16:24],
                ALU.is_ge, ALU.mult), inc=sV)
            kb.wait("ve", sV)
            kb.op("ve", lambda e, t=tt: e.tensor_scalar(
                cmb[:, t, 0:8], cmb[:, t, 0:8], rt[:, 12:13], None, ALU.mult), inc=sV)
        vxnT = sS.n

        gbank = (P0, PT0)
        ubank = (P1, PT1)
        gfree = {0: 0, 1: 0}
        ufree = {0: 0, 1: 0}
        dfree = {0: 0, 1: 0}
        vdown_prev = 0

        kb.wait("te", sV)
        for ei in range(NE + 1):
            i_g, i_u, i_d = 3 * ei, 3 * ei + 1, 3 * ei + 2
            slg = slotv(slots[i_g % 2], DC, FI)
            slu = slotv(slots[i_u % 2], DC, FI)
            kb.wait("te", ssem[i_g % 2], wcnt[i_g])
            kb.wait("te", ssem[i_u % 2], wcnt[i_u])
            kb.wait("te", sS, vxnT)
            kb.wait("ve", sT, vdown_prev)
            kb.wait("te", sV)
            gps2 = ((P0[:], PT0[:]), (PW0[:, 0:512], PW0[:, 512:1024]))
            ups2 = ((P1[:], PT1[:]), (PW1[:, 0:512], PW1[:, 512:1024]))
            sil2 = ((zbuf[:, 0:512], rbuf[:, 0:512]), (ebuf[:, 0:512], xnbuf[:, 0:512]))
            gq2free = [0, 0]
            uq2free = [0, 0]
            it = 0
            for ic in range(FIC):
                for tqp in range(2):
                    par = it % 2
                    it += 1
                    tq0, tq1 = 2 * tqp, 2 * tqp + 1
                    kb.wait("te", sS, gq2free[par])
                    kb.wait("te", sV, uq2free[par])
                    for jc in range(DC):
                        for k, tq in enumerate((tq0, tq1)):
                            kb.op("te", lambda e, j=jc, c=ic, q=tq, s=slg, p=gps2[par][k]: e.matmul(
                                p, s[:, j, c * P:(c + 1) * P],
                                xnTf[:, j, q * 512:(q + 1) * 512],
                                start=(j == 0), stop=(j == DC - 1)))
                        for k, tq in enumerate((tq0, tq1)):
                            kb.op("te", lambda e, j=jc, c=ic, q=tq, s=slu, p=ups2[par][k]: e.matmul(
                                p, s[:, j, c * P:(c + 1) * P],
                                xnTf[:, j, q * 512:(q + 1) * 512],
                                start=(j == 0), stop=(j == DC - 1)),
                                inc=sT if (jc == DC - 1 and k == 1) else None)
                    kb.wait("sc", sT)
                    kb.wait("sc", sV, uq2free[par])
                    for k in range(2):
                        kb.op("sc", lambda e, k=k, par=par: e.activation(
                            sil2[par][k], gps2[par][k], AF.Silu), inc=sS)
                    gq2free[par] = sS.n
                    kb.wait("ve", sS)
                    for k, tq in enumerate((tq0, tq1)):
                        kb.op("ve", lambda e, c=ic, q=tq, k=k, par=par: e.tensor_tensor(
                            midT[:, c, q * 512:(q + 1) * 512], sil2[par][k], ups2[par][k],
                            ALU.mult), inc=sV)
                    uq2free[par] = sV.n
            vmid = sV.n
            load_wi(i_d)
            load_wi(i_g + 3)
            sld = slotv(slots[i_d % 2], FIC, D)
            kb.wait("te", ssem[i_d % 2], wcnt[i_d])
            kb.wait("ve", sT)
            kb.wait("te", sV, vmid)
            kb.wait("te", sS)
            for tc in range(NT):
                b = tc % 2
                pw = (PW0, PW1)[b]
                kb.wait("te", sV, dfree[b])
                for ic in range(FIC):
                    kb.op("te", lambda e, c=ic, t=tc, s=sld, p=pw: e.matmul(
                        p[:, 0:512], midT[:, c, t * P:(t + 1) * P], s[:, c, 0:512],
                        start=(c == 0), stop=(c == FIC - 1)))
                    kb.op("te", lambda e, c=ic, t=tc, s=sld, p=pw: e.matmul(
                        p[:, 512:1024], midT[:, c, t * P:(t + 1) * P], s[:, c, 512:1024],
                        start=(c == 0), stop=(c == FIC - 1)),
                        inc=sT if ic == FIC - 1 else None)
                kb.wait("ve", sT)
                kb.op("ve", lambda e, t=tc, ei=ei, p=pw: e.scalar_tensor_tensor(
                    X[:, t, :], p[:], cmb[:, t, ei:ei + 1], X[:, t, :],
                    ALU.mult, ALU.add), inc=sV)
                dfree[b] = sV.n
            vdown_prev = sT.n
            load_wi(i_u + 3)
        fctx.close()

    if "ffn1" in phases:
        emit_ffn("f1")
        kb.flush_block("ffn1")

    def emit_attn_b():
        import os as _os
        b3 = int(_os.environ.get("B3LEVEL", "9"))
        bctx = contextlib.ExitStack()
        Eb = bctx.enter_context
        qhT = Eb(nc.sbuf_tensor("b_qhT", [P, 4, T], BF16))
        ob2 = Eb(nc.sbuf_tensor("b_ob2", [P, D], F32))
        pta = (slotA[:, 1024:1536], slotA[:, 1536:2048])
        ptb2 = (slotA[:, 2048:2560], slotA[:, 2560:3072])
        onesb = slotA[0:1, 3072:3136]
        dre = (slotA[0:1, 3136:3648], slotA[0:1, 3648:4160])
        abtmp = slotA[:, 4224:4736]
        bq_s = Eb(nc.sbuf_tensor("b_qs", [P, 32], F32))

        sub1 = contextlib.ExitStack()
        xnnT = sub1.enter_context(nc.sbuf_tensor("b_xnnT", [P, DC, T], BF16))
        bWqS = sub1.enter_context(nc.sbuf_tensor("b_Wq", [P, DC, 512], BF16))
        bWkvS = sub1.enter_context(nc.sbuf_tensor("b_Wkv", [P, DC, P], BF16))
        bGc = sub1.enter_context(nc.sbuf_tensor("b_gc", [P, DC], F32))
        kTl = sub1.enter_context(nc.sbuf_tensor("b_kTl", [64, T], BF16))
        vloc = sub1.enter_context(nc.sbuf_tensor("b_vloc", [P, NT, 64], BF16))

        for _e in ("ve", "sc", "te", "sy"):
            kb.wait(_e, sLA)
            kb.wait(_e, sLB)
        kb.op("ve", lambda e: e.memset(onesb, 1.0), inc=sV)
        kb.wait("sy", sM)
        kb.dma("sy", bGc[:], bNormG, sM)
        kb.dma("gp", bWqS[:], bWq.rearrange("(c p) n -> p c n", p=P), sW)
        kb.dma("gp", bWkvS[:], bWkv.rearrange("(c p) n -> p c n", p=P), sW)
        wqv = sW.n
        vgc = sM.n
        kb.wait("ve", sW, wqv)
        kb.wait("ve", sM, vgc)
        for jc in range(DC):
            kb.op("ve", lambda e, j=jc: e.tensor_scalar(
                bWqS[:, j, :], bWqS[:, j, :], bGc[:, j:j + 1], None, ALU.mult), inc=sV)
            kb.op("ve", lambda e, j=jc: e.tensor_scalar(
                bWkvS[:, j, :], bWkvS[:, j, :], bGc[:, j:j + 1], None, ALU.mult), inc=sV)
        vfold = sV.n
        kb.wait("ve", sX)

        for tt in range(NT):
            ln_tile(tt)
            ln_tile(tt, src=xnbuf[:])
            vln = sV.n
            kb.wait("te", sV, vln)
            for jc in range(DC):
                transpose128(xnbuf[:, jc * P:(jc + 1) * P],
                             xnnT[:, jc, tt * P:(tt + 1) * P], jc)
            vtr = sS.n
            kb.wait("te", sS, vtr)
            kb.wait("te", sV)
            kb.wait("te", sW, wqv)
            kb.wait("te", sV, vfold)
            for jc in range(DC):
                kb.op("te", lambda e, j=jc, t=tt: e.matmul(
                    PW0[:, 0:512], xnnT[:, j, t * P:(t + 1) * P], bWqS[:, j, :],
                    start=(j == 0), stop=(j == DC - 1)))
            for jc in range(DC):
                kb.op("te", lambda e, j=jc, t=tt: e.matmul(
                    P1[:, 0:128], xnnT[:, j, t * P:(t + 1) * P], bWkvS[:, j, :],
                    start=(j == 0), stop=(j == DC - 1)), inc=sT if jc == DC - 1 else None)
            vqkv = sT.n
            kb.wait("ve", sT, vqkv)
            kb.wait("ve", sS)
            kb.op("ve", lambda e: e.tensor_copy(zbuf[:], PW0[:, 0:512]), inc=sV)
            kb.wait("sc", sV)
            kb.op("sc", lambda e: e.activation(rbuf[:], zbuf[:], AF.Square), inc=sS)
            kb.wait("ve", sS)
            kb.op("ve", lambda e: e.reduce_sum(
                bq_s[:, 0:8], rbuf[:].rearrange("p (h d) -> p h d", h=8), axis=AX), inc=sV)
            kb.wait("sc", sV)
            kb.op("sc", lambda e: e.activation(bq_s[:, 8:16], bq_s[:, 0:8], AF.Sqrt), inc=sS)
            kb.wait("ve", sS)
            kb.op("ve", lambda e: e.tensor_scalar(bq_s[:, 8:16], bq_s[:, 8:16], 1e-12,
                                                  None, ALU.max), inc=sV)
            kb.wait("ve", sV)
            kb.op("ve", lambda e: e.reciprocal(bq_s[:, 16:24], bq_s[:, 8:16]), inc=sV)
            kb.wait("ve", sV)
            for h in range(8):
                kb.op("ve", lambda e, h=h: e.tensor_scalar(
                    rbuf[:, h * 64:(h + 1) * 64], zbuf[:, h * 64:(h + 1) * 64],
                    bq_s[:, 16 + h:17 + h], None, ALU.mult), inc=sV)
            vqh = sV.n
            kb.wait("te", sV, vqh)
            for c4 in range(4):
                transpose128(rbuf[:, c4 * P:(c4 + 1) * P], qhT[:, c4, tt * P:(tt + 1) * P],
                             c4)
            kb.wait("ve", sT, vqkv)
            kb.op("ve", lambda e: e.tensor_copy(ebuf[:, 0:128], P1[:, 0:128]), inc=sV)
            kb.wait("sc", sV)
            kb.op("sc", lambda e: e.activation(ebuf[:, 128:192], ebuf[:, 0:64], AF.Square), inc=sS)
            kb.wait("ve", sS)
            kb.op("ve", lambda e: e.reduce_sum(bq_s[:, 24:25], ebuf[:, 128:192], axis=AX), inc=sV)
            kb.wait("sc", sV)
            kb.op("sc", lambda e: e.activation(bq_s[:, 25:26], bq_s[:, 24:25], AF.Sqrt), inc=sS)
            kb.wait("ve", sS)
            kb.op("ve", lambda e: e.tensor_scalar(bq_s[:, 25:26], bq_s[:, 25:26], 1e-12,
                                                  None, ALU.max), inc=sV)
            kb.wait("ve", sV)
            kb.op("ve", lambda e: e.reciprocal(bq_s[:, 26:27], bq_s[:, 25:26]), inc=sV)
            kb.wait("ve", sV)
            kb.op("ve", lambda e: e.tensor_scalar(ebuf[:, 192:256], ebuf[:, 0:64],
                                                  bq_s[:, 26:27], None, ALU.mult), inc=sV)
            kb.op("ve", lambda e, t=tt: e.tensor_copy(vloc[:, t, :], ebuf[:, 64:128]), inc=sV)
            vkh = sV.n
            kb.wait("te", sV, vkh)
            transpose128(ebuf[:, 192:256], kTl[:, tt * P:(tt + 1) * P], tt, rows=64)
        vprojs = sS.n

        if b3 < 2:
            sub1.close()
            bctx.close()
            return
        kb.wait("gp", sS, vprojs)
        kb.wait("gp", sV)
        kb.dma("gp", cc_k_in, kTl[:], sEX)
        kb.dma("gp", ccvi.rearrange("(n p) d -> p n d", p=P), vloc[:], sEX)
        kb.wait("gp", sEX)
        kb.op("gp", lambda e: e.collective_compute(
            "AllGather", ALU.bypass, replica_groups=GROUPS,
            ins=[cc_k_in], outs=[cc_k_out]), inc=sC2)
        kb.op("gp", lambda e: e.collective_compute(
            "AllGather", ALU.bypass, replica_groups=GROUPS,
            ins=[ccvi], outs=[ccvo]), inc=sC3)
        kb.wait("gp", sC2, 1)
        kb.wait("gp", sC3, 1)
        kb.wait("gp", sT)
        sub1.close()
        kTg = Eb(nc.sbuf_tensor("b_kTg", [P, 4, T], BF16))
        vg = Eb(nc.sbuf_tensor("b_vg", [P, 64, 128], BF16))
        attT = Eb(nc.sbuf_tensor("b_attT", [P, 4, T], BF16))
        bWoS = Eb(nc.sbuf_tensor("b_Wo", [P, 4, D], BF16))
        gOutS = Eb(nc.sbuf_tensor("b_go", [P, D], F32))
        kb.dma("gp", kTg[0:64, :, :], cc_k_out.rearrange("(r d) t -> d r t", d=64), sEX)
        kb.dma("gp", kTg[64:128, :, :], cc_k_out.rearrange("(r d) t -> d r t", d=64), sEX)
        kb.dma("gp", vg[:, :, 0:64], ccvo.rearrange("(n p) d -> p n d", p=P), sEX)
        kb.dma("gp", bWoS[:], bWout.rearrange("(c p) n -> p c n", p=P), sW)
        kb.wait("sy", sM)
        kb.dma("sy", gOutS[:], gOutB, sM)
        vld2 = (sEX.n, sW.n, sM.n)
        kb.wait("ve", sT)
        kb.op("ve", lambda e: e.memset(vg[:, :, 64:128], 0.0), inc=sV)
        kb.wait("ve", sV)
        kb.op("ve", lambda e: e.memset(vg[:, :, 64:65], 1.0), inc=sV)
        vvg = sV.n

        if b3 < 3:
            bctx.close()
            return
        sim3 = ((P0[:], P1[:]), (PT0[:], PT1[:]), (PW1[:, 0:512], PW1[:, 512:1024]))
        kb.wait("te", sEX, vld2[0])
        kb.wait("te", sW, vld2[1])
        kb.wait("te", sV, vvg)
        vout_ev = 0
        for hp in range(4):
            for qt in range(4):
                qsl = slice(qt * 512, (qt + 1) * 512)
                simdone = {}
                expdone = {}
                pvdone = {}

                def rec_sim(j, hp=hp, qsl=qsl):
                    pa, pb = sim3[j % 3]
                    r, tl = j // 16, (j % 16) * 128
                    kb.op("te", lambda e, p=pa, r=r, tl=tl: e.matmul(
                        p, kTg[0:64, r, tl:tl + 128], qhT[0:64, hp, qsl],
                        start=True, stop=True))
                    kb.op("te", lambda e, p=pb, r=r, tl=tl: e.matmul(
                        p, kTg[64:128, r, tl:tl + 128], qhT[64:128, hp, qsl],
                        start=True, stop=True), inc=sT)
                    simdone[j] = sT.n

                def rec_exp(j):
                    kb.wait("sc", sT, simdone[j])
                    if j >= 2:
                        kb.wait("sc", sT, pvdone[j - 2])
                    pa, pb = sim3[j % 3]
                    kb.op("sc", lambda e, p=pa, d=pta[j % 2]: e.activation(
                        d, p, AF.Exp, bias=-16.0, scale=16.0))
                    kb.op("sc", lambda e, p=pb, d=ptb2[j % 2]: e.activation(
                        d, p, AF.Exp, bias=-16.0, scale=16.0), inc=sS)
                    expdone[j] = sS.n

                def rec_pv(j):
                    kb.wait("te", sS, expdone[j])
                    kb.op("te", lambda e, j=j, d=pta[j % 2]: e.matmul(
                        PW0[:, 0:512], vg[:, j, :], d,
                        start=(j == 0), stop=(j == 63)))
                    kb.op("te", lambda e, j=j, d=ptb2[j % 2]: e.matmul(
                        PW0[:, 512:1024], vg[:, j, :], d,
                        start=(j == 0), stop=(j == 63)), inc=sT)
                    pvdone[j] = sT.n

                kb.wait("te", sV, vout_ev)
                kb.wait("te", sS)
                rec_sim(0)
                rec_exp(0)
                rec_sim(1)
                rec_exp(1)
                for j in range(64):
                    if j + 2 < 64:
                        rec_sim(j + 2)
                    rec_pv(j)
                    if j + 2 < 64:
                        rec_exp(j + 2)
                vpv = sT.n
                kb.wait("ve", sT, vpv)
                kb.op("ve", lambda e: e.tensor_copy(xnbuf[0:1, 0:512], PW0[64:65, 0:512]))
                kb.op("ve", lambda e: e.tensor_copy(xnbuf[0:1, 512:1024], PW0[64:65, 512:1024]), inc=sV)
                kb.wait("ve", sV)
                kb.op("ve", lambda e: e.reciprocal(zbuf[0:1, 0:512], xnbuf[0:1, 0:512]), inc=sV)
                kb.op("ve", lambda e: e.reciprocal(rbuf[0:1, 0:512], xnbuf[0:1, 512:1024]), inc=sV)
                kb.wait("ve", sV)
                kb.op("ve", lambda e: e.tensor_copy(dre[0], zbuf[0:1, 0:512]))
                kb.op("ve", lambda e: e.tensor_copy(dre[1], rbuf[0:1, 0:512]), inc=sV)
                vdre = sV.n
                kb.wait("te", sV, vdre)
                kb.wait("te", sS)
                kb.op("te", lambda e: e.matmul(P0[0:64, :], onesb, dre[0],
                                               start=True, stop=True))
                kb.op("te", lambda e: e.matmul(P1[0:64, :], onesb, dre[1],
                                               start=True, stop=True), inc=sT)
                kb.wait("sc", sT)
                kb.op("sc", lambda e: e.copy(zbuf[0:64, :], P0[0:64, :]))
                kb.op("sc", lambda e: e.copy(rbuf[0:64, :], P1[0:64, :]), inc=sS)
                kb.wait("ve", sS)
                kb.op("ve", lambda e, hp=hp, q=qsl: e.tensor_mul(
                    attT[0:64, hp, q], PW0[0:64, 0:512], zbuf[0:64, :]), inc=sV)
                kb.wait("ve", sV)
                kb.op("ve", lambda e: e.tensor_mul(
                    abtmp[0:64, :], PW0[0:64, 512:1024], rbuf[0:64, :]), inc=sV)
                vout_ev = sV.n
                kb.wait("sy", sV, vout_ev)
                kb.dma("sy", attT[64:128, hp, qsl], abtmp[0:64, :], sAB)
                kb.wait("ve", sAB)

        if b3 < 4:
            bctx.close()
            return
        vatt = sAB.n
        kb.wait("te", sAB, vatt)
        ofree = 0
        xfree = {0: 0, 1: 0}
        for tc in range(NT):
            xnb = xnbuf[:] if tc % 2 == 0 else ob2[:]
            kb.wait("te", sV, ofree)
            kb.wait("te", sS)
            for hp in range(4):
                for half in range(2):
                    hs = slice(half * 512, (half + 1) * 512)
                    kb.op("te", lambda e, hp=hp, t=tc, hs=hs: e.matmul(
                        PW0[:, hs], attT[0:64, hp, t * P:(t + 1) * P], bWoS[0:64, hp, hs],
                        start=(hp == 0), stop=(hp == 3)))
                    kb.op("te", lambda e, hp=hp, t=tc, hs=hs: e.matmul(
                        PW1[:, hs], attT[64:128, hp, t * P:(t + 1) * P], bWoS[64:128, hp, hs],
                        start=(hp == 0), stop=(hp == 3)),
                        inc=sT if (hp == 3 and half == 1) else None)
            kb.wait("sc", sT)
            kb.wait("sc", sV, xfree[tc % 2])
            kb.op("sc", lambda e, d=xnb: e.copy(d, PW1[:]), inc=sS)
            kb.wait("ve", sS)
            kb.wait("ve", sM, vld2[2])
            kb.op("ve", lambda e, d=xnb: e.tensor_add(d, d, PW0[:]), inc=sV)
            kb.wait("ve", sV)
            kb.op("ve", lambda e, d=xnb: e.bn_stats(lnst[:, 0:6], d[:, 0:512]))
            kb.op("ve", lambda e, d=xnb: e.bn_stats(lnst[:, 6:12], d[:, 512:1024]), inc=sV)
            kb.wait("ve", sV)
            kb.op("ve", lambda e: e.bn_aggr(lnmv[:, 0:2], lnst[:]), inc=sV)
            kb.wait("sc", sV)
            kb.op("sc", lambda e: e.activation(lnmv[:, 2:3], lnmv[:, 1:2], AF.Sqrt,
                                               bias=EPS_LN, scale=1.0), inc=sS)
            kb.wait("ve", sS)
            kb.op("ve", lambda e: e.reciprocal(lnmv[:, 3:4], lnmv[:, 2:3]), inc=sV)
            kb.wait("ve", sV)
            kb.op("ve", lambda e, d=xnb: e.tensor_scalar(
                d, d, lnmv[:, 0:1], lnmv[:, 3:4],
                ALU.subtract, ALU.mult), inc=sV)
            kb.wait("ve", sV)
            kb.op("ve", lambda e, d=xnb: e.tensor_mul(d, d, gOutS[:]), inc=sV)
            kb.wait("ve", sV)
            kb.op("ve", lambda e, t=tc, d=xnb: e.tensor_add(X[:, t, :], X[:, t, :], d), inc=sV)
            xfree[tc % 2] = sV.n
            ofree = sV.n - 6
        bctx.close()

    if "attn_b" in phases:
        emit_attn_b()
        kb.flush_block("attn_b")

    if "ffn2" in phases:
        emit_ffn("f2")
        kb.flush_block("ffn2")

    kb.wait("sy", sV)
    kb.wait("sy", sT)
    kb.wait("sy", sS)
    kb.dma("sy", out.rearrange("(n p) d -> p n d", p=P), X[:], sO)
    kb.wait("gp", sO)
    kb.flush_block("store")

    ctx.close()
    return nc


def _prep_inputs(inputs):
    gi = lambda k: np.ascontiguousarray(np.asarray(inputs[k], np.float32))
    x = gi("L0_attn_a_x")
    xsh = np.ascontiguousarray(x.reshape(8, T, D))
    base = {
        "aWqT": np.ascontiguousarray(gi("L0_attn_a_Wq").T),
        "aWkT": np.ascontiguousarray(gi("L0_attn_a_Wk").T),
        "aWvT": np.ascontiguousarray(gi("L0_attn_a_Wv").T),
        "aWoT": np.ascontiguousarray(gi("L0_attn_a_Wo").T),
        "aBq": np.ascontiguousarray(gi("L0_attn_a_bq").reshape(DC, P).T),
        "aBk": np.ascontiguousarray(gi("L0_attn_a_bk").reshape(DC, P).T),
        "aBv": gi("L0_attn_a_bv"), "aBo": gi("L0_attn_a_bo"),
        "aBkF": gi("L0_attn_a_bk"),
        "bNormG": np.ascontiguousarray(gi("L0_attn_b_norm_g").reshape(DC, P).T),
        "bWq": gi("L0_attn_b_Wq"), "bWkv": gi("L0_attn_b_Wkv"),
        "bWout": gi("L0_attn_b_Wout"),
        "gOutB": np.ascontiguousarray(np.broadcast_to(gi("L0_attn_b_out_norm_g"), (P, D))),
        "gateW9T": np.ascontiguousarray(np.concatenate(
            [gi("L0_ffn_gate_w").T, gi("L0_ffn_shared_gate_w").T], axis=1)),
        "fWg": gi("L0_ffn_w_gate"), "fWu": gi("L0_ffn_w_up"), "fWd": gi("L0_ffn_w_down"),
        "fswG": gi("L0_ffn_sw_gate"), "fswU": gi("L0_ffn_sw_up"), "fswD": gi("L0_ffn_sw_down"),
        "consts": np.ascontiguousarray(np.broadcast_to(
            np.array([EPS_LN, -16.0, 16.0, 1e-6], np.float32), (P, 4))),
        "ident": np.eye(P, dtype=np.float32),
        "tril": np.ascontiguousarray(np.triu(np.ones((P, P), np.float32))),
    }
    in_maps = []
    for c in range(8):
        r = c % 4
        rm = np.zeros((P, 4), np.float32)
        rm[:, :r] = 1.0
        im = dict(base)
        im["xs"] = xsh[c]
        im["rmask_in"] = rm
        in_maps.append(im)
    return in_maps


def kernel(**inputs):
    in_maps = _prep_inputs(inputs)
    nc = build_program()
    res = run_bass_kernel_spmd(nc, in_maps, core_ids=list(range(8)))
    outs = np.stack([np.asarray(res.results[c]["out"], np.float32) for c in range(8)])
    return np.ascontiguousarray(outs.reshape(2, 8192, 1024))
